# version 01
# baseline (speedup 1.0000x reference)
import numpy as np
import ml_dtypes
import concourse.bass as bass
import concourse.mybir as mybir
from concourse.bass_utils import run_bass_kernel_spmd

F32 = mybir.dt.float32
BF16 = mybir.dt.bfloat16
AF = mybir.ActivationFunctionType
ALU = mybir.AluOpType
AX = mybir.AxisListType
BF = ml_dtypes.bfloat16

NCORES = 8
NROWS = 4
S = 2048
D = 1024
NCH = 8
NG = 4
NQ = 512
EPS1 = 1e-5


ENGINES = ("tensor", "vector", "scalar", "gpsimd", "sync")


class Plan:
    def __init__(self, nc):
        self.nc = nc
        self.ops = {e: [] for e in ENGINES}
        self.sems = {}
        self.counts = {}
        self.waited = {e: {} for e in ENGINES}

    def _sem(self, name):
        if name not in self.sems:
            self.sems[name] = self.nc.alloc_semaphore(name)
            self.counts[name] = 0
        return self.sems[name]

    def op(self, eng, meth, *args, inc=None, incn=1, **kwargs):
        tok = None
        if inc is not None:
            self._sem(inc)
            self.counts[inc] += incn
            tok = (inc, self.counts[inc])
        self.ops[eng].append((meth, args, kwargs, inc, incn))
        return tok

    def wait(self, eng, tok):
        if tok is None:
            return
        sname, v = tok
        self._sem(sname)
        if self.waited[eng].get(sname, 0) >= v:
            return
        self.waited[eng][sname] = v
        self.ops[eng].append(("wait_ge", (sname, v), {}, None, 0))

    def wait_all(self, eng, toks):
        for t in toks:
            self.wait(eng, t)

    def build(self):
        plan = self

        def runner(engname):
            def run(e):
                for meth, args, kwargs, inc, incn in plan.ops[engname]:
                    if meth == "wait_ge":
                        e.wait_ge(plan.sems[args[0]], args[1])
                        continue
                    inst = getattr(e, meth)(*args, **kwargs)
                    if inc is not None:
                        inst.then_inc(plan.sems[inc], incn)
            return run

        with plan.nc.Block() as block:
            block.tensor(runner("tensor"))
            block.vector(runner("vector"))
            block.scalar(runner("scalar"))
            block.gpsimd(runner("gpsimd"))
            block.sync(runner("sync"))


class Banks:
    def __init__(self, nc, n=8):
        self.banks = [nc.alloc_psum_tensor(f"bank{i}", [128, 512], F32)
                      for i in range(n)]
        self.free_tok = [None] * n
        self.order = list(range(n))

    def acquire(self, p):
        i = self.order.pop(0)
        ft = self.free_tok[i]
        if ft is not None:
            for t in (ft if isinstance(ft, list) else [ft]):
                p.wait("tensor", t)
        return i, self.banks[i]

    def release(self, i, consumer_tok):
        self.free_tok[i] = consumer_tok
        self.order.append(i)


class Arena:
    def __init__(self, nc, name, cols, dtype):
        self.t = nc.alloc_sbuf_tensor(name, [128, cols], dtype)
        self.name = name
        self.cols = cols
        self.off = 0
        self.marks = []

    def alloc(self, w):
        assert self.off + w <= self.cols, \
            f"Arena {self.name} overflow: {self.off}+{w} > {self.cols}"
        v = self.t[:, self.off:self.off + w]
        self.off += w
        return v

    def push(self):
        self.marks.append(self.off)

    def pop(self):
        self.off = self.marks.pop()



def rope_tables_l0(s, d_rope):
    inv_freq = 1.0 / 10000.0 ** (np.arange(0, d_rope, 2, dtype=np.float32) / d_rope)
    half = d_rope // 2
    idx = [(2 * i) % half for i in range(half)]
    t = np.arange(s, dtype=np.float32)
    fr = np.outer(inv_freq[idx], t)
    c, sn = np.cos(fr), np.sin(fr)
    C = np.empty((d_rope, s), np.float32)
    Sg = np.empty((d_rope, s), np.float32)
    C[0::2] = c; C[1::2] = c
    Sg[0::2] = -sn; Sg[1::2] = sn
    return C, Sg


def col_chunks(vec, pad_to=None):
    vec = np.asarray(vec, np.float32).ravel()
    n = vec.shape[0]
    if pad_to is not None and n < pad_to:
        vec = np.concatenate([vec, np.zeros(pad_to - n, np.float32)])
        n = pad_to
    assert n % 128 == 0
    return np.ascontiguousarray(vec.reshape(n // 128, 128).T)


class Consts:
    def __init__(self):
        self.f32, self.bfl = [], []
        self.f32_off, self.bf_off = {}, {}
        self.nf, self.nb = 0, 0

    def add_f32(self, name, arr):
        arr = np.asarray(arr, np.float32)
        assert arr.shape[0] == 128, name
        self.f32_off[name] = self.nf
        self.nf += arr.shape[1]
        self.f32.append(arr)

    def add_bf(self, name, arr):
        arr = np.asarray(arr, np.float32)
        assert arr.shape[0] == 128, name
        self.bf_off[name] = self.nb
        self.nb += arr.shape[1]
        self.bfl.append(arr.astype(BF))

    def finish(self):
        cf = np.ascontiguousarray(np.concatenate(self.f32, axis=1))
        cb = np.ascontiguousarray(
            np.concatenate(self.bfl, axis=1).astype(np.float32))
        return cf, cb


def build_consts_common(c):
    c.add_bf("ones_col", np.ones((128, 1), np.float32))
    oner = np.zeros((128, 128), np.float32)
    oner[0, :] = 1.0
    oner[32, :] = 1.0
    oner[64, :] = 1.0
    c.add_f32("ones_row", oner)
    c.add_f32("ident_f32", np.eye(128, dtype=np.float32))
    c.add_bf("ident_bf", np.eye(128, dtype=np.float32))
    seld = np.zeros((128, 128), np.float32)
    seld[0, 0:64] = 1.0
    seld[64, 64:128] = 1.0
    c.add_f32("sel_den", seld)


def build_consts_l0a(c, inputs):
    c.add_f32("l0a_gamma", col_chunks(inputs["L0_attn_ln_scale"]))
    c.add_f32("l0a_beta", col_chunks(inputs["L0_attn_ln_bias"]))
    C0, S0 = rope_tables_l0(S, 64)
    c.add_bf("l0_ropeC", np.concatenate([C0, C0], axis=0))
    c.add_bf("l0_ropeS", np.concatenate([S0, S0], axis=0))
    P = np.zeros((128, 128), np.float32)
    for i in range(64):
        P[2 * i + 1, 2 * i] = 1.0
        P[2 * i, 2 * i + 1] = 1.0
    c.add_bf("perm_pairswap", P)
    for m in range(8):
        sel = np.zeros((128, 16), np.float32)
        sel[0:64, 2 * m] = 1.0
        sel[64:128, 2 * m + 1] = 1.0
        c.add_bf(f"sel_qk_{m}", sel)



class Ctx:
    def __init__(self, nc, consts, fa_cols=14020, ba_cols=69888):
        self.nc = nc
        self.p = Plan(nc)
        self.banks = Banks(nc)
        self.consts = consts
        self.cf_dram = nc.dram_tensor("c_f32", [128, consts.nf], F32,
                                      kind="ExternalInput").ap()
        self.cb_dram = nc.dram_tensor("c_bf", [128, consts.nb], F32,
                                      kind="ExternalInput").ap()
        self.cf = nc.alloc_sbuf_tensor("cf", [128, consts.nf], F32)
        self.cb = nc.alloc_sbuf_tensor("cb", [128, consts.nb], BF16)
        self.fa = Arena(nc, "fa", fa_cols, F32)
        self.ba = Arena(nc, "ba", ba_cols, BF16)
        self.dum = self.fa.alloc(4)
        fa, ba = self.fa, self.ba
        cm = {}
        cm["XF"] = fa.alloc(NCH * NH)
        cm["AO"] = fa.alloc(NCH * NH)
        cm["SC1"] = fa.alloc(NCH * NH)
        cm["TIN"] = fa.alloc(128)
        cm["DEN"] = fa.alloc(NQ)
        cm["DENI"] = fa.alloc(NQ)
        cm["BROW"] = fa.alloc(16)
        cm["BCOL"] = fa.alloc(16)
        cm["SMV"] = fa.alloc(16)
        cm["SMB"] = fa.alloc(2 * S + 8 * NH)
        pt_off = ba.off
        cm["PT"] = [ba.alloc(NQ) for _ in range(4)]
        cm["XB"] = ba.t[:, pt_off:pt_off + NCH * NH]
        cm["XSQ"] = ba.t[:, pt_off + NCH * NH:pt_off + 2 * NCH * NH]
        cm["TBa"] = ba.alloc(NQ)
        cm["TBb"] = ba.alloc(NQ)
        cm["TBr"] = ba.alloc(NQ)
        cm["SQT"] = ba.alloc(NQ)
        self.cm = cm
        SMB = cm["SMB"]
        r = {}
        r["CROW"] = SMB[0:1, 0:S]
        r["NMCROW"] = SMB[0:1, S:2 * S]
        _r0 = 2 * S
        for i, nm in enumerate(("mu", "var", "tmp", "iv", "ctot", "nmct",
                                "c2", "nmc2")):
            r[nm] = SMB[0:1, _r0 + i * NH:_r0 + (i + 1) * NH]
        self.rows = r
        self.g = {
            "rows_r": None,
            "xf_w": [], "xb_mm": None, "xsq_mm": None, "sc1_w": [],
            "tbr_rot": None, "tbr_ta": None,
            "den_w": [], "deni_r": None, "tin_r": None,
            "qn_r": None, "att_r": None,
            "ao_w": [[] for _ in range(NCH)],
            "slot": [None] * 4, "hf_r": None,
        }

    def load_consts(self):
        p = self.p
        t1 = p.op("sync", "dma_start", inc="ld", incn=16,
                  out=self.cf[:], in_=self.cf_dram)
        t2 = p.op("gpsimd", "dma_start", inc="gd", incn=16,
                  out=self.cb[:], in_=self.cb_dram)
        for e in ENGINES:
            p.wait(e, t1)
            p.wait(e, t2)

    def cfv(self, name, w=1, r0=0, rn=128):
        off = self.consts.f32_off[name]
        return self.cf[r0:r0 + rn, off:off + w]

    def cbv(self, name, w=1, r0=0, rn=128):
        off = self.consts.bf_off[name]
        return self.cb[r0:r0 + rn, off:off + w]

    dbgmap = None

    def dump(self, name, view):
        if not self.dbgmap or name not in self.dbgmap:
            return
        self.barrier()
        t = self.p.op("sync", "dma_start", inc="st", incn=16,
                      out=self.dbgmap[name], in_=view)
        self.p.wait("sync", t)
        self.barrier()

    def barrier(self):
        p = self.p
        toks = [
            p.op("vector", "memset", self.dum[0:1, 0:1], 0.0, inc="ve"),
            p.op("scalar", "memzero", self.dum[0:1, 1:2], inc="sc"),
            p.op("gpsimd", "memset", self.dum[0:1, 2:3], 0.0, inc="gp"),
        ]
        for sname in ("mm", "ld", "st", "gd"):
            if self.p.counts.get(sname, 0) > 0:
                toks.append((sname, self.p.counts[sname]))
        for e in ENGINES:
            p.wait_all(e, toks)


def colsum_mms(ctx, bank, src, n_w, dep=(), rows=1, lhsT=None, nch=NCH):
    p = ctx.p
    if lhsT is None:
        lhsT = ctx.cbv("ones_col")
    p.wait_all("tensor", dep)
    tok = None
    for cc in range(nch):
        last = cc == nch - 1
        tok = p.op("tensor", "matmul", bank[0:rows, 0:n_w], lhsT,
                   src[:, cc * n_w:(cc + 1) * n_w],
                   start=(cc == 0), stop=last, inc="mm" if last else None)
    return tok


def bcast_row(ctx, bank, row_view, n_w, m=128, dep=()):
    p = ctx.p
    p.wait_all("tensor", dep)
    ctx.g["rows_r"] = True
    base = row_view.tensor_base_partition() if hasattr(row_view, "tensor_base_partition") else None
    base = getattr(row_view, "base_partition", None)
    if callable(base):
        base = base()
    if base is None:
        base = 0
    ones = ctx.cfv("ones_row", w=m, rn=1, r0=base)
    tok = p.op("tensor", "matmul", bank[0:m, 0:n_w], ones, row_view,
               start=True, stop=True, inc="mm")
    ctx.g["rows_r"] = tok
    return tok


def ln_coeffs(ctx, b_sx, b_sq, rows, n_w, mm_toks, want_outer=True,
              fused_eps2=None):
    p = ctx.p
    v = "vector"
    rr = ctx.g.get("rows_r")
    if rr is not None and rr is not True:
        p.wait(v, rr)
        p.wait("scalar", rr)
    p.wait_all(v, mm_toks)
    invD = 1.0 / D
    t0 = p.op(v, "tensor_scalar_mul", rows["mu"], b_sx[0:1, 0:n_w], invD,
              inc="ve")
    tfree = p.op(v, "tensor_scalar_mul", rows["tmp"], b_sq[0:1, 0:n_w], invD,
                 inc="ve")
    p.wait(v, tfree)
    t1 = p.op(v, "scalar_tensor_tensor", rows["var"], rows["mu"], -1.0,
              rows["mu"], ALU.mult, ALU.mult, inc="ve")
    p.wait(v, t1)
    t2 = p.op(v, "tensor_add", rows["var"], rows["var"], rows["tmp"],
              inc="ve")
    p.wait(v, t2)
    done = []
    if want_outer:
        t3 = p.op(v, "tensor_scalar_add", rows["tmp"], rows["var"], EPS1,
                  inc="ve")
        p.wait(v, t3)
        tv = p.op(v, "reciprocal", rows["iv"], rows["tmp"], inc="ve")
        p.wait("scalar", tv)
        tc = p.op("scalar", "activation", rows["c"], rows["iv"], AF.Sqrt,
                  inc="sc")
        p.wait(v, tc)
        tn = p.op(v, "scalar_tensor_tensor", rows["nmc"], rows["mu"], -1.0,
                  rows["c"], ALU.mult, ALU.mult, inc="ve")
        done += [tc, tn]
    if fused_eps2 is not None:
        e2 = fused_eps2
        t4 = p.op(v, "tensor_scalar", rows["tmp"], rows["var"], 1.0 + e2,
                  EPS1 + EPS1 * e2, ALU.mult, ALU.add, inc="ve")
        p.wait(v, t4)
        tv = p.op(v, "reciprocal", rows["iv"], rows["tmp"], inc="ve")
        p.wait("scalar", tv)
        tc = p.op("scalar", "activation", rows["ctot"], rows["iv"], AF.Sqrt,
                  inc="sc")
        p.wait(v, tc)
        tn = p.op(v, "scalar_tensor_tensor", rows["nmct"], rows["mu"], -1.0,
                  rows["ctot"], ALU.mult, ALU.mult, inc="ve")
        done += [tc, tn]
    return done, tfree



NH = 128
HPN = NQ // NH


def emit_l0a(ctx, x_in, x_out, W):
    p = ctx.p
    ba, fa = ctx.ba, ctx.fa
    ba.push(); fa.push()

    HF = ba.alloc(NCH * NQ)
    KB = ba.alloc(NCH * S)
    CQ = ba.alloc(4 * S)
    CKV = ba.alloc(4 * S)
    VT = ba.alloc(16 * 1040)
    cm = ctx.cm
    PT = cm["PT"]
    TBa, TBb, TBr, SQT = cm["TBa"], cm["TBb"], cm["TBr"], cm["SQT"]
    XB, XSQ = cm["XB"], cm["XSQ"]
    W12 = ba.alloc(NCH * 1024)
    W45 = ba.alloc(4 * 1024)
    W1 = W12[:, 0:4096]
    W2 = W12[:, 4096:8192]
    W6 = W12
    W4 = W45
    W5 = W45
    W3 = HF
    QN = CKV[:, 0:NCH * NQ]
    ATT = CKV[:, NCH * NQ:2 * NCH * NQ]

    XF, AO, SC1 = cm["XF"], cm["AO"], cm["SC1"]
    TIN, DEN, DENI = cm["TIN"], cm["DEN"], cm["DENI"]
    BROW, BCOL, SMV = cm["BROW"], cm["BCOL"], cm["SMV"]
    rows_ = ctx.rows
    CROW, NMCROW = rows_["CROW"], rows_["NMCROW"]
    R_mu, R_var, R_tmp, R_iv = rows_["mu"], rows_["var"], rows_["tmp"], rows_["iv"]
    R_ctot, R_nmct, R_c2, R_nmc2 = rows_["ctot"], rows_["nmct"], rows_["c2"], rows_["nmc2"]

    QMX = SMV[0:16, 0:1]
    KMX = SMV[0:16, 1:2]
    QSQ = SMV[0:16, 2:3]
    KSQ = SMV[0:16, 3:4]
    NB = SMV[0:16, 4:5]
    KMC = SMV[0:16, 5:9]

    gamma = ctx.cfv("l0a_gamma", w=NCH)
    beta = ctx.cfv("l0a_beta", w=NCH)
    ropeC = ctx.cbv("l0_ropeC", w=S)
    ropeS = ctx.cbv("l0_ropeS", w=S)
    perm = ctx.cbv("perm_pairswap", w=128)
    ident = ctx.cfv("ident_f32", w=128)

    def wload(dst, wname, mcols, wait=None):
        p.wait("sync", ctx.wtok.get(wname))
        p.wait("sync", wait)
        return p.op("sync", "dma_start", inc="ld", incn=16,
                    out=dst.rearrange("p (c m) -> p c m", m=mcols),
                    in_=W[wname].rearrange("(c p) m -> p c m", p=128))

    tw1 = wload(W1, "wdqT", 512)
    tw2 = wload(W2, "wdkvT", 512)
    tw4 = wload(W4, "wupkT", 1024)
    t_vones = p.op("vector", "memset", VT[:, 64:16 * 1040:65], 1.0, inc="ve")
    t_tin0 = p.op("vector", "memset", TIN, 0.0, inc="ve")
    p.op("vector", "memset", DEN, 0.0)
    t_den0 = p.op("vector", "memset", DENI, 0.0, inc="ve")
    p.wait("scalar", t_den0)
    p.wait("tensor", t_den0)

    g = {
        "xf_w": [],
        "xb_mm": None, "xsq_mm": None,
        "sc1_w": [],
        "tbr_rot": None, "tbr_ta": None,
        "den_w": [],
        "deni_r": None,
        "tin_r": None,
        "qn_r": None, "att_r": None,
        "ao_w": [[] for _ in range(NCH)],
        "slot": [None] * 4,
        "hf_r": None,
    }

    def halfload(dram, c0, width, dst):
        return p.op("sync", "dma_start", inc="ld", incn=16,
                    out=dst.rearrange("p (c w) -> p c w", w=width),
                    in_=dram[:, :, c0:c0 + width].rearrange("c p w -> p c w"))

    def stats_and_coeffs(srcf, width, rows, fused_eps2, extra_cast_dep=()):
        p.wait("vector", g["xb_mm"])
        p.wait_all("vector", [g["slot"][0], g["slot"][1]])
        p.wait_all("vector", extra_cast_dep)
        tcast = p.op("vector", "tensor_copy", XB[:, 0:NCH * width], srcf,
                     inc="ve")
        g["slot"][0] = tcast
        g["slot"][1] = tcast
        p.wait_all("scalar", [g["slot"][2], g["slot"][3]])
        p.wait("scalar", g["xsq_mm"])
        p.wait_all("scalar", extra_cast_dep)
        tsq = p.op("scalar", "activation", XSQ[:, 0:NCH * width], srcf,
                   AF.Square, inc="sc")
        g["slot"][2] = tsq
        g["slot"][3] = tsq
        isx, bsx = ctx.banks.acquire(p)
        isq, bsq = ctx.banks.acquire(p)
        t1 = colsum_mms(ctx, bsx, XB[:, 0:NCH * width], width, dep=(tcast,))
        t2 = colsum_mms(ctx, bsq, XSQ[:, 0:NCH * width], width, dep=(tsq,))
        g["xb_mm"], g["xsq_mm"] = t1, t2
        done, tfree = ln_coeffs(ctx, bsx, bsq, rows, width, (t1, t2),
                                want_outer=True, fused_eps2=fused_eps2)
        ctx.banks.release(isx, tfree)
        ctx.banks.release(isq, tfree)
        return done, tcast, tsq

    cq_last = None
    for n in range(NG):
        nq0 = n * NQ
        hf_half_toks = []
        for hh in range(HPN):
            c0 = nq0 + hh * NH
            p.wait_all("sync", g["xf_w"])
            tx = halfload(x_in, c0, NH, XF)
            p.wait("vector", tx)
            p.wait("scalar", tx)
            rows = {"mu": R_mu, "var": R_var, "tmp": R_tmp,
                    "iv": R_iv, "c": CROW[:, c0:c0 + NH],
                    "nmc": NMCROW[:, c0:c0 + NH],
                    "ctot": R_ctot, "nmct": R_nmct}
            done, tcast, tsq = stats_and_coeffs(XF, NH, rows, 1e-5)
            ibc1, bc1 = ctx.banks.acquire(p)
            ibc2, bc2 = ctx.banks.acquire(p)
            tb1 = bcast_row(ctx, bc1, R_ctot, NH, dep=done)
            tb2 = bcast_row(ctx, bc2, R_nmct, NH, dep=done)
            p.wait_all("vector", [tb1, tb2] + g["sc1_w"])
            last_tt = None
            th = None
            for cc in range(NCH):
                xc = XF[:, cc * NH:(cc + 1) * NH]
                s1 = SC1[:, cc * NH:(cc + 1) * NH]
                p.op("vector", "tensor_mul", s1, xc, bc1[0:128, 0:NH])
                last_tt = p.op("vector", "tensor_add", s1, s1,
                               bc2[0:128, 0:NH], inc="ve")
                p.wait("vector", g["hf_r"])
                th = p.op("vector", "tensor_scalar",
                          HF[:, cc * NQ + hh * NH:cc * NQ + hh * NH + NH],
                          s1, gamma[:, cc:cc + 1], beta[:, cc:cc + 1],
                          ALU.mult, ALU.add, inc="ve")
            hf_half_toks.append(th)
            g["xf_w"] = [last_tt, tsq, tcast]
            g["sc1_w"] = [th]
            ctx.banks.release(ibc1, last_tt)
            ctx.banks.release(ibc2, last_tt)
        if n == 0:
            ctx.dump("hf0", HF)
        p.wait_all("tensor", hf_half_toks + [tw1, tw2])
        for dst, wbuf in ((CQ, W1), (CKV, W2)):
            for m in range(4):
                ib, bk = ctx.banks.acquire(p)
                tok = None
                for kc in range(NCH):
                    tok = p.op("tensor", "matmul", bk[0:128, 0:NQ],
                               wbuf[:, kc * 512 + m * 128:kc * 512 + m * 128 + 128],
                               HF[:, kc * NQ:(kc + 1) * NQ],
                               start=(kc == 0), stop=(kc == 7),
                               inc="mm" if kc == 7 else None)
                p.wait("scalar", tok)
                cq_last = p.op("scalar", "activation",
                               dst[:, m * S + nq0:m * S + nq0 + NQ],
                               bk[0:128, 0:NQ], AF.Copy, inc="sc")
                ctx.banks.release(ib, cq_last)
        g["hf_r"] = ("mm", p.counts["mm"])

    ctx.dump("cq", CQ)
    ctx.dump("ckv", CKV)
    tw6 = wload(W6, "woutT", 1024, wait=g["hf_r"])
    tw3 = wload(W3, "wupqT", 1024, wait=g["hf_r"])

    p.wait_all("tensor", [cq_last, tw4])
    kb_last = None
    tkmc = None
    for n in range(NG):
        nq0 = n * NQ
        ink, bnk = ctx.banks.acquire(p)
        sel_mm = None
        for m in range(NCH):
            ib, bk = ctx.banks.acquire(p)
            tok = None
            for kc in range(4):
                tok = p.op("tensor", "matmul", bk[0:128, 0:NQ],
                           W4[:, kc * 1024 + m * 128:kc * 1024 + m * 128 + 128],
                           CKV[:, kc * S + nq0:kc * S + nq0 + NQ],
                           start=(kc == 0), stop=(kc == 3),
                           inc="mm" if kc == 3 else None)
            p.wait("scalar", tok)
            p.wait("scalar", g["tbr_rot"])
            p.wait("scalar", g["tbr_ta"])
            tkraw = p.op("scalar", "activation", TBr, bk[0:128, 0:NQ],
                         AF.Copy, inc="sc")
            ctx.banks.release(ib, tkraw)
            ir, br = ctx.banks.acquire(p)
            p.wait("tensor", tkraw)
            trot = p.op("tensor", "matmul", br[0:128, 0:NQ], perm, TBr,
                        start=True, stop=True, inc="mm")
            g["tbr_rot"] = trot
            p.wait("vector", tkraw)
            ta = p.op("vector", "tensor_mul", TBa, TBr,
                      ropeC[:, nq0:nq0 + NQ], inc="ve")
            g["tbr_ta"] = ta
            p.wait("vector", trot)
            tb = p.op("vector", "tensor_mul", TBb, br[0:128, 0:NQ],
                      ropeS[:, nq0:nq0 + NQ], inc="ve")
            ctx.banks.release(ir, tb)
            kdst = KB[:, m * S + nq0:m * S + nq0 + NQ]
            kb_last = p.op("vector", "tensor_add", kdst, TBa, TBb, inc="ve")
            p.wait("scalar", kb_last)
            p.wait("scalar", sel_mm)
            tsqk = p.op("scalar", "activation", SQT, kdst, AF.Square,
                        inc="sc")
            p.wait("tensor", tsqk)
            sel_mm = p.op("tensor", "matmul", bnk[0:16, 0:NQ],
                          ctx.cbv(f"sel_qk_{m}", w=16), SQT,
                          start=(m == 0), stop=(m == 7), inc="mm")
        p.wait("vector", sel_mm)
        tkmc = p.op("vector", "reduce_max", KMC[:, n:n + 1],
                    bnk[0:16, 0:NQ], axis=AX.X, inc="ve")
        ctx.banks.release(ink, tkmc)
    ctx.dump("kb", KB)
    p.wait("vector", tkmc)
    tkm = p.op("vector", "reduce_max", KMX, KMC, axis=AX.X, inc="ve")
    p.wait("scalar", tkm)
    tks = p.op("scalar", "activation", KSQ, KMX, AF.Sqrt, inc="sc")

    tw5 = wload(W5, "wupvT", 1024, wait=("mm", p.counts["mm"]))
    p.wait_all("tensor", [tw5, t_vones, cq_last])
    vt_last = None
    for tt in range(16):
        for dn in range(2):
            ib, bk = ctx.banks.acquire(p)
            tok = None
            for kc in range(4):
                tok = p.op("tensor", "matmul", bk[0:128, 0:NQ],
                           CKV[:, kc * S + tt * 128:kc * S + tt * 128 + 128],
                           W5[:, kc * 1024 + dn * 512:kc * 1024 + dn * 512 + 512],
                           start=(kc == 0), stop=(kc == 3),
                           inc="mm" if kc == 3 else None)
            p.wait("vector", tok)
            base = tt * 1040 + dn * 520
            dst = VT[:, base:base + 520].rearrange(
                "p (h dd) -> p h dd", dd=65)[:, :, 0:64]
            vt_last = p.op("vector", "tensor_copy", dst,
                           bk[0:128, 0:NQ].rearrange("p (h d) -> p h d", d=64),
                           inc="ve")
            ctx.banks.release(ib, vt_last)
    ctx.dump("vt", VT)
    g["qn_r"] = ("mm", p.counts["mm"])
    g["att_r"] = ("mm", p.counts["mm"])

    p.wait_all("tensor", [tw3, tw6])
    tst = None
    for n in range(NG):
        nq0 = n * NQ
        inq, bnq = ctx.banks.acquire(p)
        sel_mm = None
        qn_last = None
        p.wait("vector", g["qn_r"])
        for m in range(NCH):
            ib, bk = ctx.banks.acquire(p)
            tok = None
            for kc in range(4):
                tok = p.op("tensor", "matmul", bk[0:128, 0:NQ],
                           W3[:, kc * 1024 + m * 128:kc * 1024 + m * 128 + 128],
                           CQ[:, kc * S + nq0:kc * S + nq0 + NQ],
                           start=(kc == 0), stop=(kc == 3),
                           inc="mm" if kc == 3 else None)
            p.wait("scalar", tok)
            p.wait("scalar", g["tbr_rot"])
            p.wait("scalar", g["tbr_ta"])
            tqraw = p.op("scalar", "activation", TBr, bk[0:128, 0:NQ],
                         AF.Copy, inc="sc")
            ctx.banks.release(ib, tqraw)
            ir, br = ctx.banks.acquire(p)
            p.wait("tensor", tqraw)
            trot = p.op("tensor", "matmul", br[0:128, 0:NQ], perm, TBr,
                        start=True, stop=True, inc="mm")
            g["tbr_rot"] = trot
            p.wait("vector", tqraw)
            ta = p.op("vector", "tensor_mul", TBa, TBr,
                      ropeC[:, nq0:nq0 + NQ], inc="ve")
            g["tbr_ta"] = ta
            p.wait("vector", trot)
            tb = p.op("vector", "tensor_mul", TBb, br[0:128, 0:NQ],
                      ropeS[:, nq0:nq0 + NQ], inc="ve")
            ctx.banks.release(ir, tb)
            qdst = QN[:, m * NQ:(m + 1) * NQ]
            qn_last = p.op("vector", "tensor_add", qdst, TBa, TBb, inc="ve")
            p.wait("scalar", qn_last)
            p.wait("scalar", sel_mm)
            tsqq = p.op("scalar", "activation", SQT, qdst, AF.Square,
                        inc="sc")
            p.wait("tensor", tsqq)
            sel_mm = p.op("tensor", "matmul", bnq[0:16, 0:NQ],
                          ctx.cbv(f"sel_qk_{m}", w=16), SQT,
                          start=(m == 0), stop=(m == 7), inc="mm")
        if n == 0:
            ctx.dump("qn0", QN)
        p.wait("vector", sel_mm)
        tqm = p.op("vector", "reduce_max", QMX, bnq[0:16, 0:NQ], axis=AX.X,
                   inc="ve")
        ctx.banks.release(inq, tqm)
        p.wait("scalar", tqm)
        tqs = p.op("scalar", "activation", QSQ, QMX, AF.Sqrt, inc="sc")
        p.wait("vector", tqs)
        p.wait("vector", tks)
        p.wait("vector", g["tin_r"])
        tnb = p.op("vector", "scalar_tensor_tensor", NB, QSQ, -0.125, KSQ,
                   ALU.mult, ALU.mult, inc="ve")
        p.wait("vector", tnb)
        ttin = p.op("vector", "tensor_copy", TIN[0:16, 0:1], NB, inc="ve")
        it, btr = ctx.banks.acquire(p)
        p.wait("tensor", ttin)
        ttr = p.op("tensor", "transpose", btr[0:128, 0:128], TIN, ident,
                   inc="mm")
        g["tin_r"] = ttr
        p.wait("scalar", ttr)
        tbrow = p.op("scalar", "activation", BROW[0:1, 0:16],
                     btr[0:1, 0:16], AF.Copy, inc="sc")
        ctx.banks.release(it, tbrow)
        ibc, bbc = ctx.banks.acquire(p)
        tbc = bcast_row(ctx, bbc, BROW[0:1, 0:16], 16, dep=(tbrow,))
        p.wait("scalar", tbc)
        tbcol = p.op("scalar", "activation", BCOL, bbc[0:128, 0:16],
                     AF.Copy, inc="sc")
        ctx.banks.release(ibc, tbcol)

        if n == 0:
            ctx.dump("tin0", TIN)
            ctx.dump("smv0", SMV)
            ctx.dump("brow0", BROW)
            ctx.dump("bcol0", BCOL)
        p.wait_all("tensor", [qn_last, kb_last, vt_last, tbcol])
        att_last = None
        for hp in range(8):
            ia, boa = ctx.banks.acquire(p)
            ib2, bob = ctx.banks.acquire(p)
            tva = tvb = None
            for j in range(16):
                isa, bsa = ctx.banks.acquire(p)
                isb, bsb = ctx.banks.acquire(p)
                kc0 = hp * S + j * 128
                tsa = p.op("tensor", "matmul", bsa[0:128, 0:NQ],
                           KB[0:64, kc0:kc0 + 128],
                           QN[0:64, hp * NQ:(hp + 1) * NQ],
                           start=True, stop=True, tile_position=(0, 0),
                           inc="mm")
                tsb = p.op("tensor", "matmul", bsb[0:128, 0:NQ],
                           KB[64:128, kc0:kc0 + 128],
                           QN[64:128, hp * NQ:(hp + 1) * NQ],
                           start=True, stop=True, tile_position=(64, 0),
                           inc="mm")
                sa = (2 * j) % 4
                sb = (2 * j + 1) % 4
                p.wait("scalar", tsa)
                p.wait("scalar", g["slot"][sa])
                p.wait("scalar", g["xb_mm"] if sa < 2 else g["xsq_mm"])
                tea = p.op("scalar", "activation", PT[sa],
                           bsa[0:128, 0:NQ], AF.Exp,
                           bias=BCOL[:, 2 * hp:2 * hp + 1], scale=0.125,
                           inc="sc")
                ctx.banks.release(isa, tea)
                p.wait("scalar", tsb)
                p.wait("scalar", g["slot"][sb])
                p.wait("scalar", g["xb_mm"] if sb < 2 else g["xsq_mm"])
                teb = p.op("scalar", "activation", PT[sb],
                           bsb[0:128, 0:NQ], AF.Exp,
                           bias=BCOL[:, 2 * hp + 1:2 * hp + 2], scale=0.125,
                           inc="sc")
                ctx.banks.release(isb, teb)
                p.wait("tensor", tea)
                tva = p.op("tensor", "matmul", boa[0:65, 0:NQ],
                           VT[:, j * 1040 + (2 * hp) * 65:
                              j * 1040 + (2 * hp) * 65 + 65],
                           PT[sa], start=(j == 0), stop=(j == 15), inc="mm")
                g["slot"][sa] = tva
                p.wait("tensor", teb)
                tvb = p.op("tensor", "matmul", bob[0:65, 0:NQ],
                           VT[:, j * 1040 + (2 * hp + 1) * 65:
                              j * 1040 + (2 * hp + 1) * 65 + 65],
                           PT[sb], start=(j == 0), stop=(j == 15), inc="mm")
                g["slot"][sb] = tvb
            p.wait("scalar", tva)
            p.wait("scalar", tvb)
            p.wait_all("scalar", g["den_w"])
            td1 = p.op("scalar", "activation", DEN[0:1, 0:NQ],
                       boa[64:65, 0:NQ], AF.Copy, bias=1e-30, inc="sc")
            td2 = p.op("scalar", "activation", DEN[64:65, 0:NQ],
                       bob[64:65, 0:NQ], AF.Copy, bias=1e-30, inc="sc")
            p.wait("vector", td1)
            p.wait("vector", td2)
            p.wait("vector", g["deni_r"])
            p.op("vector", "reciprocal", DENI[0:1, 0:NQ], DEN[0:1, 0:NQ])
            tri = p.op("vector", "reciprocal", DENI[64:65, 0:NQ],
                       DEN[64:65, 0:NQ], inc="ve")
            ir2, brb = ctx.banks.acquire(p)
            p.wait("tensor", tri)
            trb = p.op("tensor", "matmul", brb[0:128, 0:NQ],
                       ctx.cfv("sel_den", w=128, rn=65), DENI[0:65, 0:NQ],
                       start=True, stop=True, inc="mm")
            g["deni_r"] = trb
            p.wait("scalar", trb)
            trbe = p.op("scalar", "activation", DEN, brb[0:128, 0:NQ],
                        AF.Copy, inc="sc")
            ctx.banks.release(ir2, trbe)
            p.wait("vector", trbe)
            p.wait("vector", g["att_r"])
            ta1 = p.op("vector", "tensor_mul", ATT[0:64, hp * NQ:(hp + 1) * NQ],
                       boa[0:64, 0:NQ], DEN[0:64, :], inc="ve")
            ctx.banks.release(ia, [td1, ta1])
            ta2 = p.op("vector", "tensor_mul",
                       ATT[64:128, hp * NQ:(hp + 1) * NQ],
                       bob[0:64, 0:NQ], DEN[64:128, :], inc="ve")
            ctx.banks.release(ib2, [td2, ta2])
            g["den_w"] = [ta1, ta2, tri]
            att_last = ta2
        g["qn_r"] = ("mm", p.counts["mm"])

        if n == 0:
            ctx.dump("att0", ATT)
        p.wait("tensor", att_last)
        for hh in range(HPN):
            c0 = nq0 + hh * NH
            p.wait_all("sync", g["xf_w"])
            tx2 = halfload(x_in, c0, NH, XF)
            ao_toks = []
            for m in range(NCH):
                ib, bk = ctx.banks.acquire(p)
                tok = None
                for kc in range(NCH):
                    tok = p.op("tensor", "matmul", bk[0:128, 0:NH],
                               W6[:, kc * 1024 + m * 128:kc * 1024 + m * 128 + 128],
                               ATT[:, kc * NQ + hh * NH:kc * NQ + hh * NH + NH],
                               start=(kc == 0), stop=(kc == 7),
                               inc="mm" if kc == 7 else None)
                p.wait("scalar", tok)
                p.wait_all("scalar", g["ao_w"][m])
                tev = p.op("scalar", "activation", AO[:, m * NH:(m + 1) * NH],
                           bk[0:128, 0:NH], AF.Copy, inc="sc")
                ao_toks.append(tev)
                ctx.banks.release(ib, tev)
            g["att_r"] = ("mm", p.counts["mm"])
            if n == 0 and hh == 0:
                ctx.dump("ao00", AO)
            ibl1, bl1 = ctx.banks.acquire(p)
            ibl2, bl2 = ctx.banks.acquire(p)
            tb1 = bcast_row(ctx, bl1, CROW[:, c0:c0 + NH], NH)
            tb2 = bcast_row(ctx, bl2, NMCROW[:, c0:c0 + NH], NH)
            p.wait_all("vector", [tb1, tb2, tx2] + g["sc1_w"])
            ell_toks = []
            last_tt = None
            for cc in range(NCH):
                xc = XF[:, cc * NH:(cc + 1) * NH]
                s1 = SC1[:, cc * NH:(cc + 1) * NH]
                aoc = AO[:, cc * NH:(cc + 1) * NH]
                p.op("vector", "tensor_mul", s1, xc, bl1[0:128, 0:NH])
                last_tt = p.op("vector", "tensor_add", s1, s1,
                               bl2[0:128, 0:NH], inc="ve")
                p.wait("gpsimd", last_tt)
                p.wait("gpsimd", ao_toks[cc])
                tg = p.op("gpsimd", "tensor_add", aoc, aoc, s1, inc="gp")
                ell_toks.append(tg)
            ctx.banks.release(ibl1, last_tt)
            ctx.banks.release(ibl2, last_tt)
            if n == 0 and hh == 0:
                ctx.dump("ell00", AO)
            rows = {"mu": R_mu, "var": R_var, "tmp": R_tmp, "iv": R_iv,
                    "c": R_c2, "nmc": R_nmc2}
            done, tcast, tsq = stats_and_coeffs(AO, NH, rows, None,
                                                extra_cast_dep=ell_toks)
            ibc1, bc1 = ctx.banks.acquire(p)
            ibc2, bc2 = ctx.banks.acquire(p)
            tb1 = bcast_row(ctx, bc1, R_c2, NH, dep=done)
            tb2 = bcast_row(ctx, bc2, R_nmc2, NH, dep=done)
            p.wait_all("vector", [tb1, tb2])
            xn_toks = []
            exit_tt = None
            for cc in range(NCH):
                aoc = AO[:, cc * NH:(cc + 1) * NH]
                xc = XF[:, cc * NH:(cc + 1) * NH]
                p.op("vector", "tensor_mul", aoc, aoc, bc1[0:128, 0:NH])
                exit_tt = p.op("vector", "tensor_add", aoc, aoc,
                               bc2[0:128, 0:NH], inc="ve")
                p.wait("gpsimd", exit_tt)
                tg = p.op("gpsimd", "tensor_add", aoc, aoc, xc, inc="gp")
                xn_toks.append(tg)
            ctx.banks.release(ibc1, exit_tt)
            ctx.banks.release(ibc2, exit_tt)
            p.wait_all("sync", xn_toks)
            tst = p.op("sync", "dma_start", inc="st", incn=16,
                       out=x_out[:, :, c0:c0 + NH].rearrange("c p w -> p c w"),
                       in_=AO.rearrange("p (c w) -> p c w", w=NH))
            for cc in range(NCH):
                g["ao_w"][cc] = [xn_toks[cc], tcast, tsq, tst]
            g["xf_w"] = [xn_toks[-1], exit_tt, last_tt, tcast, tsq]
            g["sc1_w"] = [xn_toks[-1]] + ell_toks

    ba.pop(); fa.pop()
    return tst



def emit_l0f(ctx, x_in, x_out, W):
    p = ctx.p
    ba, fa = ctx.ba, ctx.fa
    ba.push(); fa.push()
    cm, g, rows_ = ctx.cm, ctx.g, ctx.rows

    HF = ba.alloc(NCH * S)
    WT = ba.alloc(NCH * 1024)
    WG = ba.alloc(NCH * 8)
    WU = ba.alloc(NCH * 128)
    WV = ba.alloc(1024)
    TM = ba.alloc(NH)
    GTB = ba.alloc(NH)

    XF, AO, SC1 = cm["XF"], cm["AO"], cm["SC1"]
    TIN, DEN, DENI = cm["TIN"], cm["DEN"], cm["DENI"]
    BROW, BCOL, SMV = cm["BROW"], cm["BCOL"], cm["SMV"]
    XB, XSQ, SQT = cm["XB"], cm["XSQ"], cm["SQT"]
    R_mu, R_var, R_tmp, R_iv = rows_["mu"], rows_["var"], rows_["tmp"], rows_["iv"]
    R_ctot, R_nmct = rows_["ctot"], rows_["nmct"]
    R_c2, R_nmc2 = rows_["c2"], rows_["nmc2"]
    ident = ctx.cfv("ident_f32", w=128)
    selg = ctx.cfv("selg", w=128, rn=8)

    def wload(dst, wname, mcols, wait=None):
        p.wait("sync", ctx.wtok.get(wname))
        p.wait("sync", wait)
        return p.op("sync", "dma_start", inc="ld", incn=16,
                    out=dst.rearrange("p (c m) -> p c m", m=mcols),
                    in_=W[wname].rearrange("(c p) m -> p c m", p=128))

    twt = wload(WT, "l0f_wT", 1024)
    twg = wload(WG, "l0f_gwT", 8)
    twu = wload(WU, "l0f_uT", 128)
    p.wait("sync", ctx.wtok.get("l0f_vall"))
    twv = p.op("sync", "dma_start", inc="ld", incn=16, out=WV, in_=W["l0f_vall"])

    def halfload(c0, width):
        return p.op("sync", "dma_start", inc="ld", incn=16,
                    out=XF.rearrange("p (c w) -> p c w", w=width),
                    in_=x_in[:, :, c0:c0 + width].rearrange("c p w -> p c w"))

    def stats_and_coeffs(srcf, width, rows, fused_eps2, extra_cast_dep=()):
        p.wait("vector", g["xb_mm"])
        p.wait_all("vector", [g["slot"][0], g["slot"][1]])
        p.wait_all("vector", extra_cast_dep)
        tcast = p.op("vector", "tensor_copy", XB[:, 0:NCH * width], srcf,
                     inc="ve")
        g["slot"][0] = tcast
        g["slot"][1] = tcast
        p.wait_all("scalar", [g["slot"][2], g["slot"][3]])
        p.wait("scalar", g["xsq_mm"])
        p.wait_all("scalar", extra_cast_dep)
        tsq = p.op("scalar", "activation", XSQ[:, 0:NCH * width], srcf,
                   AF.Square, inc="sc")
        g["slot"][2] = tsq
        g["slot"][3] = tsq
        isx, bsx = ctx.banks.acquire(p)
        isq, bsq = ctx.banks.acquire(p)
        t1 = colsum_mms(ctx, bsx, XB[:, 0:NCH * width], width, dep=(tcast,))
        t2 = colsum_mms(ctx, bsq, XSQ[:, 0:NCH * width], width, dep=(tsq,))
        g["xb_mm"], g["xsq_mm"] = t1, t2
        done, tfree = ln_coeffs(ctx, bsx, bsq, rows, width, (t1, t2),
                                want_outer=True, fused_eps2=fused_eps2)
        ctx.banks.release(isx, tfree)
        ctx.banks.release(isq, tfree)
        return done, tcast, tsq

    hf_toks = []
    for n in range(NG):
        for hh in range(HPN):
            c0 = n * NQ + hh * NH
            p.wait_all("sync", g["xf_w"])
            tx = halfload(c0, NH)
            p.wait("vector", tx)
            p.wait("scalar", tx)
            rows = {"mu": R_mu, "var": R_var, "tmp": R_tmp, "iv": R_iv,
                    "c": R_ctot, "nmc": R_nmct}
            done, tcast, tsq = stats_and_coeffs(XF, NH, rows, None)
            ibc1, bc1 = ctx.banks.acquire(p)
            ibc2, bc2 = ctx.banks.acquire(p)
            tb1 = bcast_row(ctx, bc1, R_ctot, NH, dep=done)
            tb2 = bcast_row(ctx, bc2, R_nmct, NH, dep=done)
            p.wait_all("vector", [tb1, tb2] + g["sc1_w"])
            last_tt = None
            for cc in range(NCH):
                xc = XF[:, cc * NH:(cc + 1) * NH]
                s1 = SC1[:, cc * NH:(cc + 1) * NH]
                p.op("vector", "tensor_mul", s1, xc, bc1[0:128, 0:NH])
                last_tt = p.op("vector", "tensor_add",
                               HF[:, cc * S + c0:cc * S + c0 + NH], s1,
                               bc2[0:128, 0:NH], inc="ve")
            hf_toks.append(last_tt)
            g["xf_w"] = [last_tt, tsq, tcast]
            g["sc1_w"] = []
            ctx.banks.release(ibc1, last_tt)
            ctx.banks.release(ibc2, last_tt)

    p.wait_all("tensor", hf_toks + [twg])
    for n in range(NG):
        nq0 = n * NQ
        ib, bk = ctx.banks.acquire(p)
        tok = None
        for kc in range(NCH):
            tok = p.op("tensor", "matmul", bk[0:8, 0:NQ],
                       WG[:, kc * 8:(kc + 1) * 8],
                       HF[:, kc * S + nq0:kc * S + nq0 + NQ],
                       start=(kc == 0), stop=(kc == 7),
                       inc="mm" if kc == 7 else None)
        p.wait("vector", tok)
        tmx = p.op("vector", "reduce_max", SMV[0:8, 5 + n:6 + n],
                   bk[0:8, 0:NQ], axis=AX.X, inc="ve")
        ctx.banks.release(ib, tmx)
    tg8 = p.op("vector", "reduce_max", SMV[0:8, 0:1], SMV[0:8, 5:9],
               axis=AX.X, inc="ve")
    p.wait("vector", g["tin_r"])
    p.wait("vector", tg8)
    ttin = p.op("vector", "tensor_copy", TIN[0:8, 0:1], SMV[0:8, 0:1],
                inc="ve")
    it, btr = ctx.banks.acquire(p)
    p.wait("tensor", ttin)
    ttr = p.op("tensor", "transpose", btr[0:128, 0:128], TIN, ident, inc="mm")
    g["tin_r"] = ttr
    p.wait("scalar", ttr)
    tbrow = p.op("scalar", "activation", BROW[0:1, 0:8], btr[0:1, 0:8],
                 AF.Copy, inc="sc")
    ctx.banks.release(it, tbrow)
    p.wait("vector", tbrow)
    tbg = p.op("vector", "reduce_max", SMV[0:1, 1:2], BROW[0:1, 0:8],
               axis=AX.X, inc="ve")
    p.wait("vector", tbg)
    tnbg = p.op("vector", "tensor_scalar_mul", SMV[0:1, 2:3], SMV[0:1, 1:2],
                -1.0, inc="ve")
    ibg, bbg = ctx.banks.acquire(p)
    p.wait("tensor", tnbg)
    tbgm = p.op("tensor", "matmul", bbg[0:8, 0:1],
                ctx.cfv("ones_row", w=8, rn=1), SMV[0:1, 2:3],
                start=True, stop=True, inc="mm")
    p.wait("scalar", tbgm)
    tbgc = p.op("scalar", "activation", BCOL[0:8, 0:1], bbg[0:8, 0:1],
                AF.Copy, inc="sc")
    ctx.banks.release(ibg, tbgc)

    p.wait_all("tensor", [twt, twu, twv, tbgc])
    tst = None
    den_g = []
    for n in range(NG):
        for hh in range(HPN):
            c0 = n * NQ + hh * NH
            p.wait_all("sync", g["xf_w"])
            tx2 = halfload(c0, NH)
            igl, bgl = ctx.banks.acquire(p)
            tok = None
            for kc in range(NCH):
                tok = p.op("tensor", "matmul", bgl[0:8, 0:NH],
                           WG[:, kc * 8:(kc + 1) * 8],
                           HF[:, kc * S + c0:kc * S + c0 + NH],
                           start=(kc == 0), stop=(kc == 7),
                           inc="mm" if kc == 7 else None)
            p.wait_all("scalar", [tok, tbgc] + den_g)
            tex = p.op("scalar", "activation", DEN[0:8, 0:NH],
                       bgl[0:8, 0:NH], AF.Exp,
                       bias=BCOL[0:8, 0:1], inc="sc")
            ctx.banks.release(igl, tex)
            p.wait("vector", tex)
            texb = p.op("vector", "tensor_copy", SQT[0:8, 0:NH],
                        DEN[0:8, 0:NH], inc="ve")
            idn, bdn = ctx.banks.acquire(p)
            p.wait("tensor", texb)
            tdn = p.op("tensor", "matmul", bdn[0:1, 0:NH],
                       ctx.cbv("ones_col", rn=8), SQT[0:8, 0:NH],
                       start=True, stop=True, inc="mm")
            p.wait("vector", tdn)
            trg = p.op("vector", "reciprocal", R_c2, bdn[0:1, 0:NH], inc="ve")
            ctx.banks.release(idn, trg)
            irb, brb = ctx.banks.acquire(p)
            p.wait("tensor", trg)
            trb = p.op("tensor", "matmul", brb[0:8, 0:NH],
                       ctx.cfv("ones_row", w=8, rn=1), R_c2,
                       start=True, stop=True, inc="mm")
            p.wait("vector", trb)
            p.wait("vector", g["deni_r"])
            tegn = p.op("vector", "tensor_mul", DENI[0:8, 0:NH],
                        DEN[0:8, 0:NH], brb[0:8, 0:NH], inc="ve")
            ctx.banks.release(irb, tegn)
            den_g = [tegn, texb]
            ig1, bg1 = ctx.banks.acquire(p)
            p.wait("tensor", tegn)
            tg128 = p.op("tensor", "matmul", bg1[0:128, 0:NH], selg,
                         DENI[0:8, 0:NH], start=True, stop=True, inc="mm")
            g["deni_r"] = tg128
            it2, bt2 = ctx.banks.acquire(p)
            tok = None
            for kc in range(NCH):
                tok = p.op("tensor", "matmul", bt2[0:128, 0:NH],
                           WU[:, kc * 128:(kc + 1) * 128],
                           HF[:, kc * S + c0:kc * S + c0 + NH],
                           start=(kc == 0), stop=(kc == 7),
                           inc="mm" if kc == 7 else None)
            p.wait("scalar", tok)
            p.wait("scalar", g.get("tm_r"))
            ttm = p.op("scalar", "activation", TM, bt2[0:128, 0:NH],
                       AF.Copy, inc="sc")
            ctx.banks.release(it2, ttm)
            p.wait("vector", ttm)
            p.wait("vector", tg128)
            p.wait("vector", g.get("gtb_r"))
            tgt = p.op("vector", "tensor_mul", GTB, TM, bg1[0:128, 0:NH],
                       inc="ve")
            ctx.banks.release(ig1, tgt)
            g["tm_r"] = tgt
            p.wait("tensor", tgt)
            ao_toks = []
            for m in range(NCH):
                ib, bk = ctx.banks.acquire(p)
                for kc in range(NCH):
                    p.op("tensor", "matmul", bk[0:128, 0:NH],
                         WT[:, kc * 1024 + m * 128:kc * 1024 + m * 128 + 128],
                         HF[:, kc * S + c0:kc * S + c0 + NH],
                         start=(kc == 0), stop=False)
                tok = p.op("tensor", "matmul", bk[0:128, 0:NH],
                           WV[:, m * 128:(m + 1) * 128], GTB,
                           start=False, stop=True, inc="mm")
                g["gtb_r"] = tok
                p.wait("scalar", tok)
                p.wait_all("scalar", g["ao_w"][m])
                tev = p.op("scalar", "activation", AO[:, m * NH:(m + 1) * NH],
                           bk[0:128, 0:NH], AF.Copy, inc="sc")
                ao_toks.append(tev)
                ctx.banks.release(ib, tev)
            rows = {"mu": R_mu, "var": R_var, "tmp": R_tmp, "iv": R_iv,
                    "c": R_c2, "nmc": R_nmc2}
            done, tcast, tsq = stats_and_coeffs(AO, NH, rows, None,
                                                extra_cast_dep=ao_toks)
            ibc1, bc1 = ctx.banks.acquire(p)
            ibc2, bc2 = ctx.banks.acquire(p)
            tb1 = bcast_row(ctx, bc1, R_c2, NH, dep=done)
            tb2 = bcast_row(ctx, bc2, R_nmc2, NH, dep=done)
            p.wait_all("vector", [tb1, tb2, tx2])
            xn_toks = []
            exit_tt = None
            for cc in range(NCH):
                aoc = AO[:, cc * NH:(cc + 1) * NH]
                xc = XF[:, cc * NH:(cc + 1) * NH]
                p.op("vector", "tensor_mul", aoc, aoc, bc1[0:128, 0:NH])
                exit_tt = p.op("vector", "tensor_add", aoc, aoc,
                               bc2[0:128, 0:NH], inc="ve")
                p.wait("gpsimd", exit_tt)
                tg_ = p.op("gpsimd", "tensor_add", aoc, aoc, xc, inc="gp")
                xn_toks.append(tg_)
            ctx.banks.release(ibc1, exit_tt)
            ctx.banks.release(ibc2, exit_tt)
            p.wait_all("sync", xn_toks)
            tst = p.op("sync", "dma_start", inc="st", incn=16,
                       out=x_out[:, :, c0:c0 + NH].rearrange("c p w -> p c w"),
                       in_=AO.rearrange("p (c w) -> p c w", w=NH))
            for cc in range(NCH):
                g["ao_w"][cc] = [xn_toks[cc], tcast, tsq, tst]
            g["xf_w"] = [xn_toks[-1], exit_tt, tcast, tsq]

    ba.pop(); fa.pop()
    return tst


def host_weights_l0f(inputs):
    T = lambda a: np.ascontiguousarray(np.asarray(a, np.float32).T)
    uallT = np.ascontiguousarray(
        np.transpose(np.asarray(inputs["L0_ffn_lora_U"], np.float32),
                     (2, 0, 1)).reshape(1024, 128))
    vall = np.ascontiguousarray(
        np.transpose(np.asarray(inputs["L0_ffn_lora_V"], np.float32),
                     (0, 2, 1)).reshape(128, 1024))
    return {
        "l0f_wT": T(inputs["L0_ffn_weight"]),
        "l0f_gwT": T(inputs["L0_ffn_gate_w"]),
        "l0f_uT": uallT,
        "l0f_vall": vall,
    }


L0F_WSHAPES = {"l0f_wT": [1024, 1024], "l0f_gwT": [1024, 8],
               "l0f_uT": [1024, 128], "l0f_vall": [128, 1024]}


def build_consts_l0f(c):
    selg = np.zeros((128, 128), np.float32)
    for e in range(8):
        selg[e, 16 * e:16 * e + 16] = 1.0 / 16.0
    c.add_f32("selg", selg)



def emit_l1a(ctx, x_in, x_out, W):
    p = ctx.p
    ba, fa = ctx.ba, ctx.fa
    ba.push(); fa.push()
    cm, g, rows_ = ctx.cm, ctx.g, ctx.rows
    ell = ctx.ell

    XN = ba.alloc(NCH * S)
    HID = ba.alloc(24 * NQ)
    WM3 = [ba.alloc(3 * 1024) for _ in range(2)]
    WFF = [ba.alloc(2 * 1024) for _ in range(2)]
    WOS = [ba.alloc(3 * 1024) for _ in range(2)]
    SGB = ba.alloc(NQ)
    UPB = ba.alloc(NQ)

    fo_off = fa.off
    SCQ = fa.alloc(4 * NQ)
    RC2B = fa.alloc(2048)
    FO = fa.t[:, fo_off:fo_off + NCH * NQ]
    XSA = SCQ[:, 0:NQ]
    TMP = SCQ[:, NQ:2 * NQ]
    LM = SCQ[:, 2 * NQ:3 * NQ]
    SCO = SCQ[:, 3 * NQ:4 * NQ]
    RC2ROW = RC2B[0:1, 0:S]
    SCSTB = fa.alloc(1)
    SCST = SCSTB[0:128, 0:1]

    XF, AO, SC1 = cm["XF"], cm["AO"], cm["SC1"]
    XB, XSQ, SQT = cm["XB"], cm["XSQ"], cm["SQT"]
    R_mu, R_var, R_tmp, R_iv = rows_["mu"], rows_["var"], rows_["tmp"], rows_["iv"]
    R_ctot, R_nmct = rows_["ctot"], rows_["nmct"]
    R_c2, R_nmc2 = rows_["c2"], rows_["nmc2"]
    CROW, NMCROW = rows_["CROW"], rows_["NMCROW"]

    bg = ctx.cfv("l1a_bg", w=NCH)
    bv = ctx.cfv("l1a_bv", w=NCH)
    bd = ctx.cfv("l1a_bd", w=NCH)
    wmix = ctx.cfv("l1a_wmix", w=NCH)
    wffn = ctx.cfv("l1a_wffn", w=NCH)

    def wload_cols(dst, wname, c0, ncols, kchunks, wait=None):
        p.wait("sync", ctx.wtok.get(wname))
        p.wait("sync", wait)
        return p.op("sync", "dma_start", inc="ld", incn=16,
                    out=dst[:, 0:kchunks * ncols].rearrange(
                        "p (c m) -> p c m", m=ncols),
                    in_=W[wname][:, c0:c0 + ncols].rearrange(
                        "(c p) m -> p c m", p=128))

    def halfload(dram, c0, width, dst):
        return p.op("sync", "dma_start", inc="ld", incn=16,
                    out=dst.rearrange("p (c w) -> p c w", w=width),
                    in_=dram[:, :, c0:c0 + width].rearrange("c p w -> p c w"))

    def halfstore(dram, c0, width, srcv):
        return p.op("sync", "dma_start", inc="st", incn=16,
                    out=dram[:, :, c0:c0 + width].rearrange("c p w -> p c w"),
                    in_=srcv.rearrange("p (c w) -> p c w", w=width))

    def stats_and_coeffs(srcf, width, rows, fused_eps2, extra_cast_dep=()):
        p.wait("vector", g["xb_mm"])
        p.wait_all("vector", [g["slot"][0], g["slot"][1]])
        p.wait_all("vector", extra_cast_dep)
        tcast = p.op("vector", "tensor_copy", XB[:, 0:NCH * width], srcf,
                     inc="ve")
        g["slot"][0] = tcast
        g["slot"][1] = tcast
        p.wait_all("scalar", [g["slot"][2], g["slot"][3]])
        p.wait("scalar", g["xsq_mm"])
        p.wait_all("scalar", extra_cast_dep)
        tsq = p.op("scalar", "activation", XSQ[:, 0:NCH * width], srcf,
                   AF.Square, inc="sc")
        g["slot"][2] = tsq
        g["slot"][3] = tsq
        isx, bsx = ctx.banks.acquire(p)
        isq, bsq = ctx.banks.acquire(p)
        t1 = colsum_mms(ctx, bsx, XB[:, 0:NCH * width], width, dep=(tcast,))
        t2 = colsum_mms(ctx, bsq, XSQ[:, 0:NCH * width], width, dep=(tsq,))
        g["xb_mm"], g["xsq_mm"] = t1, t2
        done, tfree = ln_coeffs(ctx, bsx, bsq, rows, width, (t1, t2),
                                want_outer=True, fused_eps2=fused_eps2)
        ctx.banks.release(isx, tfree)
        ctx.banks.release(isq, tfree)
        return done, tcast, tsq

    xn_toks_all = []
    for n in range(NG):
        for hh in range(HPN):
            c0 = n * NQ + hh * NH
            p.wait_all("sync", g["xf_w"])
            tx = halfload(x_in, c0, NH, XF)
            p.wait("vector", tx)
            p.wait("scalar", tx)
            rows = {"mu": R_mu, "var": R_var, "tmp": R_tmp, "iv": R_iv,
                    "c": CROW[:, c0:c0 + NH], "nmc": NMCROW[:, c0:c0 + NH],
                    "ctot": R_ctot, "nmct": R_nmct}
            done, tcast, tsq = stats_and_coeffs(XF, NH, rows, 1e-6)
            ib1, bc1 = ctx.banks.acquire(p)
            ib2, bc2 = ctx.banks.acquire(p)
            ib3, bc3 = ctx.banks.acquire(p)
            ib4, bc4 = ctx.banks.acquire(p)
            tb1 = bcast_row(ctx, bc1, CROW[:, c0:c0 + NH], NH, dep=done)
            tb2 = bcast_row(ctx, bc2, NMCROW[:, c0:c0 + NH], NH, dep=done)
            tb3 = bcast_row(ctx, bc3, R_ctot, NH, dep=done)
            tb4 = bcast_row(ctx, bc4, R_nmct, NH, dep=done)
            p.wait_all("vector", [tb1, tb2, tb3, tb4] + g["sc1_w"])
            for cc in range(NCH):
                p.wait_all("vector", g["ao_w"][cc])
            last_l = None
            last_xn = None
            for cc in range(NCH):
                xc = XF[:, cc * NH:(cc + 1) * NH]
                s1 = SC1[:, cc * NH:(cc + 1) * NH]
                ac = AO[:, cc * NH:(cc + 1) * NH]
                p.op("vector", "tensor_mul", s1, xc, bc1[0:128, 0:NH])
                last_l = p.op("vector", "tensor_add", s1, s1,
                              bc2[0:128, 0:NH], inc="ve")
                p.op("vector", "tensor_mul", ac, xc, bc3[0:128, 0:NH])
                p.op("vector", "tensor_add", ac, ac, bc4[0:128, 0:NH])
                last_xn = p.op("vector", "tensor_scalar",
                               XN[:, cc * S + c0:cc * S + c0 + NH], ac,
                               wmix[:, cc:cc + 1], None, ALU.mult, inc="ve")
            p.wait("sync", last_l)
            tst_l = halfstore(ell, c0, NH, SC1)
            xn_toks_all.append(last_xn)
            g["xf_w"] = [last_l, last_xn, tsq, tcast]
            g["sc1_w"] = [tst_l]
            g["ao_w"] = [[last_xn] for _ in range(NCH)]
            for ii in (ib1, ib2, ib3, ib4):
                ctx.banks.release(ii, last_xn)

    qb = [ctx.banks.acquire(p) for _ in range(4)]
    p.wait_all("tensor", xn_toks_all)
    mix_w_tok = [None, None]
    prev_store = None
    prev_scan = None
    prev_sel = None
    prev_sq2 = None
    for m in range(NCH):
        slot = m % 2
        wload_cols(WM3[slot], "l1a_Wg", m * 128, 128, 8, wait=mix_w_tok[slot])
        wload_cols(WM3[slot][:, 1024:2048], "l1a_Wv", m * 128, 128, 8)
        tw = wload_cols(WM3[slot][:, 2048:3072], "l1a_Wd", m * 128, 128, 8)
        for qd in range(4):
            q0 = qd * NQ
            outs = []
            p.wait("tensor", tw)
            for wi in range(3):
                ib, bk = ctx.banks.acquire(p)
                tok = None
                for kc in range(NCH):
                    tok = p.op("tensor", "matmul", bk[0:128, 0:NQ],
                               WM3[slot][:, wi * 1024 + kc * 128:
                                         wi * 1024 + kc * 128 + 128],
                               XN[:, kc * S + q0:kc * S + q0 + NQ],
                               start=(kc == 0), stop=(kc == 7),
                               inc="mm" if kc == 7 else None)
                outs.append((ib, bk, tok))
            mix_w_tok[slot] = outs[2][2]
            ibg_, bkg, tokg = outs[0]
            ibv_, bkv, tokv = outs[1]
            ibd_, bkd, tokd = outs[2]
            p.wait("scalar", tokg)
            p.wait("scalar", prev_scan)
            tsg = p.op("scalar", "activation", XSA, bkg[0:128, 0:NQ],
                       AF.Sigmoid, bias=bg[:, m:m + 1], inc="sc")
            ctx.banks.release(ibg_, tsg)
            p.wait("scalar", tokv)
            ttv = p.op("scalar", "activation", TMP, bkv[0:128, 0:NQ],
                       AF.Tanh, bias=bv[:, m:m + 1], inc="sc")
            ctx.banks.release(ibv_, ttv)
            p.wait("vector", tsg)
            p.wait("vector", ttv)
            txs = p.op("vector", "tensor_mul", XSA, XSA, TMP, inc="ve")
            p.wait("scalar", tokd)
            p.wait("scalar", txs)
            tsd = p.op("scalar", "activation", TMP, bkd[0:128, 0:NQ],
                       AF.Sigmoid, bias=bd[:, m:m + 1], inc="sc")
            ctx.banks.release(ibd_, tsd)
            p.wait("vector", tsd)
            ta = p.op("vector", "tensor_scalar", TMP, TMP, 0.998, 0.001,
                      ALU.mult, ALU.add, inc="ve")
            p.wait("vector", ta)
            p.wait("vector", g.get("sco_r"))
            init = 0.0 if qd == 0 else SCST
            tsc = p.op("vector", "tensor_tensor_scan", SCO, TMP, XSA, init,
                       ALU.mult, ALU.add, inc="ve")
            p.wait("vector", tsc)
            tcar = p.op("vector", "tensor_copy", SCST, SCO[:, NQ - 1:NQ],
                        inc="ve")
            prev_scan = tcar
            p.wait("sync", prev_store)
            p.wait("sync", prev_sq2)
            tlm = p.op("sync", "dma_start", inc="ld", incn=16,
                       out=LM, in_=ell[m, :, q0:q0 + NQ])
            p.wait("gpsimd", tlm)
            p.wait("gpsimd", tsc)
            tl2 = p.op("gpsimd", "tensor_add", LM, LM, SCO, inc="gp")
            g["sco_r"] = tl2
            p.wait("sync", tl2)
            prev_store = p.op("sync", "dma_start", inc="st", incn=16,
                              out=ell[m, :, q0:q0 + NQ], in_=LM)
            p.wait("scalar", tl2)
            p.wait("scalar", prev_sel)
            tsq2 = p.op("scalar", "activation", SQT, LM, AF.Square, inc="sc")
            prev_sq2 = tsq2
            p.wait("tensor", tsq2)
            iq, bq = qb[qd]
            prev_sel = p.op("tensor", "matmul", bq[0:1, 0:NQ],
                            ctx.cbv("ones_col"), SQT,
                            start=(m == 0), stop=(m == 7), inc="mm")
    p.wait("vector", prev_sel)
    p.wait("vector", g.get("rows_r"))
    tq1 = None
    for qd in range(4):
        iq, bq = qb[qd]
        tq1 = p.op("vector", "tensor_scalar", RC2ROW[:, qd * NQ:(qd + 1) * NQ],
                   bq[0:1, 0:NQ], 1.0 / D, 1e-6, ALU.mult, ALU.add, inc="ve")
        ctx.banks.release(iq, tq1)
    p.wait("vector", tq1)
    trc = p.op("vector", "reciprocal", RC2ROW, RC2ROW, inc="ve")
    p.wait("scalar", trc)
    trc2 = p.op("scalar", "activation", RC2ROW, RC2ROW, AF.Sqrt, inc="sc")

    yn_toks = []
    xn_guard = ("mm", p.counts["mm"])
    for n in range(NG):
        for hh in range(HPN):
            c0 = n * NQ + hh * NH
            p.wait_all("sync", g["xf_w"])
            tx = halfload(ell, c0, NH, XF)
            ibc, bcq = ctx.banks.acquire(p)
            tbq = bcast_row(ctx, bcq, RC2ROW[:, c0:c0 + NH], NH, dep=(trc2,))
            p.wait_all("vector", [tx, tbq, xn_guard] + g["sc1_w"])
            last = None
            for cc in range(NCH):
                xc = XF[:, cc * NH:(cc + 1) * NH]
                s1 = SC1[:, cc * NH:(cc + 1) * NH]
                p.op("vector", "tensor_mul", s1, xc, bcq[0:128, 0:NH])
                last = p.op("vector", "tensor_scalar",
                            XN[:, cc * S + c0:cc * S + c0 + NH], s1,
                            wffn[:, cc:cc + 1], None, ALU.mult, inc="ve")
            yn_toks.append(last)
            ctx.banks.release(ibc, last)
            g["xf_w"] = [last, tx]
            g["sc1_w"] = [last]

    p.wait_all("tensor", yn_toks)
    ff_w_tok = [None, None]
    wo_tok = [None, None]
    fo_guard = [[] for _ in range(NCH)]
    hid_guard = None
    tst = None
    for n in range(NG):
        nq0 = n * NQ
        hid_toks = []
        p.wait("vector", hid_guard)
        for f in range(24):
            slot = f % 2
            twf = wload_cols(WFF[slot], "l1a_Wgate", f * 128, 128, 8,
                             wait=ff_w_tok[slot])
            twf2 = wload_cols(WFF[slot][:, 1024:2048], "l1a_Wup",
                              f * 128, 128, 8)
            ibg_, bkg = ctx.banks.acquire(p)
            ibu_, bku = ctx.banks.acquire(p)
            p.wait("tensor", twf2)
            tokg = toku = None
            for kc in range(NCH):
                tokg = p.op("tensor", "matmul", bkg[0:128, 0:NQ],
                            WFF[slot][:, kc * 128:kc * 128 + 128],
                            XN[:, kc * S + nq0:kc * S + nq0 + NQ],
                            start=(kc == 0), stop=(kc == 7),
                            inc="mm" if kc == 7 else None)
            for kc in range(NCH):
                toku = p.op("tensor", "matmul", bku[0:128, 0:NQ],
                            WFF[slot][:, 1024 + kc * 128:1024 + kc * 128 + 128],
                            XN[:, kc * S + nq0:kc * S + nq0 + NQ],
                            start=(kc == 0), stop=(kc == 7),
                            inc="mm" if kc == 7 else None)
            ff_w_tok[slot] = toku
            p.wait("scalar", tokg)
            p.wait("scalar", g.get("sgb_r"))
            tsg = p.op("scalar", "activation", SGB, bkg[0:128, 0:NQ],
                       AF.Silu, inc="sc")
            ctx.banks.release(ibg_, tsg)
            p.wait("scalar", toku)
            tup = p.op("scalar", "activation", UPB, bku[0:128, 0:NQ],
                       AF.Copy, inc="sc")
            ctx.banks.release(ibu_, tup)
            p.wait("vector", tsg)
            p.wait("vector", tup)
            th = p.op("vector", "tensor_mul", HID[:, f * NQ:(f + 1) * NQ],
                      SGB, UPB, inc="ve")
            g["sgb_r"] = th
            hid_toks.append(th)
        p.wait_all("tensor", hid_toks)
        fo_toks = []
        for m in range(NCH):
            slot = m % 2
            two = wload_cols(WOS[slot], "l1a_Wout", m * 128, 128, 24,
                             wait=wo_tok[slot])
            ib, bk = ctx.banks.acquire(p)
            p.wait("tensor", two)
            tok = None
            for kc in range(24):
                tok = p.op("tensor", "matmul", bk[0:128, 0:NQ],
                           WOS[slot][:, kc * 128:kc * 128 + 128],
                           HID[:, kc * NQ:(kc + 1) * NQ],
                           start=(kc == 0), stop=(kc == 23),
                           inc="mm" if kc == 23 else None)
            wo_tok[slot] = tok
            p.wait("scalar", tok)
            p.wait_all("scalar", fo_guard[m])
            tev = p.op("scalar", "activation", FO[:, m * NQ:(m + 1) * NQ],
                       bk[0:128, 0:NQ], AF.Copy, inc="sc")
            fo_toks.append(tev)
            ctx.banks.release(ib, tev)
        hid_guard = tok
        for hh in range(HPN):
            c0 = nq0 + hh * NH
            p.wait_all("sync", g["xf_w"])
            tx2 = halfload(x_in, c0, NH, XF)
            for cc in range(NCH):
                p.wait_all("sync", g["ao_w"][cc])
            tl2 = halfload(ell, c0, NH, AO)
            p.wait_all("gpsimd", fo_toks + [tl2])
            tgs = []
            for cc in range(NCH):
                fslice = FO[:, cc * NQ + hh * NH:cc * NQ + hh * NH + NH]
                tg_ = p.op("gpsimd", "tensor_add", fslice, fslice,
                           AO[:, cc * NH:(cc + 1) * NH], inc="gp")
                tgs.append(tg_)
            rows = {"mu": R_mu, "var": R_var, "tmp": R_tmp, "iv": R_iv,
                    "c": R_c2, "nmc": R_nmc2}
            fof = FO.rearrange("p (c x) -> p c x", x=NQ)[:, :, hh * NH:hh * NH + NH]
            done, tcast, tsq = stats_and_coeffs(fof, NH, rows, None,
                                                extra_cast_dep=tgs)
            ibc1, bc1 = ctx.banks.acquire(p)
            ibc2, bc2 = ctx.banks.acquire(p)
            tb1 = bcast_row(ctx, bc1, R_c2, NH, dep=done)
            tb2 = bcast_row(ctx, bc2, R_nmc2, NH, dep=done)
            p.wait_all("vector", [tb1, tb2, tx2] + g["sc1_w"])
            xn_toks = []
            exit_tt = None
            for cc in range(NCH):
                foc = FO[:, cc * NQ + hh * NH:cc * NQ + hh * NH + NH]
                s1 = SC1[:, cc * NH:(cc + 1) * NH]
                xc = XF[:, cc * NH:(cc + 1) * NH]
                p.op("vector", "tensor_mul", s1, foc, bc1[0:128, 0:NH])
                exit_tt = p.op("vector", "tensor_add", s1, s1,
                               bc2[0:128, 0:NH], inc="ve")
                p.wait("gpsimd", exit_tt)
                tg_ = p.op("gpsimd", "tensor_add", s1, s1, xc, inc="gp")
                xn_toks.append(tg_)
            ctx.banks.release(ibc1, exit_tt)
            ctx.banks.release(ibc2, exit_tt)
            p.wait_all("sync", xn_toks)
            tst = p.op("sync", "dma_start", inc="st", incn=16,
                       out=x_out[:, :, c0:c0 + NH].rearrange("c p w -> p c w"),
                       in_=SC1.rearrange("p (c w) -> p c w", w=NH))
            g["xf_w"] = [xn_toks[-1], exit_tt, tcast, tsq]
            g["sc1_w"] = [tst] + xn_toks
            g["ao_w"] = [[tgs[cc]] for cc in range(NCH)]
            fo_guard = [[xn_toks[cc], tcast, tsq] for cc in range(NCH)]

    ba.pop(); fa.pop()
    return tst


def build_consts_l1a(c, inputs):
    c.add_f32("l1a_bg", col_chunks(inputs["L1_attn_bg"]))
    c.add_f32("l1a_bv", col_chunks(inputs["L1_attn_bv"]))
    c.add_f32("l1a_bd", col_chunks(inputs["L1_attn_bd"]))
    c.add_f32("l1a_wmix", col_chunks(inputs["L1_attn_w_rms_mix"]))
    c.add_f32("l1a_wffn", col_chunks(inputs["L1_attn_w_rms_ffn"]))


def host_weights_l1a(inputs):
    A = lambda a: np.ascontiguousarray(np.asarray(a, np.float32))
    return {
        "l1a_Wg": A(inputs["L1_attn_Wg"]),
        "l1a_Wv": A(inputs["L1_attn_Wv"]),
        "l1a_Wd": A(inputs["L1_attn_Wd"]),
        "l1a_Wgate": A(inputs["L1_attn_W_gate"]),
        "l1a_Wup": A(inputs["L1_attn_W_up"]),
        "l1a_Wout": A(inputs["L1_attn_W_out"]),
    }


L1A_WSHAPES = {"l1a_Wg": [1024, 1024], "l1a_Wv": [1024, 1024],
               "l1a_Wd": [1024, 1024], "l1a_Wgate": [1024, 3072],
               "l1a_Wup": [1024, 3072], "l1a_Wout": [3072, 1024]}



def emit_l1f(ctx, x_in, x_out, W):
    p = ctx.p
    ba, fa = ctx.ba, ctx.fa
    ba.push(); fa.push()
    cm, g, rows_ = ctx.cm, ctx.g, ctx.rows
    ell = ctx.ell

    HB = ba.alloc(NCH * NQ)
    WQ = [ba.alloc(NCH * 512) for _ in range(2)]
    VT2 = ba.alloc(16 * 520)
    qb2_off = ba.off
    QB2 = ba.alloc(4 * S)
    KB2 = ba.alloc(4 * S)
    HID2 = ba.t[:, qb2_off:qb2_off + 22 * NQ]
    XPT = ba.alloc(2 * NQ)
    AT2 = ba.alloc(4 * NQ)
    QRAW = ba.alloc(4 * NQ)
    KRAW = ba.alloc(4 * NQ)
    W1S = [ba.alloc(NCH * 128) for _ in range(2)]
    W2S = [ba.alloc(22 * 128) for _ in range(2)]
    WOS2 = [ba.alloc(4 * 128) for _ in range(2)]
    GEL = ba.alloc(NQ)
    ABF = ba.alloc(NQ)

    XW = fa.alloc(NCH * NQ)
    RNB = fa.alloc(NQ)
    HCAR = fa.alloc(8)

    XF, AO, SC1 = cm["XF"], cm["AO"], cm["SC1"]
    XB, XSQ, SQT = cm["XB"], cm["XSQ"], cm["SQT"]
    PT = cm["PT"]
    TIN, DEN, DENI = cm["TIN"], cm["DEN"], cm["DENI"]
    BCOL, SMV = cm["BCOL"], cm["SMV"]
    TBa, TBb, TBr = cm["TBa"], cm["TBb"], cm["TBr"]
    R_mu, R_var, R_tmp, R_iv = rows_["mu"], rows_["var"], rows_["tmp"], rows_["iv"]
    R_ctot, R_nmct = rows_["ctot"], rows_["nmct"]
    ident = ctx.cfv("ident_f32", w=128)
    identb = ctx.cbv("ident_bf", w=128)
    permrh = ctx.cbv("perm_rothalf", w=128)
    maskc = ctx.cbv("l1f_maskc", w=128)
    maskw = ctx.cbv("l1f_maskw", w=128)

    def wload_cols(dst, wname, li, c0, ncols, kchunks, wait=None):
        p.wait("sync", ctx.wtok.get(wname))
        p.wait("sync", wait)
        return p.op("sync", "dma_start", inc="ld", incn=16,
                    out=dst[:, 0:kchunks * ncols].rearrange(
                        "p (c m) -> p c m", m=ncols),
                    in_=W[wname][li][:, c0:c0 + ncols].rearrange(
                        "(c p) m -> p c m", p=128))

    def halfload(dram, c0, width, dst):
        return p.op("sync", "dma_start", inc="ld", incn=16,
                    out=dst.rearrange("p (c w) -> p c w", w=width),
                    in_=dram[:, :, c0:c0 + width].rearrange("c p w -> p c w"))

    def stats_and_coeffs(srcf, width, rows, fused_eps2, extra_cast_dep=()):
        p.wait("vector", g["xb_mm"])
        p.wait_all("vector", [g["slot"][0], g["slot"][1]])
        p.wait_all("vector", extra_cast_dep)
        tcast = p.op("vector", "tensor_copy", XB[:, 0:NCH * width], srcf,
                     inc="ve")
        g["slot"][0] = tcast
        g["slot"][1] = tcast
        p.wait_all("scalar", [g["slot"][2], g["slot"][3]])
        p.wait("scalar", g["xsq_mm"])
        p.wait_all("scalar", extra_cast_dep)
        tsq = p.op("scalar", "activation", XSQ[:, 0:NCH * width], srcf,
                   AF.Square, inc="sc")
        g["slot"][2] = tsq
        g["slot"][3] = tsq
        isx, bsx = ctx.banks.acquire(p)
        isq, bsq = ctx.banks.acquire(p)
        t1 = colsum_mms(ctx, bsx, XB[:, 0:NCH * width], width, dep=(tcast,))
        t2 = colsum_mms(ctx, bsq, XSQ[:, 0:NCH * width], width, dep=(tsq,))
        g["xb_mm"], g["xsq_mm"] = t1, t2
        done, tfree = ln_coeffs(ctx, bsx, bsq, rows, width, (t1, t2),
                                want_outer=True, fused_eps2=fused_eps2)
        ctx.banks.release(isx, tfree)
        ctx.banks.release(isq, tfree)
        return done, tcast, tsq

    for n in range(NG):
        for hh in range(HPN):
            c0 = n * NQ + hh * NH
            p.wait_all("sync", g["xf_w"])
            tx = halfload(x_in, c0, NH, XF)
            p.wait("vector", tx)
            p.wait("scalar", tx)
            rows = {"mu": R_mu, "var": R_var, "tmp": R_tmp, "iv": R_iv,
                    "c": R_ctot, "nmc": R_nmct}
            done, tcast, tsq = stats_and_coeffs(XF, NH, rows, None)
            ib1, bc1 = ctx.banks.acquire(p)
            ib2, bc2 = ctx.banks.acquire(p)
            tb1 = bcast_row(ctx, bc1, R_ctot, NH, dep=done)
            tb2 = bcast_row(ctx, bc2, R_nmct, NH, dep=done)
            p.wait_all("vector", [tb1, tb2] + g["sc1_w"])
            last_l = None
            for cc in range(NCH):
                xc = XF[:, cc * NH:(cc + 1) * NH]
                s1 = SC1[:, cc * NH:(cc + 1) * NH]
                p.op("vector", "tensor_mul", s1, xc, bc1[0:128, 0:NH])
                last_l = p.op("vector", "tensor_add", s1, s1,
                              bc2[0:128, 0:NH], inc="ve")
            p.wait("sync", last_l)
            tst_l = p.op("sync", "dma_start", inc="st", incn=16,
                         out=ell[:, :, c0:c0 + NH].rearrange("c p w -> p c w"),
                         in_=SC1.rearrange("p (c w) -> p c w", w=NH))
            g["xf_w"] = [last_l, tsq, tcast]
            g["sc1_w"] = [tst_l]
            ctx.banks.release(ib1, last_l)
            ctx.banks.release(ib2, last_l)

    tst = None
    for li in range(2):
        qs8 = ctx.cfv(f"l1f_qs{li}", w=1)
        ksc = ctx.cfv(f"l1f_ks{li}", w=1)
        lg = ctx.cfv(f"l1f_lg{li}", w=NCH)
        lb = ctx.cfv(f"l1f_lb{li}", w=NCH)
        b1c = ctx.cfv(f"l1f_b1_{li}", w=44)
        b2c = ctx.cfv(f"l1f_b2_{li}", w=NCH)

        p.wait("scalar", g["tin_s"]) if g.get("tin_s") else None
        p.wait("scalar", g["tin_r"])
        tq1 = p.op("scalar", "activation", TIN[0:128, 0:1], qs8, AF.Square,
                   inc="sc")
        it, btr = ctx.banks.acquire(p)
        p.wait("tensor", tq1)
        ttr = p.op("tensor", "transpose", btr[0:128, 0:128], TIN, ident,
                   inc="mm")
        p.wait("vector", ttr)
        tqm = p.op("vector", "reduce_max", SMV[0:1, 0:1], btr[0:1, 0:128],
                   axis=AX.X, inc="ve")
        ctx.banks.release(it, tqm)
        p.wait("scalar", tqm)
        tk1 = p.op("scalar", "activation", TIN[0:128, 0:1], ksc, AF.Square,
                   inc="sc")
        g["tin_s"] = tk1
        it, btr = ctx.banks.acquire(p)
        p.wait("tensor", tk1)
        ttr = p.op("tensor", "transpose", btr[0:128, 0:128], TIN, ident,
                   inc="mm")
        g["tin_r"] = ttr
        p.wait("vector", ttr)
        tkm = p.op("vector", "reduce_max", SMV[0:1, 1:2], btr[0:1, 0:128],
                   axis=AX.X, inc="ve")
        ctx.banks.release(it, tkm)
        p.wait("vector", tkm)
        tb2q = p.op("vector", "scalar_tensor_tensor", SMV[0:1, 2:3],
                    SMV[0:1, 0:1], 1.0, SMV[0:1, 1:2], ALU.mult, ALU.mult,
                    inc="ve")
        p.wait("scalar", tb2q)
        tsb = p.op("scalar", "activation", SMV[0:1, 3:4], SMV[0:1, 2:3],
                   AF.Sqrt, inc="sc")
        p.wait("vector", tsb)
        tnb = p.op("vector", "tensor_scalar_mul", SMV[0:1, 2:3],
                   SMV[0:1, 3:4], -1.0, inc="ve")
        ibb, bbb = ctx.banks.acquire(p)
        p.wait("tensor", tnb)
        tbm = p.op("tensor", "matmul", bbb[0:128, 0:1],
                   ctx.cfv("ones_row", w=128, rn=1), SMV[0:1, 2:3],
                   start=True, stop=True, inc="mm")
        p.wait("scalar", tbm)
        tbc = p.op("scalar", "activation", BCOL[0:128, 0:1], bbb[0:128, 0:1],
                   AF.Copy, inc="sc")
        ctx.banks.release(ibb, tbc)

        p.wait("vector", g.get("vt2_r"))
        t_vones = p.op("vector", "memset", VT2[:, 64:16 * 520:65], 1.0,
                       inc="ve")
        hcar_z = p.op("vector", "memset", HCAR, 0.0, inc="ve")
        wq_tok = [None, None]
        qkv_last = None
        for n in range(NG):
            nq0 = n * NQ
            hb_toks = []
            for hh in range(HPN):
                c0 = nq0 + hh * NH
                p.wait_all("sync", g["xf_w"])
                tx = halfload(ell, c0, NH, XF)
                p.wait("vector", tx)
                p.wait("vector", g.get("hb_r"))
                p.wait("vector", hcar_z)
                tcp = None
                for cc in range(NCH):
                    dst0 = cc * NQ + hh * NH
                    if cc < 4:
                        tcp = p.op("vector", "tensor_copy",
                                   HB[:, dst0:dst0 + NH],
                                   XF[:, cc * NH:(cc + 1) * NH], inc="ve")
                    else:
                        hcv = HCAR[:, cc - 4:cc - 3]
                        tcp = p.op("vector", "tensor_copy",
                                   HB[:, dst0:dst0 + 1], hcv, inc="ve")
                        tcp = p.op("vector", "tensor_copy",
                                   HB[:, dst0 + 1:dst0 + NH],
                                   XF[:, cc * NH:(cc + 1) * NH - 1], inc="ve")
                        p.wait("vector", tcp)
                        tcp = p.op("vector", "tensor_copy", hcv,
                                   XF[:, (cc + 1) * NH - 1:(cc + 1) * NH],
                                   inc="ve")
                hb_toks.append(tcp)
                g["xf_w"] = [tcp, tx]
            p.wait_all("tensor", hb_toks)
            inq, bnq = ctx.banks.acquire(p)
            ink, bnk = ctx.banks.acquire(p)
            sel_tok = {"q": None, "k": None}
            for which, dstraw, bnx in (("q", QRAW, bnq), ("k", KRAW, bnk)):
                base_f = 0 if which == "q" else 512
                for mq in range(4):
                    slot = mq % 2
                    tw = wload_cols(WQ[slot], "l1f_wqkv", li,
                                    base_f + mq * 128, 128, 8,
                                    wait=wq_tok[slot])
                    ib, bk = ctx.banks.acquire(p)
                    p.wait("tensor", tw)
                    tok = None
                    for kc in range(NCH):
                        tok = p.op("tensor", "matmul", bk[0:128, 0:NQ],
                                   WQ[slot][:, kc * 128:kc * 128 + 128],
                                   HB[:, kc * NQ:(kc + 1) * NQ],
                                   start=(kc == 0), stop=(kc == 7),
                                   inc="mm" if kc == 7 else None)
                    wq_tok[slot] = tok
                    p.wait("scalar", tok)
                    p.wait_all("scalar", g.get(f"raw_r_{which}_{mq}", []))
                    tev = p.op("scalar", "activation",
                               dstraw[:, mq * NQ:(mq + 1) * NQ],
                               bk[0:128, 0:NQ], AF.Copy, inc="sc")
                    ctx.banks.release(ib, tev)
                    p.wait("scalar", g.get("sqt_r"))
                    tsq_ = p.op("scalar", "activation", SQT,
                                dstraw[:, mq * NQ:(mq + 1) * NQ], AF.Square,
                                inc="sc")
                    p.wait("tensor", tsq_)
                    sel_tok[which] = p.op(
                        "tensor", "matmul", bnx[0:16, 0:NQ],
                        ctx.cbv(f"sel_qk_{mq}", w=16), SQT,
                        start=(mq == 0), stop=(mq == 3), inc="mm")
                    g["sqt_r"] = sel_tok[which]
            tw = wload_cols(WQ[0], "l1f_wqkv", li, 1024, 512, 8,
                            wait=wq_tok[0])
            vtok = None
            for tt4 in range(4):
                tt = n * 4 + tt4
                ib, bk = ctx.banks.acquire(p)
                p.wait("tensor", tw)
                tok = None
                for kc in range(NCH):
                    tok = p.op("tensor", "matmul", bk[0:128, 0:NQ],
                               HB[:, kc * NQ + tt4 * 128:
                                  kc * NQ + tt4 * 128 + 128],
                               WQ[0][:, kc * 512:(kc + 1) * 512],
                               start=(kc == 0), stop=(kc == 7),
                               inc="mm" if kc == 7 else None)
                vtok = tok
                p.wait("vector", tok)
                p.wait("vector", t_vones)
                base = tt * 520
                dst = VT2[:, base:base + 520].rearrange(
                    "p (h dd) -> p h dd", dd=65)[:, :, 0:64]
                tev = p.op("vector", "tensor_copy", dst,
                           bk[0:128, 0:NQ].rearrange("p (h d) -> p h d", d=64),
                           inc="ve")
                ctx.banks.release(ib, tev)
                g["vt2_last"] = tev
            wq_tok[0] = vtok
            tfin = None
            for which, bnx, inx in (("q", bnq, inq), ("k", bnk, ink)):
                p.wait("scalar", sel_tok[which])
                p.wait("scalar", g.get("rnb_r"))
                tsr = p.op("scalar", "activation", RNB[0:8, 0:NQ],
                           bnx[0:8, 0:NQ], AF.Sqrt, inc="sc")
                ctx.banks.release(inx, tsr)
                p.wait("vector", tsr)
                tmx = p.op("vector", "tensor_scalar_max", RNB[0:8, 0:NQ],
                           RNB[0:8, 0:NQ], 1e-12, inc="ve")
                p.wait("vector", tmx)
                p.wait("vector", g.get("deni8_r"))
                trc = p.op("vector", "reciprocal", DENI[0:8, 0:NQ],
                           RNB[0:8, 0:NQ], inc="ve")
                g["rnb_r"] = trc
                sraw = QRAW if which == "q" else KRAW
                sdst = QB2 if which == "q" else KB2
                scol = qs8 if which == "q" else ksc
                r0 = 0 if which == "q" else 256
                p.wait("sync", g.get("xpt_r"))
                txp = p.op("sync", "dma_start", inc="ld", incn=16,
                           out=XPT.rearrange("p (c w) -> p c w", w=NQ),
                           in_=W["l1f_xpos"][r0:r0 + 256, nq0:nq0 + NQ].rearrange(
                               "(c p) w -> p c w", p=128))
                ctab = XPT[:, 0:NQ]
                stab = XPT[:, NQ:2 * NQ]
                for mq in range(4):
                    irn, brn = ctx.banks.acquire(p)
                    p.wait("tensor", trc)
                    trb = p.op("tensor", "matmul", brn[0:128, 0:NQ],
                               ctx.cfv(f"sel8_{mq}", w=128, rn=8),
                               DENI[0:8, 0:NQ],
                               start=True, stop=True, inc="mm")
                    g["deni8_r"] = trb
                    p.wait("vector", trb)
                    p.wait("vector", g.get("tb_r"))
                    p.wait("vector", txp)
                    p.op("vector", "tensor_mul", TBa,
                         sraw[:, mq * NQ:(mq + 1) * NQ], brn[0:128, 0:NQ])
                    t2_ = p.op("vector", "tensor_scalar", TBr, TBa,
                               scol, None, ALU.mult, inc="ve")
                    ctx.banks.release(irn, t2_)
                    ir, br = ctx.banks.acquire(p)
                    p.wait("tensor", t2_)
                    trot = p.op("tensor", "matmul", br[0:128, 0:NQ], permrh,
                                TBr, start=True, stop=True, inc="mm")
                    p.op("vector", "tensor_mul", TBa, TBr, ctab)
                    p.wait("vector", trot)
                    tb_ = p.op("vector", "tensor_mul", TBb,
                               br[0:128, 0:NQ], stab, inc="ve")
                    ctx.banks.release(ir, tb_)
                    g["tb_r"] = tb_
                    p.wait("vector", g.get("qb2_r"))
                    tfin = p.op("vector", "tensor_add",
                                sdst[:, mq * S + nq0:mq * S + nq0 + NQ],
                                TBa, TBb, inc="ve")
                    g[f"raw_r_{which}_{mq}"] = [tfin]
                g["xpt_r"] = tfin
            qkv_last = tfin
        g["hb_r"] = ("mm", p.counts["mm"])

        p.wait_all("tensor", [qkv_last, g["vt2_last"], tbc])
        at_guard = None
        for w in range(NG):
            nq0 = w * NQ
            at_toks = []
            for h in range(8):
                ch = h // 2
                base = (h % 2) * 64
                tp = (0, 0) if base == 0 else (64, 0)
                io_, obank = ctx.banks.acquire(p)
                m_order = [4, 5, 6, 7] + ([0, 1, 2, 3] if w > 0 else [])
                nm = len(m_order)
                for mi, m in enumerate(m_order):
                    jglob = 4 * (w - 1) + m
                    if m >= 4:
                        off = 128 * (m - 4)
                        wd = NQ - off
                        moff = off
                        msk = maskc
                    else:
                        off = 0
                        wd = 128 * (m + 1)
                        moff = wd - 128
                        msk = maskw
                    isc, bsc = ctx.banks.acquire(p)
                    p.op("tensor", "matmul", bsc[0:128, off:off + wd],
                         KB2[base:base + 64, ch * S + jglob * 128:
                             ch * S + jglob * 128 + 128],
                         QB2[base:base + 64, ch * S + nq0 + off:
                             ch * S + nq0 + off + wd],
                         start=True, stop=False, tile_position=tp)
                    tms = p.op("tensor", "matmul",
                               bsc[0:128, moff:moff + 128], identb, msk,
                               start=False, stop=True, inc="mm")
                    sl = g["ptrr"] = (g.get("ptrr") or 0) + 1
                    sl = sl % 4
                    p.wait("scalar", tms)
                    p.wait("scalar", g["slot"][sl])
                    p.wait("scalar", g["xb_mm"] if sl < 2 else g["xsq_mm"])
                    tex = p.op("scalar", "activation", PT[sl][:, off:off + wd],
                               bsc[0:128, off:off + wd], AF.Exp,
                               bias=BCOL[0:128, 0:1], inc="sc")
                    ctx.banks.release(isc, tex)
                    p.wait("tensor", tex)
                    tvv = p.op("tensor", "matmul", obank[0:65, off:off + wd],
                               VT2[:, jglob * 520 + h * 65:
                                   jglob * 520 + h * 65 + 65],
                               PT[sl][:, off:off + wd],
                               start=(mi == 0), stop=(mi == nm - 1), inc="mm")
                    g["slot"][sl] = tvv
                p.wait("scalar", tvv)
                p.wait_all("scalar", g["den_w"])
                td1 = p.op("scalar", "activation", DEN[0:1, 0:NQ],
                           obank[64:65, 0:NQ], AF.Copy, bias=1e-30, inc="sc")
                p.wait("vector", td1)
                p.wait("vector", g.get("deni_r"))
                tri = p.op("vector", "reciprocal", DENI[0:1, 0:NQ],
                           DEN[0:1, 0:NQ], inc="ve")
                ir2, brb = ctx.banks.acquire(p)
                p.wait("tensor", tri)
                trb = p.op("tensor", "matmul", brb[0:128, 0:NQ],
                           ctx.cfv("ones_row", w=128, rn=1), DENI[0:1, 0:NQ],
                           start=True, stop=True, inc="mm")
                g["deni_r"] = trb
                p.wait("scalar", trb)
                trbe = p.op("scalar", "activation", DEN, brb[0:128, 0:NQ],
                            AF.Copy, inc="sc")
                ctx.banks.release(ir2, trbe)
                p.wait("vector", trbe)
                p.wait("vector", g.get("at2_r"))
                ta1 = p.op("vector", "tensor_mul",
                           AT2[base:base + 64, ch * NQ:(ch + 1) * NQ],
                           obank[0:64, 0:NQ], DEN[base:base + 64, 0:NQ],
                           inc="ve")
                ctx.banks.release(io_, [td1, ta1])
                g["den_w"] = [ta1, tri]
                at_toks.append(ta1)
            p.wait_all("sync", g.get("xw_w", []))
            txw = p.op("sync", "dma_start", inc="ld", incn=16,
                       out=XW.rearrange("p (c w) -> p c w", w=NQ),
                       in_=ell[:, :, nq0:nq0 + NQ].rearrange("c p w -> p c w"))
            p.wait_all("tensor", at_toks)
            wo_toks = []
            for m in range(NCH):
                slot = m % 2
                two = wload_cols(WOS2[slot], "l1f_wo", li, m * 128, 128, 4,
                                 wait=g.get(f"wos_{slot}"))
                ib, bk = ctx.banks.acquire(p)
                p.wait("tensor", two)
                tok = None
                for kc in range(4):
                    tok = p.op("tensor", "matmul", bk[0:128, 0:NQ],
                               WOS2[slot][:, kc * 128:kc * 128 + 128],
                               AT2[:, kc * NQ:(kc + 1) * NQ],
                               start=(kc == 0), stop=(kc == 3),
                               inc="mm" if kc == 3 else None)
                g[f"wos_{slot}"] = tok
                p.wait("vector", tok)
                p.wait("vector", txw)
                txx = p.op("vector", "tensor_add",
                           XW[:, m * NQ:(m + 1) * NQ],
                           XW[:, m * NQ:(m + 1) * NQ], bk[0:128, 0:NQ],
                           inc="ve")
                wo_toks.append(txx)
                ctx.banks.release(ib, txx)
            p.wait_all("sync", wo_toks)
            tso = p.op("sync", "dma_start", inc="st", incn=16,
                       out=ell[:, :, nq0:nq0 + NQ].rearrange("c p w -> p c w"),
                       in_=XW.rearrange("p (c w) -> p c w", w=NQ))
            g["xw_w"] = [tso] + wo_toks
            g["at2_r"] = ("mm", p.counts["mm"])
        g["qb2_r"] = ("mm", p.counts["mm"])
        g["vt2_r"] = ("mm", p.counts["mm"])
        g["hid2_r"] = g["qb2_r"]
        g["hb2_r"] = g["qb2_r"]

        hcar_z2 = p.op("vector", "memset", HCAR[:, 4:8], 0.0, inc="ve")
        final = li == 1
        for n in range(NG):
            nq0 = n * NQ
            p.wait_all("sync", g.get("xw_w", []))
            txw = p.op("sync", "dma_start", inc="ld", incn=16,
                       out=XW.rearrange("p (c w) -> p c w", w=NQ),
                       in_=ell[:, :, nq0:nq0 + NQ].rearrange("c p w -> p c w"))
            yn_toks = []
            for hh in range(HPN):
                c0 = nq0 + hh * NH
                p.wait("vector", txw)
                p.wait("vector", hcar_z2)
                for cc in range(NCH):
                    p.wait_all("vector", g["ao_w"][cc])
                sh_toks = []
                tcp = None
                for cc in range(NCH):
                    xs_ = XW[:, cc * NQ + hh * NH:cc * NQ + hh * NH + NH]
                    dst = AO[:, cc * NH:(cc + 1) * NH]
                    if cc < 4:
                        tcp = p.op("vector", "tensor_copy", dst, xs_, inc="ve")
                    else:
                        hcv = HCAR[:, cc:cc + 1]
                        tcp = p.op("vector", "tensor_copy", dst[:, 0:1], hcv,
                                   inc="ve")
                        tcp = p.op("vector", "tensor_copy", dst[:, 1:NH],
                                   xs_[:, 0:NH - 1], inc="ve")
                        p.wait("vector", tcp)
                        tcp = p.op("vector", "tensor_copy", hcv,
                                   xs_[:, NH - 1:NH], inc="ve")
                    sh_toks.append(tcp)
                rows = {"mu": R_mu, "var": R_var, "tmp": R_tmp, "iv": R_iv,
                        "c": R_ctot, "nmc": R_nmct}
                done, tcast, tsq = stats_and_coeffs(AO, NH, rows, None,
                                                    extra_cast_dep=sh_toks)
                ib1, bc1 = ctx.banks.acquire(p)
                ib2, bc2 = ctx.banks.acquire(p)
                tb1 = bcast_row(ctx, bc1, R_ctot, NH, dep=done)
                tb2 = bcast_row(ctx, bc2, R_nmct, NH, dep=done)
                p.wait_all("vector", [tb1, tb2, g.get("hb2_r")])
                tyn = None
                for cc in range(NCH):
                    ac = AO[:, cc * NH:(cc + 1) * NH]
                    s1 = SC1[:, cc * NH:(cc + 1) * NH]
                    p.wait_all("vector", g["sc1_w"])
                    p.op("vector", "tensor_mul", s1, ac, bc1[0:128, 0:NH])
                    p.op("vector", "tensor_add", s1, s1, bc2[0:128, 0:NH])
                    tyn = p.op("vector", "tensor_scalar",
                               HB[:, cc * NQ + hh * NH:cc * NQ + hh * NH + NH],
                               s1, lg[:, cc:cc + 1], lb[:, cc:cc + 1],
                               ALU.mult, ALU.add, inc="ve")
                yn_toks.append(tyn)
                g["ao_w"] = [[tyn] for _ in range(NCH)]
                g["sc1_w"] = [tyn]
                ctx.banks.release(ib1, tyn)
                ctx.banks.release(ib2, tyn)
            p.wait_all("tensor", yn_toks)
            hid_toks = []
            p.wait("vector", g.get("hid2_r"))
            for ma in range(22):
                slot = ma % 2
                twa = wload_cols(W1S[slot], "l1f_w1p", li, ma * 128, 128, 8,
                                 wait=g.get(f"w1s_{slot}"))
                ib1_, bka = ctx.banks.acquire(p)
                p.wait("tensor", twa)
                toka = None
                for kc in range(NCH):
                    toka = p.op("tensor", "matmul", bka[0:128, 0:NQ],
                                W1S[slot][:, kc * 128:kc * 128 + 128],
                                HB[:, kc * NQ:(kc + 1) * NQ],
                                start=(kc == 0), stop=(kc == 7),
                                inc="mm" if kc == 7 else None)
                twg = wload_cols(W1S[slot], "l1f_w1p", li, (22 + ma) * 128,
                                 128, 8, wait=toka)
                ib2_, bkg = ctx.banks.acquire(p)
                p.wait("tensor", twg)
                tokg = None
                for kc in range(NCH):
                    tokg = p.op("tensor", "matmul", bkg[0:128, 0:NQ],
                                W1S[slot][:, kc * 128:kc * 128 + 128],
                                HB[:, kc * NQ:(kc + 1) * NQ],
                                start=(kc == 0), stop=(kc == 7),
                                inc="mm" if kc == 7 else None)
                g[f"w1s_{slot}"] = tokg
                p.wait("vector", toka)
                p.wait("vector", g.get("gel_r"))
                tab = p.op("vector", "tensor_scalar", ABF, bka[0:128, 0:NQ],
                           b1c[:, ma:ma + 1], None, ALU.add, inc="ve")
                ctx.banks.release(ib1_, tab)
                p.wait("scalar", tokg)
                p.wait("scalar", g.get("gel_r"))
                tge = p.op("scalar", "activation", GEL, bkg[0:128, 0:NQ],
                           AF.Gelu, bias=b1c[:, 22 + ma:23 + ma], inc="sc")
                ctx.banks.release(ib2_, tge)
                p.wait("vector", tab)
                p.wait("vector", tge)
                th = p.op("vector", "tensor_mul",
                          HID2[:, ma * NQ:(ma + 1) * NQ], ABF, GEL, inc="ve")
                g["gel_r"] = th
                hid_toks.append(th)
            p.wait_all("tensor", hid_toks)
            w2_toks = []
            for m in range(NCH):
                slot = m % 2
                tw2 = wload_cols(W2S[slot], "l1f_w2p", li, m * 128, 128, 22,
                                 wait=g.get(f"w2s_{slot}"))
                ib, bk = ctx.banks.acquire(p)
                p.wait("tensor", tw2)
                tok = None
                for kc in range(22):
                    tok = p.op("tensor", "matmul", bk[0:128, 0:NQ],
                               W2S[slot][:, kc * 128:kc * 128 + 128],
                               HID2[:, kc * NQ:(kc + 1) * NQ],
                               start=(kc == 0), stop=(kc == 21),
                               inc="mm" if kc == 21 else None)
                g[f"w2s_{slot}"] = tok
                p.wait("vector", tok)
                txx = p.op("vector", "scalar_tensor_tensor",
                           XW[:, m * NQ:(m + 1) * NQ], bk[0:128, 0:NQ],
                           b2c[:, m:m + 1], XW[:, m * NQ:(m + 1) * NQ],
                           ALU.add, ALU.add, inc="ve")
                w2_toks.append(txx)
                ctx.banks.release(ib, txx)
            g["hid2_r"] = ("mm", p.counts["mm"])
            g["hb2_r"] = ("mm", p.counts["mm"])
            if not final:
                p.wait_all("sync", w2_toks)
                tso = p.op("sync", "dma_start", inc="st", incn=16,
                           out=ell[:, :, nq0:nq0 + NQ].rearrange(
                               "c p w -> p c w"),
                           in_=XW.rearrange("p (c w) -> p c w", w=NQ))
                g["xw_w"] = [tso] + w2_toks
            else:
                for hh in range(HPN):
                    c0 = nq0 + hh * NH
                    p.wait_all("sync", g["xf_w"])
                    tx2 = halfload(x_in, c0, NH, XF)
                    rows = {"mu": R_mu, "var": R_var, "tmp": R_tmp,
                            "iv": R_iv, "c": R_ctot, "nmc": R_nmct}
                    xwf = XW.rearrange("p (c x) -> p c x", x=NQ)[
                        :, :, hh * NH:hh * NH + NH]
                    done, tcast, tsq = stats_and_coeffs(
                        xwf, NH, rows, None, extra_cast_dep=w2_toks)
                    ib1, bc1 = ctx.banks.acquire(p)
                    ib2, bc2 = ctx.banks.acquire(p)
                    tb1 = bcast_row(ctx, bc1, R_ctot, NH, dep=done)
                    tb2 = bcast_row(ctx, bc2, R_nmct, NH, dep=done)
                    p.wait_all("vector", [tb1, tb2, tx2] + g["sc1_w"])
                    xn_toks = []
                    exit_tt = None
                    for cc in range(NCH):
                        xwc = XW[:, cc * NQ + hh * NH:cc * NQ + hh * NH + NH]
                        s1 = SC1[:, cc * NH:(cc + 1) * NH]
                        xc = XF[:, cc * NH:(cc + 1) * NH]
                        p.op("vector", "tensor_mul", s1, xwc,
                             bc1[0:128, 0:NH])
                        exit_tt = p.op("vector", "tensor_add", s1, s1,
                                       bc2[0:128, 0:NH], inc="ve")
                        p.wait("gpsimd", exit_tt)
                        tg_ = p.op("gpsimd", "tensor_add", s1, s1, xc,
                                   inc="gp")
                        xn_toks.append(tg_)
                    ctx.banks.release(ib1, exit_tt)
                    ctx.banks.release(ib2, exit_tt)
                    p.wait_all("sync", xn_toks)
                    tst = p.op("sync", "dma_start", inc="st", incn=16,
                               out=x_out[:, :, c0:c0 + NH].rearrange(
                                   "c p w -> p c w"),
                               in_=SC1.rearrange("p (c w) -> p c w", w=NH))
                    g["xf_w"] = [xn_toks[-1], exit_tt, tcast, tsq]
                    g["sc1_w"] = [tst] + xn_toks
                g["xw_w"] = w2_toks + [("ve", p.counts["ve"])]

        g["qb2_r"] = ("mm", p.counts["mm"])

    ba.pop(); fa.pop()
    return tst


def build_consts_l1f(c, inputs):
    P = np.zeros((128, 128), np.float32)
    for b in (0, 64):
        for dd in range(32):
            P[b + dd + 32, b + dd] = -1.0
            P[b + dd, b + dd + 32] = 1.0
    c.add_bf("perm_rothalf", P)
    mc = np.where(np.arange(128)[None, :] < np.arange(128)[:, None],
                  -1e9, 0.0).astype(np.float32)
    mw = np.where(np.arange(128)[None, :] > np.arange(128)[:, None],
                  -1e9, 0.0).astype(np.float32)
    c.add_bf("l1f_maskc", mc)
    c.add_bf("l1f_maskw", mw)
    for mq in range(4):
        s = np.zeros((128, 128), np.float32)
        s[2 * mq, 0:64] = 1.0
        s[2 * mq + 1, 64:128] = 1.0
        c.add_f32(f"sel8_{mq}", s)
    for li in range(2):
        qs = np.asarray(inputs["L1_ffn_q_scale"][li], np.float32)
        ks = np.asarray(inputs["L1_ffn_k_scale"][li], np.float32)
        c.add_f32(f"l1f_qs{li}", np.tile(qs, 2)[:, None] * 8.0)
        c.add_f32(f"l1f_ks{li}", np.tile(ks, 2)[:, None])
        c.add_f32(f"l1f_lg{li}", col_chunks(inputs["L1_ffn_ln_g"][li]))
        c.add_f32(f"l1f_lb{li}", col_chunks(inputs["L1_ffn_ln_b"][li]))
        b1 = np.asarray(inputs["L1_ffn_b1"][li], np.float32)
        b1p = np.zeros(44 * 128, np.float32)
        b1p[0:2730] = b1[0:2730]
        b1p[22 * 128:22 * 128 + 2730] = b1[2730:5460]
        c.add_f32(f"l1f_b1_{li}", col_chunks(b1p))
        c.add_f32(f"l1f_b2_{li}", col_chunks(inputs["L1_ffn_b2"][li]))


def xpos_tables_host():
    dim_head, window, n = 64, 512, S
    half = np.arange(0, dim_head, 2, dtype=np.float32)
    inv_freq = 1.0 / 10000.0 ** (half / dim_head)
    t = np.arange(n, dtype=np.float32)
    fr = inv_freq[:, None] * t[None, :]
    freqs = np.concatenate([fr, fr], axis=0)
    base = (half + 0.4 * dim_head) / (1.4 * dim_head)
    power = (t - n // 2) / float(window // 2)
    scale = base[:, None] ** power[None, :]
    scale = np.concatenate([scale, scale], axis=0)
    cos, sin = np.cos(freqs), np.sin(freqs)
    Cq = cos * scale
    Sq = sin * scale
    Ck = cos / scale
    Sk = sin / scale
    out = np.concatenate([np.tile(T, (2, 1)) for T in (Cq, Sq, Ck, Sk)],
                         axis=0)
    return np.ascontiguousarray(out)


def host_weights_l1f(inputs):
    A = lambda a: np.ascontiguousarray(np.asarray(a, np.float32))
    w1 = np.asarray(inputs["L1_ffn_w1"], np.float32)
    w1p = np.zeros((2, 1024, 44 * 128), np.float32)
    w1p[:, :, 0:2730] = w1[:, :, 0:2730]
    w1p[:, :, 22 * 128:22 * 128 + 2730] = w1[:, :, 2730:5460]
    w2 = np.asarray(inputs["L1_ffn_w2"], np.float32)
    w2p = np.zeros((2, 22 * 128, 1024), np.float32)
    w2p[:, 0:2730, :] = w2
    return {
        "l1f_wqkv": A(inputs["L1_ffn_wqkv"]),
        "l1f_wo": A(inputs["L1_ffn_wo"]),
        "l1f_w1p": np.ascontiguousarray(w1p),
        "l1f_w2p": np.ascontiguousarray(w2p),
        "l1f_xpos": xpos_tables_host(),
    }


L1F_WSHAPES = {"l1f_wqkv": [2, 1024, 1536], "l1f_wo": [2, 512, 1024],
               "l1f_w1p": [2, 1024, 5632], "l1f_w2p": [2, 2816, 1024],
               "l1f_xpos": [512, 2048]}



def host_weights_l0a(inputs):
    T = lambda a: np.ascontiguousarray(np.asarray(a, np.float32).T)
    return {
        "wdqT": T(inputs["L0_attn_w_down_q"]),
        "wdkvT": T(inputs["L0_attn_w_down_kv"]),
        "wupqT": T(inputs["L0_attn_w_up_q"]),
        "wupkT": T(inputs["L0_attn_w_up_k"]),
        "wupvT": T(inputs["L0_attn_w_up_v"]),
        "woutT": T(inputs["L0_attn_w_out"]),
    }


def declare_weights(nc, wdict):
    out = {}
    for k, v in wdict.items():
        out[k] = nc.dram_tensor(k, list(v.shape), F32, kind="ExternalInput").ap()
    return out


BLOCK_WSHAPES = {
    "l0a": {"wdqT": [1024, 512], "wdkvT": [1024, 512],
            "wupqT": [512, 1024], "wupkT": [512, 1024],
            "wupvT": [512, 1024], "woutT": [1024, 1024]},
    "l0f": L0F_WSHAPES,
}

BLOCK_EMIT = {}


def _register_blocks():
    BLOCK_EMIT["l0a"] = emit_l0a
    BLOCK_EMIT["l0f"] = emit_l0f
    BLOCK_EMIT["l1a"] = emit_l1a
    BLOCK_WSHAPES["l1a"] = L1A_WSHAPES
    BLOCK_EMIT["l1f"] = emit_l1f
    BLOCK_WSHAPES["l1f"] = L1F_WSHAPES


def build_chain_program(consts, blocks, nrows=NROWS, dbg_shapes=None):
    _register_blocks()
    nc = bass.Bass("TRN2", target_bir_lowering=False)
    ctx = Ctx(nc, consts)
    if dbg_shapes:
        ctx.dbgmap = {k: nc.dram_tensor("dbg_" + k, v[0], v[1],
                                        kind="ExternalOutput").ap()
                      for k, v in dbg_shapes.items()}
    shape = [nrows, NCH, 128, S]
    x_in = nc.dram_tensor("x_in", shape, F32, kind="ExternalInput").ap()
    x_out = nc.dram_tensor("x_out", shape, F32, kind="ExternalOutput").ap()
    ios = [x_in]
    for i in range(len(blocks) - 1):
        ios.append(nc.dram_tensor(f"x_mid{i}", shape, F32, kind="Internal").ap())
    ios.append(x_out)
    ctx.ell = nc.dram_tensor("ell_scr", [NCH, 128, S], F32, kind="Internal").ap()
    W = {}
    ctx.wtok = {}
    worder = []
    for b in blocks:
        for k, v in BLOCK_WSHAPES[b].items():
            if k not in W:
                wf = nc.dram_tensor(k, v, F32, kind="ExternalInput").ap()
                wb = nc.dram_tensor(k + "__bf", v, BF16, kind="Internal").ap()
                W[k] = wb
                worder.append((k, wf, wb))
    ctx.load_consts()
    for k, wf, wb in worder:
        ctx.wtok[k] = ctx.p.op("gpsimd", "dma_start", inc="gd", incn=16,
                               out=wb, in_=wf)
    for r in range(nrows):
        for bi, b in enumerate(blocks):
            BLOCK_EMIT[b](ctx, ios[bi][r], ios[bi + 1][r], W)
            ctx.barrier()
    ctx.p.wait("sync", ("st", ctx.p.counts["st"]))
    ctx.p.build()
    return nc


def build_l0a_program(consts, nrows=NROWS):
    return build_chain_program(consts, ["l0a"], nrows)


def to_fm(x_rows):
    r = x_rows.shape[0]
    return np.ascontiguousarray(
        x_rows.transpose(0, 2, 1).reshape(r, NCH, 128, S))


def from_fm(x_fm):
    r = x_fm.shape[0]
    return np.ascontiguousarray(
        x_fm.reshape(r, D, S).transpose(0, 2, 1))


ALL_BLOCKS = ["l0a", "l0f", "l1a", "l1f"]


def kernel(**inputs):
    consts = Consts()
    build_consts_common(consts)
    build_consts_l0a(consts, inputs)
    build_consts_l0f(consts)
    build_consts_l1a(consts, inputs)
    build_consts_l1f(consts, inputs)
    cf, cb = consts.finish()
    wmap = {}
    wmap.update(host_weights_l0a(inputs))
    wmap.update(host_weights_l0f(inputs))
    wmap.update(host_weights_l1a(inputs))
    wmap.update(host_weights_l1f(inputs))

    x = np.asarray(inputs["L0_attn_x"], np.float32)
    x_fm = to_fm(x)

    nc = build_chain_program(consts, ALL_BLOCKS, nrows=NROWS)
    in_maps = []
    for c in range(NCORES):
        m = dict(wmap)
        m["c_f32"] = cf
        m["c_bf"] = cb
        m["x_in"] = np.ascontiguousarray(x_fm[c * NROWS:(c + 1) * NROWS])
        in_maps.append(m)
    res = run_bass_kernel_spmd(nc, in_maps, core_ids=list(range(NCORES)))
    out = np.concatenate([from_fm(r["x_out"]) for r in res.results], axis=0)
    return np.ascontiguousarray(out.astype(np.float32))


# version 02
# speedup vs baseline: 1.3337x; 1.3337x over previous
import numpy as np
import ml_dtypes
import concourse.bass as bass
import concourse.mybir as mybir
from concourse.bass_utils import run_bass_kernel_spmd

F32 = mybir.dt.float32
BF16 = mybir.dt.bfloat16
AF = mybir.ActivationFunctionType
ALU = mybir.AluOpType
AX = mybir.AxisListType
BF = ml_dtypes.bfloat16

NCORES = 8
NROWS = 4
S = 2048
D = 1024
NCH = 8
NG = 4
NQ = 512
EPS1 = 1e-5


ENGINES = ("tensor", "vector", "scalar", "gpsimd", "sync")


class Plan:
    def __init__(self, nc):
        self.nc = nc
        self.ops = {e: [] for e in ENGINES}
        self.sems = {}
        self.counts = {}
        self.waited = {e: {} for e in ENGINES}
        self.label = ""
        self.labels = {e: [] for e in ENGINES}

    def _sem(self, name):
        if name not in self.sems:
            self.sems[name] = self.nc.alloc_semaphore(name)
            self.counts[name] = 0
        return self.sems[name]

    def op(self, eng, meth, *args, inc=None, incn=1, **kwargs):
        tok = None
        if inc is not None:
            self._sem(inc)
            self.counts[inc] += incn
            tok = (inc, self.counts[inc])
        self.ops[eng].append((meth, args, kwargs, inc, incn))
        self.labels[eng].append(self.label)
        return tok

    def wait(self, eng, tok):
        if tok is None:
            return
        sname, v = tok
        self._sem(sname)
        if self.waited[eng].get(sname, 0) >= v:
            return
        self.waited[eng][sname] = v
        self.ops[eng].append(("wait_ge", (sname, v), {}, None, 0))

    def wait_all(self, eng, toks):
        for t in toks:
            self.wait(eng, t)

    def build(self):
        plan = self

        def runner(engname):
            def run(e):
                for meth, args, kwargs, inc, incn in plan.ops[engname]:
                    if meth == "wait_ge":
                        e.wait_ge(plan.sems[args[0]], args[1])
                        continue
                    inst = getattr(e, meth)(*args, **kwargs)
                    if inc is not None:
                        inst.then_inc(plan.sems[inc], incn)
            return run

        with plan.nc.Block() as block:
            block.tensor(runner("tensor"))
            block.vector(runner("vector"))
            block.scalar(runner("scalar"))
            block.gpsimd(runner("gpsimd"))
            block.sync(runner("sync"))


class Banks:
    def __init__(self, nc, n=8):
        self.banks = [nc.alloc_psum_tensor(f"bank{i}", [128, 512], F32)
                      for i in range(n)]
        self.free_tok = [None] * n
        self.order = list(range(n))

    def acquire(self, p):
        i = self.order.pop(0)
        ft = self.free_tok[i]
        if ft is not None:
            for t in (ft if isinstance(ft, list) else [ft]):
                p.wait("tensor", t)
        return i, self.banks[i]

    def release(self, i, consumer_tok):
        self.free_tok[i] = consumer_tok
        self.order.append(i)


class Arena:
    def __init__(self, nc, name, cols, dtype):
        self.t = nc.alloc_sbuf_tensor(name, [128, cols], dtype)
        self.name = name
        self.cols = cols
        self.off = 0
        self.marks = []

    def alloc(self, w):
        assert self.off + w <= self.cols, \
            f"Arena {self.name} overflow: {self.off}+{w} > {self.cols}"
        v = self.t[:, self.off:self.off + w]
        self.off += w
        return v

    def push(self):
        self.marks.append(self.off)

    def pop(self):
        self.off = self.marks.pop()



def rope_tables_l0(s, d_rope):
    inv_freq = 1.0 / 10000.0 ** (np.arange(0, d_rope, 2, dtype=np.float32) / d_rope)
    half = d_rope // 2
    idx = [(2 * i) % half for i in range(half)]
    t = np.arange(s, dtype=np.float32)
    fr = np.outer(inv_freq[idx], t)
    c, sn = np.cos(fr), np.sin(fr)
    C = np.empty((d_rope, s), np.float32)
    Sg = np.empty((d_rope, s), np.float32)
    C[0::2] = c; C[1::2] = c
    Sg[0::2] = -sn; Sg[1::2] = sn
    return C, Sg


def col_chunks(vec, pad_to=None):
    vec = np.asarray(vec, np.float32).ravel()
    n = vec.shape[0]
    if pad_to is not None and n < pad_to:
        vec = np.concatenate([vec, np.zeros(pad_to - n, np.float32)])
        n = pad_to
    assert n % 128 == 0
    return np.ascontiguousarray(vec.reshape(n // 128, 128).T)


class Consts:
    def __init__(self):
        self.f32, self.bfl = [], []
        self.f32_off, self.bf_off = {}, {}
        self.nf, self.nb = 0, 0

    def add_f32(self, name, arr):
        arr = np.asarray(arr, np.float32)
        assert arr.shape[0] == 128, name
        self.f32_off[name] = self.nf
        self.nf += arr.shape[1]
        self.f32.append(arr)

    def add_bf(self, name, arr):
        arr = np.asarray(arr, np.float32)
        assert arr.shape[0] == 128, name
        self.bf_off[name] = self.nb
        self.nb += arr.shape[1]
        self.bfl.append(arr.astype(BF))

    def finish(self):
        cf = np.ascontiguousarray(np.concatenate(self.f32, axis=1))
        cb = np.ascontiguousarray(
            np.concatenate(self.bfl, axis=1).astype(np.float32))
        return cf, cb


def build_consts_common(c):
    eps = np.array([1e-5, 1e-5 + 1e-5 * 1e-5, 1e-5 + 1e-5 * 1e-6,
                    1e-24, 1e-38], np.float32)
    c.add_f32("epsc", np.tile(eps[None, :], (128, 1)))
    c.add_bf("ones_col", np.ones((128, 1), np.float32))
    oner = np.zeros((128, 128), np.float32)
    oner[0, :] = 1.0
    oner[32, :] = 1.0
    oner[64, :] = 1.0
    c.add_f32("ones_row", oner)
    c.add_f32("ident_f32", np.eye(128, dtype=np.float32))
    c.add_bf("ident_bf", np.eye(128, dtype=np.float32))
    seld = np.zeros((128, 128), np.float32)
    seld[0, 0:64] = 1.0
    seld[64, 64:128] = 1.0
    c.add_f32("sel_den", seld)


def build_consts_l0a(c, inputs):
    c.add_f32("l0a_gamma", col_chunks(inputs["L0_attn_ln_scale"]))
    c.add_f32("l0a_beta", col_chunks(inputs["L0_attn_ln_bias"]))
    C0, S0 = rope_tables_l0(S, 64)
    c.add_bf("l0_ropeC", np.concatenate([C0, C0], axis=0))
    c.add_bf("l0_ropeS", np.concatenate([S0, S0], axis=0))
    P = np.zeros((128, 128), np.float32)
    for i in range(64):
        P[2 * i + 1, 2 * i] = 1.0
        P[2 * i, 2 * i + 1] = 1.0
    c.add_bf("perm_pairswap", P)
    for m in range(8):
        sel = np.zeros((128, 16), np.float32)
        sel[0:64, 2 * m] = 1.0
        sel[64:128, 2 * m + 1] = 1.0
        c.add_bf(f"sel_qk_{m}", sel)



class Ctx:
    def __init__(self, nc, consts, fa_cols=14020, ba_cols=69888):
        self.nc = nc
        self.p = Plan(nc)
        self.banks = Banks(nc)
        self.consts = consts
        self.cf_dram = nc.dram_tensor("c_f32", [128, consts.nf], F32,
                                      kind="ExternalInput").ap()
        self.cb_dram = nc.dram_tensor("c_bf", [128, consts.nb], F32,
                                      kind="ExternalInput").ap()
        self.cf = nc.alloc_sbuf_tensor("cf", [128, consts.nf], F32)
        self.cb = nc.alloc_sbuf_tensor("cb", [128, consts.nb], BF16)
        self.fa = Arena(nc, "fa", fa_cols, F32)
        self.ba = Arena(nc, "ba", ba_cols, BF16)
        self.dum = self.fa.alloc(4)
        fa, ba = self.fa, self.ba
        cm = {}
        cm["XF"] = fa.alloc(NCH * NH)
        cm["AO"] = fa.alloc(NCH * NH)
        cm["SC1"] = fa.alloc(NCH * NH)
        cm["TIN"] = fa.alloc(128)
        cm["DEN"] = fa.alloc(NQ)
        cm["DENI"] = fa.alloc(NQ)
        cm["BROW"] = fa.alloc(16)
        cm["BCOL"] = fa.alloc(16)
        cm["SMV"] = fa.alloc(16)
        cm["SMB"] = fa.alloc(2 * S + 8 * NH)
        pt_off = ba.off
        cm["PT"] = [ba.alloc(NQ) for _ in range(4)]
        cm["XB"] = ba.t[:, pt_off:pt_off + NCH * NH]
        cm["XSQ"] = ba.t[:, pt_off + NCH * NH:pt_off + 2 * NCH * NH]
        cm["TBa"] = ba.alloc(NQ)
        cm["TBb"] = ba.alloc(NQ)
        cm["TBr"] = ba.alloc(NQ)
        cm["SQT"] = ba.alloc(NQ)
        self.cm = cm
        SMB = cm["SMB"]
        r = {}
        r["CROW"] = SMB[0:1, 0:S]
        r["NMCROW"] = SMB[0:1, S:2 * S]
        _r0 = 2 * S
        for i, nm in enumerate(("mu", "var", "tmp", "iv", "ctot", "nmct",
                                "c2", "nmc2")):
            r[nm] = SMB[0:1, _r0 + i * NH:_r0 + (i + 1) * NH]
        self.rows = r
        self.g = {
            "rows_r": None,
            "xf_w": [], "xb_mm": None, "xsq_mm": None, "sc1_w": [],
            "tbr_rot": None, "tbr_ta": None,
            "den_w": [], "deni_r": None, "tin_r": None,
            "qn_r": None, "att_r": None,
            "ao_w": [[] for _ in range(NCH)],
            "slot": [None] * 4, "hf_r": None,
        }

    def load_consts(self):
        p = self.p
        t1 = p.op("sync", "dma_start", inc="ld", incn=16,
                  out=self.cf[:], in_=self.cf_dram)
        t2 = p.op("gpsimd", "dma_start", inc="gd", incn=16,
                  out=self.cb[:], in_=self.cb_dram)
        for e in ENGINES:
            p.wait(e, t1)
            p.wait(e, t2)

    def cfv(self, name, w=1, r0=0, rn=128):
        off = self.consts.f32_off[name]
        return self.cf[r0:r0 + rn, off:off + w]

    def cbv(self, name, w=1, r0=0, rn=128):
        off = self.consts.bf_off[name]
        return self.cb[r0:r0 + rn, off:off + w]

    dbgmap = None

    def dump(self, name, view):
        if not self.dbgmap or name not in self.dbgmap:
            return
        self.barrier()
        t = self.p.op("sync", "dma_start", inc="st", incn=16,
                      out=self.dbgmap[name], in_=view)
        self.p.wait("sync", t)
        self.barrier()

    def barrier(self):
        p = self.p
        toks = [
            p.op("vector", "memset", self.dum[0:1, 0:1], 0.0, inc="ve"),
            p.op("scalar", "memzero", self.dum[0:1, 1:2], inc="sc"),
            p.op("gpsimd", "memset", self.dum[0:1, 2:3], 0.0, inc="gp"),
        ]
        for sname in ("mm", "ld", "st", "gd"):
            if self.p.counts.get(sname, 0) > 0:
                toks.append((sname, self.p.counts[sname]))
        for e in ENGINES:
            p.wait_all(e, toks)


def colsum_mms(ctx, bank, src, n_w, dep=(), rows=1, lhsT=None, nch=NCH):
    p = ctx.p
    if lhsT is None:
        lhsT = ctx.cbv("ones_col")
    p.wait_all("tensor", dep)
    tok = None
    for cc in range(nch):
        last = cc == nch - 1
        tok = p.op("tensor", "matmul", bank[0:rows, 0:n_w], lhsT,
                   src[:, cc * n_w:(cc + 1) * n_w],
                   start=(cc == 0), stop=last, inc="mm" if last else None)
    return tok


def bcast_row(ctx, bank, row_view, n_w, m=128, dep=()):
    p = ctx.p
    p.wait_all("tensor", dep)
    ctx.g["rows_r"] = True
    base = row_view.tensor_base_partition() if hasattr(row_view, "tensor_base_partition") else None
    base = getattr(row_view, "base_partition", None)
    if callable(base):
        base = base()
    if base is None:
        base = 0
    ones = ctx.cfv("ones_row", w=m, rn=1, r0=base)
    tok = p.op("tensor", "matmul", bank[0:m, 0:n_w], ones, row_view,
               start=True, stop=True, inc="mm")
    ctx.g["rows_r"] = tok
    return tok


def ln_coeffs(ctx, b_sx, b_sq, rows, n_w, mm_toks, want_outer=True,
              fused_eps2=None):
    p = ctx.p
    v = "vector"
    rr = ctx.g.get("rows_r")
    if rr is not None and rr is not True:
        p.wait(v, rr)
        p.wait("scalar", rr)
    p.wait_all(v, mm_toks)
    sx = b_sx[0:1, 0:n_w]
    sq = b_sq[0:1, 0:n_w]
    tmu = p.op(v, "tensor_scalar_mul", rows["mu"], sx, 1.0 / D, inc="ve")
    p.wait(v, tmu)
    t1 = p.op(v, "scalar_tensor_tensor", rows["tmp"], rows["mu"], -1.0,
              rows["mu"], ALU.mult, ALU.mult, inc="ve")
    p.wait(v, t1)
    tvar = p.op(v, "scalar_tensor_tensor", rows["var"], sq, 1.0 / D,
                rows["tmp"], ALU.mult, ALU.add, inc="ve")
    done = []
    free = [tmu, tvar]
    p.wait("scalar", tvar)
    epsc = ctx.cfv("epsc", w=5, rn=1)
    if want_outer:
        tl = p.op("scalar", "activation", rows["iv"], rows["var"], AF.Ln,
                  bias=epsc[:, 0:1], inc="sc")
        p.wait("scalar", tl)
        tc = p.op("scalar", "activation", rows["c"], rows["iv"], AF.Exp,
                  scale=-0.5, inc="sc")
        p.wait(v, tc)
        tn = p.op(v, "scalar_tensor_tensor", rows["nmc"], rows["mu"], -1.0,
                  rows["c"], ALU.mult, ALU.mult, inc="ve")
        done += [tc, tn]
    if fused_eps2 is not None:
        e2 = fused_eps2
        ecol = 1 if e2 == 1e-5 else 2
        tl = p.op("scalar", "activation", rows["tmp"], rows["var"], AF.Ln,
                  scale=1.0 + e2, bias=epsc[:, ecol:ecol + 1], inc="sc")
        p.wait("scalar", tl)
        tc = p.op("scalar", "activation", rows["ctot"], rows["tmp"], AF.Exp,
                  scale=-0.5, inc="sc")
        p.wait(v, tc)
        tn = p.op(v, "scalar_tensor_tensor", rows["nmct"], rows["mu"], -1.0,
                  rows["ctot"], ALU.mult, ALU.mult, inc="ve")
        done += [tc, tn]
    return done, free



NH = 128
HPN = NQ // NH


def emit_l0a(ctx, x_in, x_out, W):
    p = ctx.p
    ba, fa = ctx.ba, ctx.fa
    ba.push(); fa.push()

    HF = ba.alloc(NCH * NQ)
    KB = ba.alloc(NCH * S)
    CQ = ba.alloc(4 * S)
    CKV = ba.alloc(4 * S)
    VT = ba.alloc(16 * 1040)
    cm = ctx.cm
    PT = cm["PT"]
    TBa, TBb, TBr, SQT = cm["TBa"], cm["TBb"], cm["TBr"], cm["SQT"]
    XB, XSQ = cm["XB"], cm["XSQ"]
    W12 = ba.alloc(NCH * 1024)
    W45 = ba.alloc(4 * 1024)
    W1 = W12[:, 0:4096]
    W2 = W12[:, 4096:8192]
    W6 = W12
    W4 = W45
    W5 = W45
    W3 = HF
    QN = CKV[:, 0:NCH * NQ]
    ATT = CKV[:, NCH * NQ:2 * NCH * NQ]

    XF, AO, SC1 = cm["XF"], cm["AO"], cm["SC1"]
    TIN, DEN, DENI = cm["TIN"], cm["DEN"], cm["DENI"]
    BROW, BCOL, SMV = cm["BROW"], cm["BCOL"], cm["SMV"]
    rows_ = ctx.rows
    CROW, NMCROW = rows_["CROW"], rows_["NMCROW"]
    R_mu, R_var, R_tmp, R_iv = rows_["mu"], rows_["var"], rows_["tmp"], rows_["iv"]
    R_ctot, R_nmct, R_c2, R_nmc2 = rows_["ctot"], rows_["nmct"], rows_["c2"], rows_["nmc2"]

    QMX = SMV[0:16, 0:1]
    KMX = SMV[0:16, 1:2]
    QSQ = SMV[0:16, 2:3]
    KSQ = SMV[0:16, 3:4]
    NB = SMV[0:16, 4:5]
    KMC = SMV[0:16, 5:9]

    gamma = ctx.cfv("l0a_gamma", w=NCH)
    beta = ctx.cfv("l0a_beta", w=NCH)
    ropeC = ctx.cbv("l0_ropeC", w=S)
    ropeS = ctx.cbv("l0_ropeS", w=S)
    perm = ctx.cbv("perm_pairswap", w=128)
    ident = ctx.cfv("ident_f32", w=128)

    def wload(dst, wname, mcols, wait=None):
        p.wait("sync", ctx.wtok.get(wname))
        p.wait("sync", wait)
        return p.op("sync", "dma_start", inc="ld", incn=16,
                    out=dst.rearrange("p (c m) -> p c m", m=mcols),
                    in_=W[wname].rearrange("(c p) m -> p c m", p=128))

    tw1 = wload(W1, "wdqT", 512)
    tw2 = wload(W2, "wdkvT", 512)
    tw4 = wload(W4, "wupkT", 1024)
    t_vones = p.op("vector", "memset", VT[:, 64:16 * 1040:65], 1.0, inc="ve")
    t_tin0 = p.op("vector", "memset", TIN, 0.0, inc="ve")
    p.op("vector", "memset", DEN, 0.0)
    t_den0 = p.op("vector", "memset", DENI, 0.0, inc="ve")
    p.wait("scalar", t_den0)
    p.wait("tensor", t_den0)

    g = {
        "xf_w": [],
        "xb_mm": None, "xsq_mm": None,
        "sc1_w": [],
        "tbr_rot": None, "tbr_ta": None,
        "den_w": [],
        "deni_r": None,
        "tin_r": None,
        "qn_r": None, "att_r": None,
        "ao_w": [[] for _ in range(NCH)],
        "slot": [None] * 4,
        "hf_r": None,
    }

    def halfload(dram, c0, width, dst):
        return p.op("sync", "dma_start", inc="ld", incn=16,
                    out=dst.rearrange("p (c w) -> p c w", w=width),
                    in_=dram[:, :, c0:c0 + width].rearrange("c p w -> p c w"))

    def stats_and_coeffs(srcf, width, rows, fused_eps2, extra_cast_dep=()):
        p.wait("vector", g["xb_mm"])
        p.wait_all("vector", [g["slot"][0], g["slot"][1]])
        p.wait_all("vector", extra_cast_dep)
        tcast = p.op("vector", "tensor_copy", XB[:, 0:NCH * width], srcf,
                     inc="ve")
        g["slot"][0] = tcast
        g["slot"][1] = tcast
        p.wait_all("scalar", [g["slot"][2], g["slot"][3]])
        p.wait("scalar", g["xsq_mm"])
        p.wait_all("scalar", extra_cast_dep)
        tsq = p.op("scalar", "activation", XSQ[:, 0:NCH * width], srcf,
                   AF.Square, inc="sc")
        g["slot"][2] = tsq
        g["slot"][3] = tsq
        isx, bsx = ctx.banks.acquire(p)
        isq, bsq = ctx.banks.acquire(p)
        t1 = colsum_mms(ctx, bsx, XB[:, 0:NCH * width], width, dep=(tcast,))
        t2 = colsum_mms(ctx, bsq, XSQ[:, 0:NCH * width], width, dep=(tsq,))
        g["xb_mm"], g["xsq_mm"] = t1, t2
        done, tfree = ln_coeffs(ctx, bsx, bsq, rows, width, (t1, t2),
                                want_outer=True, fused_eps2=fused_eps2)
        ctx.banks.release(isx, tfree)
        ctx.banks.release(isq, tfree)
        return done, tcast, tsq

    p.label = "l0a.entry"
    cq_last = None
    for n in range(NG):
        nq0 = n * NQ
        hf_half_toks = []
        for hh in range(HPN):
            c0 = nq0 + hh * NH
            p.wait_all("sync", g["xf_w"])
            tx = halfload(x_in, c0, NH, XF)
            p.wait("vector", tx)
            p.wait("scalar", tx)
            rows = {"mu": R_mu, "var": R_var, "tmp": R_tmp,
                    "iv": R_iv, "c": CROW[:, c0:c0 + NH],
                    "nmc": NMCROW[:, c0:c0 + NH],
                    "ctot": R_ctot, "nmct": R_nmct}
            done, tcast, tsq = stats_and_coeffs(XF, NH, rows, 1e-5)
            ibc1, bc1 = ctx.banks.acquire(p)
            ibc2, bc2 = ctx.banks.acquire(p)
            tb1 = bcast_row(ctx, bc1, R_ctot, NH, dep=done)
            tb2 = bcast_row(ctx, bc2, R_nmct, NH, dep=done)
            p.wait_all("vector", [tb1, tb2] + g["sc1_w"])
            last_tt = None
            th = None
            for cc in range(NCH):
                xc = XF[:, cc * NH:(cc + 1) * NH]
                s1 = SC1[:, cc * NH:(cc + 1) * NH]
                p.op("vector", "tensor_mul", s1, xc, bc1[0:128, 0:NH])
                last_tt = p.op("vector", "tensor_add", s1, s1,
                               bc2[0:128, 0:NH], inc="ve")
                p.wait("vector", g["hf_r"])
                th = p.op("vector", "tensor_scalar",
                          HF[:, cc * NQ + hh * NH:cc * NQ + hh * NH + NH],
                          s1, gamma[:, cc:cc + 1], beta[:, cc:cc + 1],
                          ALU.mult, ALU.add, inc="ve")
            hf_half_toks.append(th)
            g["xf_w"] = [last_tt, tsq, tcast]
            g["sc1_w"] = [th]
            ctx.banks.release(ibc1, last_tt)
            ctx.banks.release(ibc2, last_tt)
        if n == 0:
            ctx.dump("hf0", HF)
        p.wait_all("tensor", hf_half_toks + [tw1, tw2])
        for dst, wbuf in ((CQ, W1), (CKV, W2)):
            for m in range(4):
                ib, bk = ctx.banks.acquire(p)
                tok = None
                for kc in range(NCH):
                    tok = p.op("tensor", "matmul", bk[0:128, 0:NQ],
                               wbuf[:, kc * 512 + m * 128:kc * 512 + m * 128 + 128],
                               HF[:, kc * NQ:(kc + 1) * NQ],
                               start=(kc == 0), stop=(kc == 7),
                               inc="mm" if kc == 7 else None)
                p.wait("scalar", tok)
                cq_last = p.op("scalar", "activation",
                               dst[:, m * S + nq0:m * S + nq0 + NQ],
                               bk[0:128, 0:NQ], AF.Copy, inc="sc")
                ctx.banks.release(ib, cq_last)
        g["hf_r"] = ("mm", p.counts["mm"])

    ctx.dump("cq", CQ)
    ctx.dump("ckv", CKV)
    tw6 = wload(W6, "woutT", 1024, wait=g["hf_r"])
    tw3 = wload(W3, "wupqT", 1024, wait=g["hf_r"])

    p.label = "l0a.k"
    p.wait_all("tensor", [cq_last, tw4])
    kb_last = None
    tkmc = None
    for n in range(NG):
        nq0 = n * NQ
        ink, bnk = ctx.banks.acquire(p)
        sel_mm = None
        for m in range(NCH):
            ib, bk = ctx.banks.acquire(p)
            tok = None
            for kc in range(4):
                tok = p.op("tensor", "matmul", bk[0:128, 0:NQ],
                           W4[:, kc * 1024 + m * 128:kc * 1024 + m * 128 + 128],
                           CKV[:, kc * S + nq0:kc * S + nq0 + NQ],
                           start=(kc == 0), stop=(kc == 3),
                           inc="mm" if kc == 3 else None)
            p.wait("scalar", tok)
            p.wait("scalar", g["tbr_rot"])
            p.wait("scalar", g["tbr_ta"])
            tkraw = p.op("scalar", "activation", TBr, bk[0:128, 0:NQ],
                         AF.Copy, inc="sc")
            ctx.banks.release(ib, tkraw)
            ir, br = ctx.banks.acquire(p)
            p.wait("tensor", tkraw)
            trot = p.op("tensor", "matmul", br[0:128, 0:NQ], perm, TBr,
                        start=True, stop=True, inc="mm")
            g["tbr_rot"] = trot
            p.wait("vector", tkraw)
            ta = p.op("vector", "tensor_mul", TBa, TBr,
                      ropeC[:, nq0:nq0 + NQ], inc="ve")
            g["tbr_ta"] = ta
            p.wait("vector", trot)
            tb = p.op("vector", "tensor_mul", TBb, br[0:128, 0:NQ],
                      ropeS[:, nq0:nq0 + NQ], inc="ve")
            ctx.banks.release(ir, tb)
            kdst = KB[:, m * S + nq0:m * S + nq0 + NQ]
            kb_last = p.op("vector", "tensor_add", kdst, TBa, TBb, inc="ve")
            p.wait("scalar", kb_last)
            p.wait("scalar", sel_mm)
            tsqk = p.op("scalar", "activation", SQT, kdst, AF.Square,
                        inc="sc")
            p.wait("tensor", tsqk)
            sel_mm = p.op("tensor", "matmul", bnk[0:16, 0:NQ],
                          ctx.cbv(f"sel_qk_{m}", w=16), SQT,
                          start=(m == 0), stop=(m == 7), inc="mm")
        p.wait("vector", sel_mm)
        tkmc = p.op("vector", "reduce_max", KMC[:, n:n + 1],
                    bnk[0:16, 0:NQ], axis=AX.X, inc="ve")
        ctx.banks.release(ink, tkmc)
    ctx.dump("kb", KB)
    p.wait("vector", tkmc)
    tkm = p.op("vector", "reduce_max", KMX, KMC, axis=AX.X, inc="ve")
    tks = p.op("vector", "tensor_copy", KSQ, KMX, inc="ve")

    p.label = "l0a.v"
    tw5 = wload(W5, "wupvT", 1024, wait=("mm", p.counts["mm"]))
    p.wait_all("tensor", [tw5, t_vones, cq_last])
    vt_last = None
    for tt in range(16):
        for dn in range(2):
            ib, bk = ctx.banks.acquire(p)
            tok = None
            for kc in range(4):
                tok = p.op("tensor", "matmul", bk[0:128, 0:NQ],
                           CKV[:, kc * S + tt * 128:kc * S + tt * 128 + 128],
                           W5[:, kc * 1024 + dn * 512:kc * 1024 + dn * 512 + 512],
                           start=(kc == 0), stop=(kc == 3),
                           inc="mm" if kc == 3 else None)
            p.wait("vector", tok)
            base = tt * 1040 + dn * 520
            dst = VT[:, base:base + 520].rearrange(
                "p (h dd) -> p h dd", dd=65)[:, :, 0:64]
            vt_last = p.op("vector", "tensor_copy", dst,
                           bk[0:128, 0:NQ].rearrange("p (h d) -> p h d", d=64),
                           inc="ve")
            ctx.banks.release(ib, vt_last)
    ctx.dump("vt", VT)
    g["qn_r"] = ("mm", p.counts["mm"])
    g["att_r"] = ("mm", p.counts["mm"])

    p.wait_all("tensor", [tw3, tw6])
    tst = None
    for n in range(NG):
        nq0 = n * NQ
        p.label = "l0a.q"
        inq, bnq = ctx.banks.acquire(p)
        sel_mm = None
        qn_last = None
        p.wait("vector", g["qn_r"])
        for m in range(NCH):
            ib, bk = ctx.banks.acquire(p)
            tok = None
            for kc in range(4):
                tok = p.op("tensor", "matmul", bk[0:128, 0:NQ],
                           W3[:, kc * 1024 + m * 128:kc * 1024 + m * 128 + 128],
                           CQ[:, kc * S + nq0:kc * S + nq0 + NQ],
                           start=(kc == 0), stop=(kc == 3),
                           inc="mm" if kc == 3 else None)
            p.wait("scalar", tok)
            p.wait("scalar", g["tbr_rot"])
            p.wait("scalar", g["tbr_ta"])
            tqraw = p.op("scalar", "activation", TBr, bk[0:128, 0:NQ],
                         AF.Copy, inc="sc")
            ctx.banks.release(ib, tqraw)
            ir, br = ctx.banks.acquire(p)
            p.wait("tensor", tqraw)
            trot = p.op("tensor", "matmul", br[0:128, 0:NQ], perm, TBr,
                        start=True, stop=True, inc="mm")
            g["tbr_rot"] = trot
            p.wait("vector", tqraw)
            ta = p.op("vector", "tensor_mul", TBa, TBr,
                      ropeC[:, nq0:nq0 + NQ], inc="ve")
            g["tbr_ta"] = ta
            p.wait("vector", trot)
            tb = p.op("vector", "tensor_mul", TBb, br[0:128, 0:NQ],
                      ropeS[:, nq0:nq0 + NQ], inc="ve")
            ctx.banks.release(ir, tb)
            qdst = QN[:, m * NQ:(m + 1) * NQ]
            qn_last = p.op("vector", "tensor_add", qdst, TBa, TBb, inc="ve")
            p.wait("scalar", qn_last)
            p.wait("scalar", sel_mm)
            tsqq = p.op("scalar", "activation", SQT, qdst, AF.Square,
                        inc="sc")
            p.wait("tensor", tsqq)
            sel_mm = p.op("tensor", "matmul", bnq[0:16, 0:NQ],
                          ctx.cbv(f"sel_qk_{m}", w=16), SQT,
                          start=(m == 0), stop=(m == 7), inc="mm")
        if n == 0:
            ctx.dump("qn0", QN)
        p.label = "l0a.bound"
        p.wait("vector", sel_mm)
        tqm = p.op("vector", "reduce_max", QMX, bnq[0:16, 0:NQ], axis=AX.X,
                   inc="ve")
        ctx.banks.release(inq, tqm)
        p.wait("scalar", tqm)
        tqs = p.op("scalar", "activation", QSQ, QMX, AF.Sqrt, inc="sc")
        p.wait("vector", tqs)
        p.wait("vector", tks)
        p.wait("vector", g["tin_r"])
        tnb = p.op("vector", "scalar_tensor_tensor", NB, QSQ, -0.125, KSQ,
                   ALU.mult, ALU.mult, inc="ve")
        p.wait("vector", tnb)
        ttin = p.op("vector", "tensor_copy", TIN[0:16, 0:1], NB, inc="ve")
        it, btr = ctx.banks.acquire(p)
        p.wait("tensor", ttin)
        ttr = p.op("tensor", "transpose", btr[0:128, 0:128], TIN, ident,
                   inc="mm")
        g["tin_r"] = ttr
        p.wait("scalar", ttr)
        tbrow = p.op("scalar", "activation", BROW[0:1, 0:16],
                     btr[0:1, 0:16], AF.Copy, inc="sc")
        ctx.banks.release(it, tbrow)
        ibc, bbc = ctx.banks.acquire(p)
        tbc = bcast_row(ctx, bbc, BROW[0:1, 0:16], 16, dep=(tbrow,))
        p.wait("scalar", tbc)
        tbcol = p.op("scalar", "activation", BCOL, bbc[0:128, 0:16],
                     AF.Copy, inc="sc")
        ctx.banks.release(ibc, tbcol)

        if n == 0:
            ctx.dump("tin0", TIN)
            ctx.dump("smv0", SMV)
            ctx.dump("brow0", BROW)
            ctx.dump("bcol0", BCOL)
        p.label = "l0a.attn"
        p.wait_all("tensor", [qn_last, kb_last, vt_last, tbcol])
        att_last = None
        for hp in range(8):
            ia, boa = ctx.banks.acquire(p)
            ib2, bob = ctx.banks.acquire(p)
            tva = tvb = None
            for j in range(16):
                isa, bsa = ctx.banks.acquire(p)
                isb, bsb = ctx.banks.acquire(p)
                kc0 = hp * S + j * 128
                tsa = p.op("tensor", "matmul", bsa[0:128, 0:NQ],
                           KB[0:64, kc0:kc0 + 128],
                           QN[0:64, hp * NQ:(hp + 1) * NQ],
                           start=True, stop=True, tile_position=(0, 0),
                           inc="mm")
                tsb = p.op("tensor", "matmul", bsb[0:128, 0:NQ],
                           KB[64:128, kc0:kc0 + 128],
                           QN[64:128, hp * NQ:(hp + 1) * NQ],
                           start=True, stop=True, tile_position=(64, 0),
                           inc="mm")
                sa = (2 * j) % 4
                sb = (2 * j + 1) % 4
                p.wait("scalar", tsa)
                p.wait("scalar", g["slot"][sa])
                p.wait("scalar", g["xb_mm"] if sa < 2 else g["xsq_mm"])
                tea = p.op("scalar", "activation", PT[sa],
                           bsa[0:128, 0:NQ], AF.Exp,
                           bias=BCOL[:, 2 * hp:2 * hp + 1], scale=0.125,
                           inc="sc")
                ctx.banks.release(isa, tea)
                p.wait("scalar", tsb)
                p.wait("scalar", g["slot"][sb])
                p.wait("scalar", g["xb_mm"] if sb < 2 else g["xsq_mm"])
                teb = p.op("scalar", "activation", PT[sb],
                           bsb[0:128, 0:NQ], AF.Exp,
                           bias=BCOL[:, 2 * hp + 1:2 * hp + 2], scale=0.125,
                           inc="sc")
                ctx.banks.release(isb, teb)
                p.wait("tensor", tea)
                tva = p.op("tensor", "matmul", boa[0:65, 0:NQ],
                           VT[:, j * 1040 + (2 * hp) * 65:
                              j * 1040 + (2 * hp) * 65 + 65],
                           PT[sa], start=(j == 0), stop=(j == 15), inc="mm")
                g["slot"][sa] = tva
                p.wait("tensor", teb)
                tvb = p.op("tensor", "matmul", bob[0:65, 0:NQ],
                           VT[:, j * 1040 + (2 * hp + 1) * 65:
                              j * 1040 + (2 * hp + 1) * 65 + 65],
                           PT[sb], start=(j == 0), stop=(j == 15), inc="mm")
                g["slot"][sb] = tvb
            p.wait("scalar", tva)
            p.wait("scalar", tvb)
            p.wait_all("scalar", g["den_w"])
            td1 = p.op("scalar", "activation", DEN[0:1, 0:NQ],
                       boa[64:65, 0:NQ], AF.Copy, bias=1e-30, inc="sc")
            td2 = p.op("scalar", "activation", DEN[64:65, 0:NQ],
                       bob[64:65, 0:NQ], AF.Copy, bias=1e-30, inc="sc")
            p.wait("vector", td1)
            p.wait("vector", td2)
            p.wait("vector", g["deni_r"])
            p.op("vector", "reciprocal", DENI[0:1, 0:NQ], DEN[0:1, 0:NQ])
            tri = p.op("vector", "reciprocal", DENI[64:65, 0:NQ],
                       DEN[64:65, 0:NQ], inc="ve")
            ir2, brb = ctx.banks.acquire(p)
            p.wait("tensor", tri)
            trb = p.op("tensor", "matmul", brb[0:128, 0:NQ],
                       ctx.cfv("sel_den", w=128, rn=65), DENI[0:65, 0:NQ],
                       start=True, stop=True, inc="mm")
            g["deni_r"] = trb
            p.wait("scalar", trb)
            trbe = p.op("scalar", "activation", DEN, brb[0:128, 0:NQ],
                        AF.Copy, inc="sc")
            ctx.banks.release(ir2, trbe)
            p.wait("vector", trbe)
            p.wait("vector", g["att_r"])
            ta1 = p.op("vector", "tensor_mul", ATT[0:64, hp * NQ:(hp + 1) * NQ],
                       boa[0:64, 0:NQ], DEN[0:64, :], inc="ve")
            ctx.banks.release(ia, [td1, ta1])
            ta2 = p.op("vector", "tensor_mul",
                       ATT[64:128, hp * NQ:(hp + 1) * NQ],
                       bob[0:64, 0:NQ], DEN[64:128, :], inc="ve")
            ctx.banks.release(ib2, [td2, ta2])
            g["den_w"] = [ta1, ta2, tri]
            att_last = ta2
        g["qn_r"] = ("mm", p.counts["mm"])

        if n == 0:
            ctx.dump("att0", ATT)
        p.label = "l0a.exit"
        p.wait("tensor", att_last)
        for hh in range(HPN):
            c0 = nq0 + hh * NH
            p.wait_all("sync", g["xf_w"])
            tx2 = halfload(x_in, c0, NH, XF)
            ao_toks = []
            for m in range(NCH):
                ib, bk = ctx.banks.acquire(p)
                tok = None
                for kc in range(NCH):
                    tok = p.op("tensor", "matmul", bk[0:128, 0:NH],
                               W6[:, kc * 1024 + m * 128:kc * 1024 + m * 128 + 128],
                               ATT[:, kc * NQ + hh * NH:kc * NQ + hh * NH + NH],
                               start=(kc == 0), stop=(kc == 7),
                               inc="mm" if kc == 7 else None)
                p.wait("scalar", tok)
                p.wait_all("scalar", g["ao_w"][m])
                tev = p.op("scalar", "activation", AO[:, m * NH:(m + 1) * NH],
                           bk[0:128, 0:NH], AF.Copy, inc="sc")
                ao_toks.append(tev)
                ctx.banks.release(ib, tev)
            g["att_r"] = ("mm", p.counts["mm"])
            if n == 0 and hh == 0:
                ctx.dump("ao00", AO)
            ibl1, bl1 = ctx.banks.acquire(p)
            ibl2, bl2 = ctx.banks.acquire(p)
            tb1 = bcast_row(ctx, bl1, CROW[:, c0:c0 + NH], NH)
            tb2 = bcast_row(ctx, bl2, NMCROW[:, c0:c0 + NH], NH)
            p.wait_all("vector", [tb1, tb2, tx2] + g["sc1_w"])
            ell_toks = []
            last_tt = None
            for cc in range(NCH):
                xc = XF[:, cc * NH:(cc + 1) * NH]
                s1 = SC1[:, cc * NH:(cc + 1) * NH]
                aoc = AO[:, cc * NH:(cc + 1) * NH]
                p.op("vector", "tensor_mul", s1, xc, bl1[0:128, 0:NH])
                last_tt = p.op("vector", "tensor_add", s1, s1,
                               bl2[0:128, 0:NH], inc="ve")
                p.wait("gpsimd", last_tt)
                p.wait("gpsimd", ao_toks[cc])
                tg = p.op("gpsimd", "tensor_add", aoc, aoc, s1, inc="gp")
                ell_toks.append(tg)
            ctx.banks.release(ibl1, last_tt)
            ctx.banks.release(ibl2, last_tt)
            if n == 0 and hh == 0:
                ctx.dump("ell00", AO)
            rows = {"mu": R_mu, "var": R_var, "tmp": R_tmp, "iv": R_iv,
                    "c": R_c2, "nmc": R_nmc2}
            done, tcast, tsq = stats_and_coeffs(AO, NH, rows, None,
                                                extra_cast_dep=ell_toks)
            ibc1, bc1 = ctx.banks.acquire(p)
            ibc2, bc2 = ctx.banks.acquire(p)
            tb1 = bcast_row(ctx, bc1, R_c2, NH, dep=done)
            tb2 = bcast_row(ctx, bc2, R_nmc2, NH, dep=done)
            p.wait_all("vector", [tb1, tb2])
            xn_toks = []
            exit_tt = None
            for cc in range(NCH):
                aoc = AO[:, cc * NH:(cc + 1) * NH]
                xc = XF[:, cc * NH:(cc + 1) * NH]
                p.op("vector", "tensor_mul", aoc, aoc, bc1[0:128, 0:NH])
                exit_tt = p.op("vector", "tensor_add", aoc, aoc,
                               bc2[0:128, 0:NH], inc="ve")
                p.wait("gpsimd", exit_tt)
                tg = p.op("gpsimd", "tensor_add", aoc, aoc, xc, inc="gp")
                xn_toks.append(tg)
            ctx.banks.release(ibc1, exit_tt)
            ctx.banks.release(ibc2, exit_tt)
            p.wait_all("sync", xn_toks)
            tst = p.op("sync", "dma_start", inc="st", incn=16,
                       out=x_out[:, :, c0:c0 + NH].rearrange("c p w -> p c w"),
                       in_=AO.rearrange("p (c w) -> p c w", w=NH))
            for cc in range(NCH):
                g["ao_w"][cc] = [xn_toks[cc], tcast, tsq, tst]
            g["xf_w"] = [xn_toks[-1], exit_tt, last_tt, tcast, tsq]
            g["sc1_w"] = [xn_toks[-1]] + ell_toks

    ba.pop(); fa.pop()
    return tst



def emit_l0f(ctx, x_in, x_out, W):
    p = ctx.p
    ba, fa = ctx.ba, ctx.fa
    ba.push(); fa.push()
    cm, g, rows_ = ctx.cm, ctx.g, ctx.rows

    HF = ba.alloc(NCH * S)
    WT = ba.alloc(NCH * 1024)
    WG = ba.alloc(NCH * 8)
    WU = ba.alloc(NCH * 128)
    WV = ba.alloc(1024)
    TM = ba.alloc(NH)
    GTB = ba.alloc(NH)

    XF, AO, SC1 = cm["XF"], cm["AO"], cm["SC1"]
    TIN, DEN, DENI = cm["TIN"], cm["DEN"], cm["DENI"]
    BROW, BCOL, SMV = cm["BROW"], cm["BCOL"], cm["SMV"]
    XB, XSQ, SQT = cm["XB"], cm["XSQ"], cm["SQT"]
    R_mu, R_var, R_tmp, R_iv = rows_["mu"], rows_["var"], rows_["tmp"], rows_["iv"]
    R_ctot, R_nmct = rows_["ctot"], rows_["nmct"]
    R_c2, R_nmc2 = rows_["c2"], rows_["nmc2"]
    ident = ctx.cfv("ident_f32", w=128)
    selg = ctx.cfv("selg", w=128, rn=8)

    def wload(dst, wname, mcols, wait=None):
        p.wait("sync", ctx.wtok.get(wname))
        p.wait("sync", wait)
        return p.op("sync", "dma_start", inc="ld", incn=16,
                    out=dst.rearrange("p (c m) -> p c m", m=mcols),
                    in_=W[wname].rearrange("(c p) m -> p c m", p=128))

    twt = wload(WT, "l0f_wT", 1024)
    twg = wload(WG, "l0f_gwT", 8)
    twu = wload(WU, "l0f_uT", 128)
    p.wait("sync", ctx.wtok.get("l0f_vall"))
    twv = p.op("sync", "dma_start", inc="ld", incn=16, out=WV, in_=W["l0f_vall"])

    def halfload(c0, width):
        return p.op("sync", "dma_start", inc="ld", incn=16,
                    out=XF.rearrange("p (c w) -> p c w", w=width),
                    in_=x_in[:, :, c0:c0 + width].rearrange("c p w -> p c w"))

    def stats_and_coeffs(srcf, width, rows, fused_eps2, extra_cast_dep=()):
        p.wait("vector", g["xb_mm"])
        p.wait_all("vector", [g["slot"][0], g["slot"][1]])
        p.wait_all("vector", extra_cast_dep)
        tcast = p.op("vector", "tensor_copy", XB[:, 0:NCH * width], srcf,
                     inc="ve")
        g["slot"][0] = tcast
        g["slot"][1] = tcast
        p.wait_all("scalar", [g["slot"][2], g["slot"][3]])
        p.wait("scalar", g["xsq_mm"])
        p.wait_all("scalar", extra_cast_dep)
        tsq = p.op("scalar", "activation", XSQ[:, 0:NCH * width], srcf,
                   AF.Square, inc="sc")
        g["slot"][2] = tsq
        g["slot"][3] = tsq
        isx, bsx = ctx.banks.acquire(p)
        isq, bsq = ctx.banks.acquire(p)
        t1 = colsum_mms(ctx, bsx, XB[:, 0:NCH * width], width, dep=(tcast,))
        t2 = colsum_mms(ctx, bsq, XSQ[:, 0:NCH * width], width, dep=(tsq,))
        g["xb_mm"], g["xsq_mm"] = t1, t2
        done, tfree = ln_coeffs(ctx, bsx, bsq, rows, width, (t1, t2),
                                want_outer=True, fused_eps2=fused_eps2)
        ctx.banks.release(isx, tfree)
        ctx.banks.release(isq, tfree)
        return done, tcast, tsq

    p.label = "l0f.entry"
    hf_toks = []
    for n in range(NG):
        for hh in range(HPN):
            c0 = n * NQ + hh * NH
            p.wait_all("sync", g["xf_w"])
            tx = halfload(c0, NH)
            p.wait("vector", tx)
            p.wait("scalar", tx)
            rows = {"mu": R_mu, "var": R_var, "tmp": R_tmp, "iv": R_iv,
                    "c": R_ctot, "nmc": R_nmct}
            done, tcast, tsq = stats_and_coeffs(XF, NH, rows, None)
            ibc1, bc1 = ctx.banks.acquire(p)
            ibc2, bc2 = ctx.banks.acquire(p)
            tb1 = bcast_row(ctx, bc1, R_ctot, NH, dep=done)
            tb2 = bcast_row(ctx, bc2, R_nmct, NH, dep=done)
            p.wait_all("vector", [tb1, tb2] + g["sc1_w"])
            last_tt = None
            for cc in range(NCH):
                xc = XF[:, cc * NH:(cc + 1) * NH]
                s1 = SC1[:, cc * NH:(cc + 1) * NH]
                p.op("vector", "tensor_mul", s1, xc, bc1[0:128, 0:NH])
                last_tt = p.op("vector", "tensor_add",
                               HF[:, cc * S + c0:cc * S + c0 + NH], s1,
                               bc2[0:128, 0:NH], inc="ve")
            hf_toks.append(last_tt)
            g["xf_w"] = [last_tt, tsq, tcast]
            g["sc1_w"] = []
            ctx.banks.release(ibc1, last_tt)
            ctx.banks.release(ibc2, last_tt)

    p.label = "l0f.gates"
    p.wait_all("tensor", hf_toks + [twg])
    for n in range(NG):
        nq0 = n * NQ
        ib, bk = ctx.banks.acquire(p)
        tok = None
        for kc in range(NCH):
            tok = p.op("tensor", "matmul", bk[0:8, 0:NQ],
                       WG[:, kc * 8:(kc + 1) * 8],
                       HF[:, kc * S + nq0:kc * S + nq0 + NQ],
                       start=(kc == 0), stop=(kc == 7),
                       inc="mm" if kc == 7 else None)
        p.wait("vector", tok)
        tmx = p.op("vector", "reduce_max", SMV[0:8, 5 + n:6 + n],
                   bk[0:8, 0:NQ], axis=AX.X, inc="ve")
        ctx.banks.release(ib, tmx)
    tg8 = p.op("vector", "reduce_max", SMV[0:8, 0:1], SMV[0:8, 5:9],
               axis=AX.X, inc="ve")
    p.wait("vector", g["tin_r"])
    p.wait("vector", tg8)
    ttin = p.op("vector", "tensor_copy", TIN[0:8, 0:1], SMV[0:8, 0:1],
                inc="ve")
    it, btr = ctx.banks.acquire(p)
    p.wait("tensor", ttin)
    ttr = p.op("tensor", "transpose", btr[0:128, 0:128], TIN, ident, inc="mm")
    g["tin_r"] = ttr
    p.wait("scalar", ttr)
    tbrow = p.op("scalar", "activation", BROW[0:1, 0:8], btr[0:1, 0:8],
                 AF.Copy, inc="sc")
    ctx.banks.release(it, tbrow)
    p.wait("vector", tbrow)
    tbg = p.op("vector", "reduce_max", SMV[0:1, 1:2], BROW[0:1, 0:8],
               axis=AX.X, inc="ve")
    p.wait("vector", tbg)
    tnbg = p.op("vector", "tensor_scalar_mul", SMV[0:1, 2:3], SMV[0:1, 1:2],
                -1.0, inc="ve")
    ibg, bbg = ctx.banks.acquire(p)
    p.wait("tensor", tnbg)
    tbgm = p.op("tensor", "matmul", bbg[0:8, 0:1],
                ctx.cfv("ones_row", w=8, rn=1), SMV[0:1, 2:3],
                start=True, stop=True, inc="mm")
    p.wait("scalar", tbgm)
    tbgc = p.op("scalar", "activation", BCOL[0:8, 0:1], bbg[0:8, 0:1],
                AF.Copy, inc="sc")
    ctx.banks.release(ibg, tbgc)

    p.label = "l0f.body"
    p.wait_all("tensor", [twt, twu, twv, tbgc])
    tst = None
    den_g = []
    for n in range(NG):
        for hh in range(HPN):
            c0 = n * NQ + hh * NH
            p.wait_all("sync", g["xf_w"])
            tx2 = halfload(c0, NH)
            igl, bgl = ctx.banks.acquire(p)
            tok = None
            for kc in range(NCH):
                tok = p.op("tensor", "matmul", bgl[0:8, 0:NH],
                           WG[:, kc * 8:(kc + 1) * 8],
                           HF[:, kc * S + c0:kc * S + c0 + NH],
                           start=(kc == 0), stop=(kc == 7),
                           inc="mm" if kc == 7 else None)
            p.wait_all("scalar", [tok, tbgc] + den_g)
            tex = p.op("scalar", "activation", DEN[0:8, 0:NH],
                       bgl[0:8, 0:NH], AF.Exp,
                       bias=BCOL[0:8, 0:1], inc="sc")
            ctx.banks.release(igl, tex)
            p.wait("vector", tex)
            texb = p.op("vector", "tensor_copy", SQT[0:8, 0:NH],
                        DEN[0:8, 0:NH], inc="ve")
            idn, bdn = ctx.banks.acquire(p)
            p.wait("tensor", texb)
            tdn = p.op("tensor", "matmul", bdn[0:1, 0:NH],
                       ctx.cbv("ones_col", rn=8), SQT[0:8, 0:NH],
                       start=True, stop=True, inc="mm")
            p.wait("scalar", tdn)
            trg0 = p.op("scalar", "activation", R_c2, bdn[0:1, 0:NH], AF.Ln,
                        inc="sc")
            ctx.banks.release(idn, trg0)
            p.wait("scalar", trg0)
            trg = p.op("scalar", "activation", R_c2, R_c2, AF.Exp,
                       scale=-1.0, inc="sc")
            irb, brb = ctx.banks.acquire(p)
            p.wait("tensor", trg)
            trb = p.op("tensor", "matmul", brb[0:8, 0:NH],
                       ctx.cfv("ones_row", w=8, rn=1), R_c2,
                       start=True, stop=True, inc="mm")
            p.wait("vector", trb)
            p.wait("vector", g["deni_r"])
            tegn = p.op("vector", "tensor_mul", DENI[0:8, 0:NH],
                        DEN[0:8, 0:NH], brb[0:8, 0:NH], inc="ve")
            ctx.banks.release(irb, tegn)
            den_g = [tegn, texb]
            ig1, bg1 = ctx.banks.acquire(p)
            p.wait("tensor", tegn)
            tg128 = p.op("tensor", "matmul", bg1[0:128, 0:NH], selg,
                         DENI[0:8, 0:NH], start=True, stop=True, inc="mm")
            g["deni_r"] = tg128
            it2, bt2 = ctx.banks.acquire(p)
            tok = None
            for kc in range(NCH):
                tok = p.op("tensor", "matmul", bt2[0:128, 0:NH],
                           WU[:, kc * 128:(kc + 1) * 128],
                           HF[:, kc * S + c0:kc * S + c0 + NH],
                           start=(kc == 0), stop=(kc == 7),
                           inc="mm" if kc == 7 else None)
            p.wait("scalar", tok)
            p.wait("scalar", g.get("tm_r"))
            ttm = p.op("scalar", "activation", TM, bt2[0:128, 0:NH],
                       AF.Copy, inc="sc")
            ctx.banks.release(it2, ttm)
            p.wait("vector", ttm)
            p.wait("vector", tg128)
            p.wait("vector", g.get("gtb_r"))
            tgt = p.op("vector", "tensor_mul", GTB, TM, bg1[0:128, 0:NH],
                       inc="ve")
            ctx.banks.release(ig1, tgt)
            g["tm_r"] = tgt
            p.wait("tensor", tgt)
            ao_toks = []
            for m in range(NCH):
                ib, bk = ctx.banks.acquire(p)
                for kc in range(NCH):
                    p.op("tensor", "matmul", bk[0:128, 0:NH],
                         WT[:, kc * 1024 + m * 128:kc * 1024 + m * 128 + 128],
                         HF[:, kc * S + c0:kc * S + c0 + NH],
                         start=(kc == 0), stop=False)
                tok = p.op("tensor", "matmul", bk[0:128, 0:NH],
                           WV[:, m * 128:(m + 1) * 128], GTB,
                           start=False, stop=True, inc="mm")
                g["gtb_r"] = tok
                p.wait("scalar", tok)
                p.wait_all("scalar", g["ao_w"][m])
                tev = p.op("scalar", "activation", AO[:, m * NH:(m + 1) * NH],
                           bk[0:128, 0:NH], AF.Copy, inc="sc")
                ao_toks.append(tev)
                ctx.banks.release(ib, tev)
            rows = {"mu": R_mu, "var": R_var, "tmp": R_tmp, "iv": R_iv,
                    "c": R_c2, "nmc": R_nmc2}
            done, tcast, tsq = stats_and_coeffs(AO, NH, rows, None,
                                                extra_cast_dep=ao_toks)
            ibc1, bc1 = ctx.banks.acquire(p)
            ibc2, bc2 = ctx.banks.acquire(p)
            tb1 = bcast_row(ctx, bc1, R_c2, NH, dep=done)
            tb2 = bcast_row(ctx, bc2, R_nmc2, NH, dep=done)
            p.wait_all("vector", [tb1, tb2, tx2])
            xn_toks = []
            exit_tt = None
            for cc in range(NCH):
                aoc = AO[:, cc * NH:(cc + 1) * NH]
                xc = XF[:, cc * NH:(cc + 1) * NH]
                p.op("vector", "tensor_mul", aoc, aoc, bc1[0:128, 0:NH])
                exit_tt = p.op("vector", "tensor_add", aoc, aoc,
                               bc2[0:128, 0:NH], inc="ve")
                p.wait("gpsimd", exit_tt)
                tg_ = p.op("gpsimd", "tensor_add", aoc, aoc, xc, inc="gp")
                xn_toks.append(tg_)
            ctx.banks.release(ibc1, exit_tt)
            ctx.banks.release(ibc2, exit_tt)
            p.wait_all("sync", xn_toks)
            tst = p.op("sync", "dma_start", inc="st", incn=16,
                       out=x_out[:, :, c0:c0 + NH].rearrange("c p w -> p c w"),
                       in_=AO.rearrange("p (c w) -> p c w", w=NH))
            for cc in range(NCH):
                g["ao_w"][cc] = [xn_toks[cc], tcast, tsq, tst]
            g["xf_w"] = [xn_toks[-1], exit_tt, tcast, tsq]

    ba.pop(); fa.pop()
    return tst


def host_weights_l0f(inputs):
    T = lambda a: np.ascontiguousarray(np.asarray(a, np.float32).T)
    uallT = np.ascontiguousarray(
        np.transpose(np.asarray(inputs["L0_ffn_lora_U"], np.float32),
                     (2, 0, 1)).reshape(1024, 128))
    vall = np.ascontiguousarray(
        np.transpose(np.asarray(inputs["L0_ffn_lora_V"], np.float32),
                     (0, 2, 1)).reshape(128, 1024))
    return {
        "l0f_wT": T(inputs["L0_ffn_weight"]),
        "l0f_gwT": T(inputs["L0_ffn_gate_w"]),
        "l0f_uT": uallT,
        "l0f_vall": vall,
    }


L0F_WSHAPES = {"l0f_wT": [1024, 1024], "l0f_gwT": [1024, 8],
               "l0f_uT": [1024, 128], "l0f_vall": [128, 1024]}


def build_consts_l0f(c):
    selg = np.zeros((128, 128), np.float32)
    for e in range(8):
        selg[e, 16 * e:16 * e + 16] = 1.0 / 16.0
    c.add_f32("selg", selg)



def emit_l1a(ctx, x_in, x_out, W):
    p = ctx.p
    ba, fa = ctx.ba, ctx.fa
    ba.push(); fa.push()
    cm, g, rows_ = ctx.cm, ctx.g, ctx.rows
    ell = ctx.ell

    XN = ba.alloc(NCH * S)
    HID = ba.alloc(24 * NQ)
    WM3 = [ba.alloc(3 * 1024) for _ in range(2)]
    WFF = [ba.alloc(2 * 1024) for _ in range(2)]
    WOS = [ba.alloc(3 * 1024) for _ in range(2)]
    SGB = ba.alloc(NQ)
    UPB = ba.alloc(NQ)

    fo_off = fa.off
    SCQ = fa.alloc(4 * NQ)
    RC2B = fa.alloc(2048)
    FO = fa.t[:, fo_off:fo_off + NCH * NQ]
    XSA = SCQ[:, 0:NQ]
    TMP = SCQ[:, NQ:2 * NQ]
    LM = SCQ[:, 2 * NQ:3 * NQ]
    SCO = SCQ[:, 3 * NQ:4 * NQ]
    RC2ROW = RC2B[0:1, 0:S]
    SCSTB = fa.alloc(1)
    SCST = SCSTB[0:128, 0:1]

    XF, AO, SC1 = cm["XF"], cm["AO"], cm["SC1"]
    XB, XSQ, SQT = cm["XB"], cm["XSQ"], cm["SQT"]
    R_mu, R_var, R_tmp, R_iv = rows_["mu"], rows_["var"], rows_["tmp"], rows_["iv"]
    R_ctot, R_nmct = rows_["ctot"], rows_["nmct"]
    R_c2, R_nmc2 = rows_["c2"], rows_["nmc2"]
    CROW, NMCROW = rows_["CROW"], rows_["NMCROW"]

    bg = ctx.cfv("l1a_bg", w=NCH)
    bv = ctx.cfv("l1a_bv", w=NCH)
    bd = ctx.cfv("l1a_bd", w=NCH)
    wmix = ctx.cfv("l1a_wmix", w=NCH)
    wffn = ctx.cfv("l1a_wffn", w=NCH)

    def wload_cols(dst, wname, c0, ncols, kchunks, wait=None):
        p.wait("sync", ctx.wtok.get(wname))
        p.wait("sync", wait)
        return p.op("sync", "dma_start", inc="ld", incn=16,
                    out=dst[:, 0:kchunks * ncols].rearrange(
                        "p (c m) -> p c m", m=ncols),
                    in_=W[wname][:, c0:c0 + ncols].rearrange(
                        "(c p) m -> p c m", p=128))

    def halfload(dram, c0, width, dst):
        return p.op("sync", "dma_start", inc="ld", incn=16,
                    out=dst.rearrange("p (c w) -> p c w", w=width),
                    in_=dram[:, :, c0:c0 + width].rearrange("c p w -> p c w"))

    def halfstore(dram, c0, width, srcv):
        return p.op("sync", "dma_start", inc="st", incn=16,
                    out=dram[:, :, c0:c0 + width].rearrange("c p w -> p c w"),
                    in_=srcv.rearrange("p (c w) -> p c w", w=width))

    def stats_and_coeffs(srcf, width, rows, fused_eps2, extra_cast_dep=()):
        p.wait("vector", g["xb_mm"])
        p.wait_all("vector", [g["slot"][0], g["slot"][1]])
        p.wait_all("vector", extra_cast_dep)
        tcast = p.op("vector", "tensor_copy", XB[:, 0:NCH * width], srcf,
                     inc="ve")
        g["slot"][0] = tcast
        g["slot"][1] = tcast
        p.wait_all("scalar", [g["slot"][2], g["slot"][3]])
        p.wait("scalar", g["xsq_mm"])
        p.wait_all("scalar", extra_cast_dep)
        tsq = p.op("scalar", "activation", XSQ[:, 0:NCH * width], srcf,
                   AF.Square, inc="sc")
        g["slot"][2] = tsq
        g["slot"][3] = tsq
        isx, bsx = ctx.banks.acquire(p)
        isq, bsq = ctx.banks.acquire(p)
        t1 = colsum_mms(ctx, bsx, XB[:, 0:NCH * width], width, dep=(tcast,))
        t2 = colsum_mms(ctx, bsq, XSQ[:, 0:NCH * width], width, dep=(tsq,))
        g["xb_mm"], g["xsq_mm"] = t1, t2
        done, tfree = ln_coeffs(ctx, bsx, bsq, rows, width, (t1, t2),
                                want_outer=True, fused_eps2=fused_eps2)
        ctx.banks.release(isx, tfree)
        ctx.banks.release(isq, tfree)
        return done, tcast, tsq

    p.label = "l1a.entry"
    xn_toks_all = []
    for n in range(NG):
        for hh in range(HPN):
            c0 = n * NQ + hh * NH
            p.wait_all("sync", g["xf_w"])
            tx = halfload(x_in, c0, NH, XF)
            p.wait("vector", tx)
            p.wait("scalar", tx)
            rows = {"mu": R_mu, "var": R_var, "tmp": R_tmp, "iv": R_iv,
                    "c": CROW[:, c0:c0 + NH], "nmc": NMCROW[:, c0:c0 + NH],
                    "ctot": R_ctot, "nmct": R_nmct}
            done, tcast, tsq = stats_and_coeffs(XF, NH, rows, 1e-6)
            ib1, bc1 = ctx.banks.acquire(p)
            ib2, bc2 = ctx.banks.acquire(p)
            ib3, bc3 = ctx.banks.acquire(p)
            ib4, bc4 = ctx.banks.acquire(p)
            tb1 = bcast_row(ctx, bc1, CROW[:, c0:c0 + NH], NH, dep=done)
            tb2 = bcast_row(ctx, bc2, NMCROW[:, c0:c0 + NH], NH, dep=done)
            tb3 = bcast_row(ctx, bc3, R_ctot, NH, dep=done)
            tb4 = bcast_row(ctx, bc4, R_nmct, NH, dep=done)
            p.wait_all("vector", [tb1, tb2, tb3, tb4] + g["sc1_w"])
            for cc in range(NCH):
                p.wait_all("vector", g["ao_w"][cc])
            last_l = None
            last_xn = None
            for cc in range(NCH):
                xc = XF[:, cc * NH:(cc + 1) * NH]
                s1 = SC1[:, cc * NH:(cc + 1) * NH]
                ac = AO[:, cc * NH:(cc + 1) * NH]
                p.op("vector", "tensor_mul", s1, xc, bc1[0:128, 0:NH])
                last_l = p.op("vector", "tensor_add", s1, s1,
                              bc2[0:128, 0:NH], inc="ve")
                p.op("vector", "tensor_mul", ac, xc, bc3[0:128, 0:NH])
                p.op("vector", "tensor_add", ac, ac, bc4[0:128, 0:NH])
                last_xn = p.op("vector", "tensor_scalar",
                               XN[:, cc * S + c0:cc * S + c0 + NH], ac,
                               wmix[:, cc:cc + 1], None, ALU.mult, inc="ve")
            p.wait("sync", last_l)
            tst_l = halfstore(ell, c0, NH, SC1)
            xn_toks_all.append(last_xn)
            g["xf_w"] = [last_l, last_xn, tsq, tcast]
            g["sc1_w"] = [tst_l]
            g["ao_w"] = [[last_xn] for _ in range(NCH)]
            for ii in (ib1, ib2, ib3, ib4):
                ctx.banks.release(ii, last_xn)

    p.label = "l1a.scan"
    qb = [ctx.banks.acquire(p) for _ in range(4)]
    p.wait_all("tensor", xn_toks_all)
    mix_w_tok = [None, None]
    prev_store = None
    prev_scan = None
    prev_sel = None
    prev_sq2 = None
    for m in range(NCH):
        slot = m % 2
        wload_cols(WM3[slot], "l1a_Wg", m * 128, 128, 8, wait=mix_w_tok[slot])
        wload_cols(WM3[slot][:, 1024:2048], "l1a_Wv", m * 128, 128, 8)
        tw = wload_cols(WM3[slot][:, 2048:3072], "l1a_Wd", m * 128, 128, 8)
        for qd in range(4):
            q0 = qd * NQ
            outs = []
            p.wait("tensor", tw)
            for wi in range(3):
                ib, bk = ctx.banks.acquire(p)
                tok = None
                for kc in range(NCH):
                    tok = p.op("tensor", "matmul", bk[0:128, 0:NQ],
                               WM3[slot][:, wi * 1024 + kc * 128:
                                         wi * 1024 + kc * 128 + 128],
                               XN[:, kc * S + q0:kc * S + q0 + NQ],
                               start=(kc == 0), stop=(kc == 7),
                               inc="mm" if kc == 7 else None)
                outs.append((ib, bk, tok))
            mix_w_tok[slot] = outs[2][2]
            ibg_, bkg, tokg = outs[0]
            ibv_, bkv, tokv = outs[1]
            ibd_, bkd, tokd = outs[2]
            p.wait("scalar", tokg)
            p.wait("scalar", prev_scan)
            tsg = p.op("scalar", "activation", XSA, bkg[0:128, 0:NQ],
                       AF.Sigmoid, bias=bg[:, m:m + 1], inc="sc")
            ctx.banks.release(ibg_, tsg)
            p.wait("scalar", tokv)
            ttv = p.op("scalar", "activation", TMP, bkv[0:128, 0:NQ],
                       AF.Tanh, bias=bv[:, m:m + 1], inc="sc")
            ctx.banks.release(ibv_, ttv)
            p.wait("vector", tsg)
            p.wait("vector", ttv)
            txs = p.op("vector", "tensor_mul", XSA, XSA, TMP, inc="ve")
            p.wait("scalar", tokd)
            p.wait("scalar", txs)
            tsd = p.op("scalar", "activation", TMP, bkd[0:128, 0:NQ],
                       AF.Sigmoid, bias=bd[:, m:m + 1], inc="sc")
            ctx.banks.release(ibd_, tsd)
            p.wait("vector", tsd)
            ta = p.op("vector", "tensor_scalar", TMP, TMP, 0.998, 0.001,
                      ALU.mult, ALU.add, inc="ve")
            p.wait("vector", ta)
            p.wait("vector", g.get("sco_r"))
            init = 0.0 if qd == 0 else SCST
            tsc = p.op("vector", "tensor_tensor_scan", SCO, TMP, XSA, init,
                       ALU.mult, ALU.add, inc="ve")
            p.wait("vector", tsc)
            tcar = p.op("vector", "tensor_copy", SCST, SCO[:, NQ - 1:NQ],
                        inc="ve")
            prev_scan = tcar
            p.wait("sync", prev_store)
            p.wait("sync", prev_sq2)
            tlm = p.op("sync", "dma_start", inc="ld", incn=16,
                       out=LM, in_=ell[m, :, q0:q0 + NQ])
            p.wait("gpsimd", tlm)
            p.wait("gpsimd", tsc)
            tl2 = p.op("gpsimd", "tensor_add", LM, LM, SCO, inc="gp")
            g["sco_r"] = tl2
            p.wait("sync", tl2)
            prev_store = p.op("sync", "dma_start", inc="st", incn=16,
                              out=ell[m, :, q0:q0 + NQ], in_=LM)
            p.wait("scalar", tl2)
            p.wait("scalar", prev_sel)
            tsq2 = p.op("scalar", "activation", SQT, LM, AF.Square, inc="sc")
            prev_sq2 = tsq2
            p.wait("tensor", tsq2)
            iq, bq = qb[qd]
            prev_sel = p.op("tensor", "matmul", bq[0:1, 0:NQ],
                            ctx.cbv("ones_col"), SQT,
                            start=(m == 0), stop=(m == 7), inc="mm")
    p.wait("vector", prev_sel)
    p.wait("vector", g.get("rows_r"))
    tq1 = None
    for qd in range(4):
        iq, bq = qb[qd]
        tq1 = p.op("vector", "tensor_scalar", RC2ROW[:, qd * NQ:(qd + 1) * NQ],
                   bq[0:1, 0:NQ], 1.0 / D, 1e-6, ALU.mult, ALU.add, inc="ve")
        ctx.banks.release(iq, tq1)
    p.wait("scalar", tq1)
    trc = p.op("scalar", "activation", RC2ROW, RC2ROW, AF.Ln, inc="sc")
    p.wait("scalar", trc)
    trc2 = p.op("scalar", "activation", RC2ROW, RC2ROW, AF.Exp, scale=-0.5,
                inc="sc")

    p.label = "l1a.yn"
    yn_toks = []
    xn_guard = ("mm", p.counts["mm"])
    for n in range(NG):
        for hh in range(HPN):
            c0 = n * NQ + hh * NH
            p.wait_all("sync", g["xf_w"])
            tx = halfload(ell, c0, NH, XF)
            ibc, bcq = ctx.banks.acquire(p)
            tbq = bcast_row(ctx, bcq, RC2ROW[:, c0:c0 + NH], NH, dep=(trc2,))
            p.wait_all("vector", [tx, tbq, xn_guard] + g["sc1_w"])
            last = None
            for cc in range(NCH):
                xc = XF[:, cc * NH:(cc + 1) * NH]
                s1 = SC1[:, cc * NH:(cc + 1) * NH]
                p.op("vector", "tensor_mul", s1, xc, bcq[0:128, 0:NH])
                last = p.op("vector", "tensor_scalar",
                            XN[:, cc * S + c0:cc * S + c0 + NH], s1,
                            wffn[:, cc:cc + 1], None, ALU.mult, inc="ve")
            yn_toks.append(last)
            ctx.banks.release(ibc, last)
            g["xf_w"] = [last, tx]
            g["sc1_w"] = [last]

    p.label = "l1a.ffn"
    p.wait_all("tensor", yn_toks)
    ff_w_tok = [None, None]
    wo_tok = [None, None]
    fo_guard = [[] for _ in range(NCH)]
    hid_guard = None
    tst = None
    for n in range(NG):
        nq0 = n * NQ
        hid_toks = []
        p.wait("vector", hid_guard)
        for f in range(24):
            slot = f % 2
            twf = wload_cols(WFF[slot], "l1a_Wgate", f * 128, 128, 8,
                             wait=ff_w_tok[slot])
            twf2 = wload_cols(WFF[slot][:, 1024:2048], "l1a_Wup",
                              f * 128, 128, 8)
            ibg_, bkg = ctx.banks.acquire(p)
            ibu_, bku = ctx.banks.acquire(p)
            p.wait("tensor", twf2)
            tokg = toku = None
            for kc in range(NCH):
                tokg = p.op("tensor", "matmul", bkg[0:128, 0:NQ],
                            WFF[slot][:, kc * 128:kc * 128 + 128],
                            XN[:, kc * S + nq0:kc * S + nq0 + NQ],
                            start=(kc == 0), stop=(kc == 7),
                            inc="mm" if kc == 7 else None)
            for kc in range(NCH):
                toku = p.op("tensor", "matmul", bku[0:128, 0:NQ],
                            WFF[slot][:, 1024 + kc * 128:1024 + kc * 128 + 128],
                            XN[:, kc * S + nq0:kc * S + nq0 + NQ],
                            start=(kc == 0), stop=(kc == 7),
                            inc="mm" if kc == 7 else None)
            ff_w_tok[slot] = toku
            p.wait("scalar", tokg)
            p.wait("scalar", g.get("sgb_r"))
            tsg = p.op("scalar", "activation", SGB, bkg[0:128, 0:NQ],
                       AF.Silu, inc="sc")
            ctx.banks.release(ibg_, tsg)
            p.wait("scalar", toku)
            tup = p.op("scalar", "activation", UPB, bku[0:128, 0:NQ],
                       AF.Copy, inc="sc")
            ctx.banks.release(ibu_, tup)
            p.wait("vector", tsg)
            p.wait("vector", tup)
            th = p.op("vector", "tensor_mul", HID[:, f * NQ:(f + 1) * NQ],
                      SGB, UPB, inc="ve")
            g["sgb_r"] = th
            hid_toks.append(th)
        p.wait_all("tensor", hid_toks)
        fo_toks = []
        for m in range(NCH):
            slot = m % 2
            two = wload_cols(WOS[slot], "l1a_Wout", m * 128, 128, 24,
                             wait=wo_tok[slot])
            ib, bk = ctx.banks.acquire(p)
            p.wait("tensor", two)
            tok = None
            for kc in range(24):
                tok = p.op("tensor", "matmul", bk[0:128, 0:NQ],
                           WOS[slot][:, kc * 128:kc * 128 + 128],
                           HID[:, kc * NQ:(kc + 1) * NQ],
                           start=(kc == 0), stop=(kc == 23),
                           inc="mm" if kc == 23 else None)
            wo_tok[slot] = tok
            p.wait("scalar", tok)
            p.wait_all("scalar", fo_guard[m])
            tev = p.op("scalar", "activation", FO[:, m * NQ:(m + 1) * NQ],
                       bk[0:128, 0:NQ], AF.Copy, inc="sc")
            fo_toks.append(tev)
            ctx.banks.release(ib, tev)
        hid_guard = tok
        for hh in range(HPN):
            c0 = nq0 + hh * NH
            p.wait_all("sync", g["xf_w"])
            tx2 = halfload(x_in, c0, NH, XF)
            for cc in range(NCH):
                p.wait_all("sync", g["ao_w"][cc])
            tl2 = halfload(ell, c0, NH, AO)
            p.wait_all("gpsimd", fo_toks + [tl2])
            tgs = []
            for cc in range(NCH):
                fslice = FO[:, cc * NQ + hh * NH:cc * NQ + hh * NH + NH]
                tg_ = p.op("gpsimd", "tensor_add", fslice, fslice,
                           AO[:, cc * NH:(cc + 1) * NH], inc="gp")
                tgs.append(tg_)
            rows = {"mu": R_mu, "var": R_var, "tmp": R_tmp, "iv": R_iv,
                    "c": R_c2, "nmc": R_nmc2}
            fof = FO.rearrange("p (c x) -> p c x", x=NQ)[:, :, hh * NH:hh * NH + NH]
            done, tcast, tsq = stats_and_coeffs(fof, NH, rows, None,
                                                extra_cast_dep=tgs)
            ibc1, bc1 = ctx.banks.acquire(p)
            ibc2, bc2 = ctx.banks.acquire(p)
            tb1 = bcast_row(ctx, bc1, R_c2, NH, dep=done)
            tb2 = bcast_row(ctx, bc2, R_nmc2, NH, dep=done)
            p.wait_all("vector", [tb1, tb2, tx2] + g["sc1_w"])
            xn_toks = []
            exit_tt = None
            for cc in range(NCH):
                foc = FO[:, cc * NQ + hh * NH:cc * NQ + hh * NH + NH]
                s1 = SC1[:, cc * NH:(cc + 1) * NH]
                xc = XF[:, cc * NH:(cc + 1) * NH]
                p.op("vector", "tensor_mul", s1, foc, bc1[0:128, 0:NH])
                exit_tt = p.op("vector", "tensor_add", s1, s1,
                               bc2[0:128, 0:NH], inc="ve")
                p.wait("gpsimd", exit_tt)
                tg_ = p.op("gpsimd", "tensor_add", s1, s1, xc, inc="gp")
                xn_toks.append(tg_)
            ctx.banks.release(ibc1, exit_tt)
            ctx.banks.release(ibc2, exit_tt)
            p.wait_all("sync", xn_toks)
            tst = p.op("sync", "dma_start", inc="st", incn=16,
                       out=x_out[:, :, c0:c0 + NH].rearrange("c p w -> p c w"),
                       in_=SC1.rearrange("p (c w) -> p c w", w=NH))
            g["xf_w"] = [xn_toks[-1], exit_tt, tcast, tsq]
            g["sc1_w"] = [tst] + xn_toks
            g["ao_w"] = [[tgs[cc]] for cc in range(NCH)]
            fo_guard = [[xn_toks[cc], tcast, tsq] for cc in range(NCH)]

    ba.pop(); fa.pop()
    return tst


def build_consts_l1a(c, inputs):
    c.add_f32("l1a_bg", col_chunks(inputs["L1_attn_bg"]))
    c.add_f32("l1a_bv", col_chunks(inputs["L1_attn_bv"]))
    c.add_f32("l1a_bd", col_chunks(inputs["L1_attn_bd"]))
    c.add_f32("l1a_wmix", col_chunks(inputs["L1_attn_w_rms_mix"]))
    c.add_f32("l1a_wffn", col_chunks(inputs["L1_attn_w_rms_ffn"]))


def host_weights_l1a(inputs):
    A = lambda a: np.ascontiguousarray(np.asarray(a, np.float32))
    return {
        "l1a_Wg": A(inputs["L1_attn_Wg"]),
        "l1a_Wv": A(inputs["L1_attn_Wv"]),
        "l1a_Wd": A(inputs["L1_attn_Wd"]),
        "l1a_Wgate": A(inputs["L1_attn_W_gate"]),
        "l1a_Wup": A(inputs["L1_attn_W_up"]),
        "l1a_Wout": A(inputs["L1_attn_W_out"]),
    }


L1A_WSHAPES = {"l1a_Wg": [1024, 1024], "l1a_Wv": [1024, 1024],
               "l1a_Wd": [1024, 1024], "l1a_Wgate": [1024, 3072],
               "l1a_Wup": [1024, 3072], "l1a_Wout": [3072, 1024]}



def emit_l1f(ctx, x_in, x_out, W):
    p = ctx.p
    ba, fa = ctx.ba, ctx.fa
    ba.push(); fa.push()
    cm, g, rows_ = ctx.cm, ctx.g, ctx.rows
    ell = ctx.ell

    HB = ba.alloc(NCH * NQ)
    WQ = [ba.alloc(NCH * 512) for _ in range(2)]
    VT2 = ba.alloc(16 * 520)
    qb2_off = ba.off
    QB2 = ba.alloc(4 * S)
    KB2 = ba.alloc(4 * S)
    HID2 = ba.t[:, qb2_off:qb2_off + 22 * NQ]
    XPT = ba.alloc(2 * NQ)
    AT2 = ba.alloc(4 * NQ)
    QRAW = ba.alloc(4 * NQ)
    KRAW = ba.alloc(4 * NQ)
    W1S = [ba.alloc(NCH * 128) for _ in range(4)]
    W2S = [ba.alloc(22 * 128) for _ in range(2)]
    WOS2 = [ba.alloc(4 * 128) for _ in range(2)]
    GEL = ba.alloc(NQ)
    ABF = ba.alloc(NQ)

    XW = fa.alloc(NCH * NQ)
    RNB = fa.alloc(NQ)
    HCAR = fa.alloc(8)

    XF, AO, SC1 = cm["XF"], cm["AO"], cm["SC1"]
    XB, XSQ, SQT = cm["XB"], cm["XSQ"], cm["SQT"]
    PT = cm["PT"]
    TIN, DEN, DENI = cm["TIN"], cm["DEN"], cm["DENI"]
    BCOL, SMV = cm["BCOL"], cm["SMV"]
    TBa, TBb, TBr = cm["TBa"], cm["TBb"], cm["TBr"]
    R_mu, R_var, R_tmp, R_iv = rows_["mu"], rows_["var"], rows_["tmp"], rows_["iv"]
    R_ctot, R_nmct = rows_["ctot"], rows_["nmct"]
    ident = ctx.cfv("ident_f32", w=128)
    identb = ctx.cbv("ident_bf", w=128)
    permrh = ctx.cbv("perm_rothalf", w=128)
    maskc = ctx.cbv("l1f_maskc", w=128)
    maskw = ctx.cbv("l1f_maskw", w=128)

    def wload_cols(dst, wname, li, c0, ncols, kchunks, wait=None):
        p.wait("sync", ctx.wtok.get(wname))
        p.wait("sync", wait)
        return p.op("sync", "dma_start", inc="ld", incn=16,
                    out=dst[:, 0:kchunks * ncols].rearrange(
                        "p (c m) -> p c m", m=ncols),
                    in_=W[wname][li][:, c0:c0 + ncols].rearrange(
                        "(c p) m -> p c m", p=128))

    def halfload(dram, c0, width, dst):
        return p.op("sync", "dma_start", inc="ld", incn=16,
                    out=dst.rearrange("p (c w) -> p c w", w=width),
                    in_=dram[:, :, c0:c0 + width].rearrange("c p w -> p c w"))

    def stats_and_coeffs(srcf, width, rows, fused_eps2, extra_cast_dep=()):
        p.wait("vector", g["xb_mm"])
        p.wait_all("vector", [g["slot"][0], g["slot"][1]])
        p.wait_all("vector", extra_cast_dep)
        tcast = p.op("vector", "tensor_copy", XB[:, 0:NCH * width], srcf,
                     inc="ve")
        g["slot"][0] = tcast
        g["slot"][1] = tcast
        p.wait_all("scalar", [g["slot"][2], g["slot"][3]])
        p.wait("scalar", g["xsq_mm"])
        p.wait_all("scalar", extra_cast_dep)
        tsq = p.op("scalar", "activation", XSQ[:, 0:NCH * width], srcf,
                   AF.Square, inc="sc")
        g["slot"][2] = tsq
        g["slot"][3] = tsq
        isx, bsx = ctx.banks.acquire(p)
        isq, bsq = ctx.banks.acquire(p)
        t1 = colsum_mms(ctx, bsx, XB[:, 0:NCH * width], width, dep=(tcast,))
        t2 = colsum_mms(ctx, bsq, XSQ[:, 0:NCH * width], width, dep=(tsq,))
        g["xb_mm"], g["xsq_mm"] = t1, t2
        done, tfree = ln_coeffs(ctx, bsx, bsq, rows, width, (t1, t2),
                                want_outer=True, fused_eps2=fused_eps2)
        ctx.banks.release(isx, tfree)
        ctx.banks.release(isq, tfree)
        return done, tcast, tsq

    p.label = "l1f.entry"
    for n in range(NG):
        for hh in range(HPN):
            c0 = n * NQ + hh * NH
            p.wait_all("sync", g["xf_w"])
            tx = halfload(x_in, c0, NH, XF)
            p.wait("vector", tx)
            p.wait("scalar", tx)
            rows = {"mu": R_mu, "var": R_var, "tmp": R_tmp, "iv": R_iv,
                    "c": R_ctot, "nmc": R_nmct}
            done, tcast, tsq = stats_and_coeffs(XF, NH, rows, None)
            ib1, bc1 = ctx.banks.acquire(p)
            ib2, bc2 = ctx.banks.acquire(p)
            tb1 = bcast_row(ctx, bc1, R_ctot, NH, dep=done)
            tb2 = bcast_row(ctx, bc2, R_nmct, NH, dep=done)
            p.wait_all("vector", [tb1, tb2] + g["sc1_w"])
            last_l = None
            for cc in range(NCH):
                xc = XF[:, cc * NH:(cc + 1) * NH]
                s1 = SC1[:, cc * NH:(cc + 1) * NH]
                p.op("vector", "tensor_mul", s1, xc, bc1[0:128, 0:NH])
                last_l = p.op("vector", "tensor_add", s1, s1,
                              bc2[0:128, 0:NH], inc="ve")
            p.wait("sync", last_l)
            tst_l = p.op("sync", "dma_start", inc="st", incn=16,
                         out=ell[:, :, c0:c0 + NH].rearrange("c p w -> p c w"),
                         in_=SC1.rearrange("p (c w) -> p c w", w=NH))
            g["xf_w"] = [last_l, tsq, tcast]
            g["sc1_w"] = [tst_l]
            ctx.banks.release(ib1, last_l)
            ctx.banks.release(ib2, last_l)

    tst = None
    for li in range(2):
        qs8 = ctx.cfv(f"l1f_qs{li}", w=1)
        ksc = ctx.cfv(f"l1f_ks{li}", w=1)
        lg = ctx.cfv(f"l1f_lg{li}", w=NCH)
        lb = ctx.cfv(f"l1f_lb{li}", w=NCH)
        b1c = ctx.cfv(f"l1f_b1_{li}", w=44)
        b2c = ctx.cfv(f"l1f_b2_{li}", w=NCH)

        p.wait("scalar", g["tin_s"]) if g.get("tin_s") else None
        p.wait("scalar", g["tin_r"])
        tq1 = p.op("scalar", "activation", TIN[0:128, 0:1], qs8, AF.Square,
                   inc="sc")
        it, btr = ctx.banks.acquire(p)
        p.wait("tensor", tq1)
        ttr = p.op("tensor", "transpose", btr[0:128, 0:128], TIN, ident,
                   inc="mm")
        p.wait("vector", ttr)
        tqm = p.op("vector", "reduce_max", SMV[0:1, 0:1], btr[0:1, 0:128],
                   axis=AX.X, inc="ve")
        ctx.banks.release(it, tqm)
        p.wait("scalar", tqm)
        tk1 = p.op("scalar", "activation", TIN[0:128, 0:1], ksc, AF.Square,
                   inc="sc")
        g["tin_s"] = tk1
        it, btr = ctx.banks.acquire(p)
        p.wait("tensor", tk1)
        ttr = p.op("tensor", "transpose", btr[0:128, 0:128], TIN, ident,
                   inc="mm")
        g["tin_r"] = ttr
        p.wait("vector", ttr)
        tkm = p.op("vector", "reduce_max", SMV[0:1, 1:2], btr[0:1, 0:128],
                   axis=AX.X, inc="ve")
        ctx.banks.release(it, tkm)
        p.wait("vector", tkm)
        tb2q = p.op("vector", "scalar_tensor_tensor", SMV[0:1, 2:3],
                    SMV[0:1, 0:1], 1.0, SMV[0:1, 1:2], ALU.mult, ALU.mult,
                    inc="ve")
        p.wait("scalar", tb2q)
        tlb = p.op("scalar", "activation", SMV[0:1, 3:4], SMV[0:1, 2:3],
                   AF.Ln, bias=ctx.cfv("epsc", w=5, rn=1)[:, 4:5], inc="sc")
        p.wait("scalar", tlb)
        tsb = p.op("scalar", "activation", SMV[0:1, 3:4], SMV[0:1, 3:4],
                   AF.Exp, scale=0.5, inc="sc")
        p.wait("vector", tsb)
        tnb = p.op("vector", "tensor_scalar_mul", SMV[0:1, 2:3],
                   SMV[0:1, 3:4], -1.0, inc="ve")
        ibb, bbb = ctx.banks.acquire(p)
        p.wait("tensor", tnb)
        tbm = p.op("tensor", "matmul", bbb[0:128, 0:1],
                   ctx.cfv("ones_row", w=128, rn=1), SMV[0:1, 2:3],
                   start=True, stop=True, inc="mm")
        p.wait("scalar", tbm)
        tbc = p.op("scalar", "activation", BCOL[0:128, 0:1], bbb[0:128, 0:1],
                   AF.Copy, inc="sc")
        ctx.banks.release(ibb, tbc)

        p.label = "l1f.qkv"
        p.wait("vector", g.get("vt2_r"))
        t_vones = p.op("vector", "memset", VT2[:, 64:16 * 520:65], 1.0,
                       inc="ve")
        hcar_z = p.op("vector", "memset", HCAR, 0.0, inc="ve")
        wq_tok = [None, None]
        qkv_last = None
        for n in range(NG):
            nq0 = n * NQ
            hb_toks = []
            for hh in range(HPN):
                c0 = nq0 + hh * NH
                p.wait_all("sync", g["xf_w"])
                tx = halfload(ell, c0, NH, XF)
                p.wait("vector", tx)
                p.wait("vector", g.get("hb_r"))
                p.wait("vector", hcar_z)
                tcp = None
                for cc in range(NCH):
                    dst0 = cc * NQ + hh * NH
                    if cc < 4:
                        tcp = p.op("vector", "tensor_copy",
                                   HB[:, dst0:dst0 + NH],
                                   XF[:, cc * NH:(cc + 1) * NH], inc="ve")
                    else:
                        hcv = HCAR[:, cc - 4:cc - 3]
                        tcp = p.op("vector", "tensor_copy",
                                   HB[:, dst0:dst0 + 1], hcv, inc="ve")
                        tcp = p.op("vector", "tensor_copy",
                                   HB[:, dst0 + 1:dst0 + NH],
                                   XF[:, cc * NH:(cc + 1) * NH - 1], inc="ve")
                        p.wait("vector", tcp)
                        tcp = p.op("vector", "tensor_copy", hcv,
                                   XF[:, (cc + 1) * NH - 1:(cc + 1) * NH],
                                   inc="ve")
                hb_toks.append(tcp)
                g["xf_w"] = [tcp, tx]
            p.wait_all("tensor", hb_toks)
            inq, bnq = ctx.banks.acquire(p)
            ink, bnk = ctx.banks.acquire(p)
            sel_tok = {"q": None, "k": None}
            for which, dstraw, bnx in (("q", QRAW, bnq), ("k", KRAW, bnk)):
                base_f = 0 if which == "q" else 512
                for mq in range(4):
                    slot = mq % 2
                    tw = wload_cols(WQ[slot], "l1f_wqkv", li,
                                    base_f + mq * 128, 128, 8,
                                    wait=wq_tok[slot])
                    ib, bk = ctx.banks.acquire(p)
                    p.wait("tensor", tw)
                    tok = None
                    for kc in range(NCH):
                        tok = p.op("tensor", "matmul", bk[0:128, 0:NQ],
                                   WQ[slot][:, kc * 128:kc * 128 + 128],
                                   HB[:, kc * NQ:(kc + 1) * NQ],
                                   start=(kc == 0), stop=(kc == 7),
                                   inc="mm" if kc == 7 else None)
                    wq_tok[slot] = tok
                    p.wait("scalar", tok)
                    p.wait_all("scalar", g.get(f"raw_r_{which}_{mq}", []))
                    tev = p.op("scalar", "activation",
                               dstraw[:, mq * NQ:(mq + 1) * NQ],
                               bk[0:128, 0:NQ], AF.Copy, inc="sc")
                    ctx.banks.release(ib, tev)
                    p.wait("scalar", g.get("sqt_r"))
                    tsq_ = p.op("scalar", "activation", SQT,
                                dstraw[:, mq * NQ:(mq + 1) * NQ], AF.Square,
                                inc="sc")
                    p.wait("tensor", tsq_)
                    sel_tok[which] = p.op(
                        "tensor", "matmul", bnx[0:16, 0:NQ],
                        ctx.cbv(f"sel_qk_{mq}", w=16), SQT,
                        start=(mq == 0), stop=(mq == 3), inc="mm")
                    g["sqt_r"] = sel_tok[which]
            p.label = "l1f.v"
            tw = wload_cols(WQ[0], "l1f_wqkv", li, 1024, 512, 8,
                            wait=wq_tok[0])
            vtok = None
            for tt4 in range(4):
                tt = n * 4 + tt4
                ib, bk = ctx.banks.acquire(p)
                p.wait("tensor", tw)
                tok = None
                for kc in range(NCH):
                    tok = p.op("tensor", "matmul", bk[0:128, 0:NQ],
                               HB[:, kc * NQ + tt4 * 128:
                                  kc * NQ + tt4 * 128 + 128],
                               WQ[0][:, kc * 512:(kc + 1) * 512],
                               start=(kc == 0), stop=(kc == 7),
                               inc="mm" if kc == 7 else None)
                vtok = tok
                p.wait("vector", tok)
                p.wait("vector", t_vones)
                base = tt * 520
                dst = VT2[:, base:base + 520].rearrange(
                    "p (h dd) -> p h dd", dd=65)[:, :, 0:64]
                tev = p.op("vector", "tensor_copy", dst,
                           bk[0:128, 0:NQ].rearrange("p (h d) -> p h d", d=64),
                           inc="ve")
                ctx.banks.release(ib, tev)
                g["vt2_last"] = tev
            wq_tok[0] = vtok
            tfin = None
            for which, bnx, inx in (("q", bnq, inq), ("k", bnk, ink)):
                p.wait("scalar", sel_tok[which])
                p.wait("scalar", g.get("rnb_r"))
                tsr = p.op("scalar", "activation", RNB[0:8, 0:NQ],
                           bnx[0:8, 0:NQ], AF.Ln,
                           bias=ctx.cfv("epsc", w=5, rn=8)[:, 3:4], inc="sc")
                ctx.banks.release(inx, tsr)
                p.wait("scalar", tsr)
                p.wait("scalar", g.get("deni8_r"))
                trc = p.op("scalar", "activation", DENI[0:8, 0:NQ],
                           RNB[0:8, 0:NQ], AF.Exp, scale=-0.5, inc="sc")
                g["rnb_r"] = trc
                sraw = QRAW if which == "q" else KRAW
                sdst = QB2 if which == "q" else KB2
                scol = qs8 if which == "q" else ksc
                r0 = 0 if which == "q" else 256
                p.wait("sync", g.get("xpt_r"))
                txp = p.op("sync", "dma_start", inc="ld", incn=16,
                           out=XPT.rearrange("p (c w) -> p c w", w=NQ),
                           in_=W["l1f_xpos"][r0:r0 + 256, nq0:nq0 + NQ].rearrange(
                               "(c p) w -> p c w", p=128))
                ctab = XPT[:, 0:NQ]
                stab = XPT[:, NQ:2 * NQ]
                for mq in range(4):
                    irn, brn = ctx.banks.acquire(p)
                    p.wait("tensor", trc)
                    trb = p.op("tensor", "matmul", brn[0:128, 0:NQ],
                               ctx.cfv(f"sel8_{mq}", w=128, rn=8),
                               DENI[0:8, 0:NQ],
                               start=True, stop=True, inc="mm")
                    g["deni8_r"] = trb
                    p.wait("vector", trb)
                    p.wait("vector", g.get("tb_r"))
                    p.wait("vector", txp)
                    p.op("vector", "tensor_mul", TBa,
                         sraw[:, mq * NQ:(mq + 1) * NQ], brn[0:128, 0:NQ])
                    t2_ = p.op("vector", "tensor_scalar", TBr, TBa,
                               scol, None, ALU.mult, inc="ve")
                    ctx.banks.release(irn, t2_)
                    ir, br = ctx.banks.acquire(p)
                    p.wait("tensor", t2_)
                    trot = p.op("tensor", "matmul", br[0:128, 0:NQ], permrh,
                                TBr, start=True, stop=True, inc="mm")
                    p.op("vector", "tensor_mul", TBa, TBr, ctab)
                    p.wait("vector", trot)
                    tb_ = p.op("vector", "tensor_mul", TBb,
                               br[0:128, 0:NQ], stab, inc="ve")
                    ctx.banks.release(ir, tb_)
                    g["tb_r"] = tb_
                    p.wait("vector", g.get("qb2_r"))
                    tfin = p.op("vector", "tensor_add",
                                sdst[:, mq * S + nq0:mq * S + nq0 + NQ],
                                TBa, TBb, inc="ve")
                    g[f"raw_r_{which}_{mq}"] = [tfin]
                g["xpt_r"] = tfin
            qkv_last = tfin
        g["hb_r"] = ("mm", p.counts["mm"])

        p.label = "l1f.attn"
        p.wait_all("tensor", [qkv_last, g["vt2_last"], tbc])
        at_guard = None
        for w in range(NG):
            nq0 = w * NQ
            at_toks = []
            for h in range(8):
                ch = h // 2
                base = (h % 2) * 64
                tp = (0, 0) if base == 0 else (64, 0)
                io_, obank = ctx.banks.acquire(p)
                m_order = [4, 5, 6, 7] + ([0, 1, 2, 3] if w > 0 else [])
                nm = len(m_order)
                for mi, m in enumerate(m_order):
                    jglob = 4 * (w - 1) + m
                    if m >= 4:
                        off = 128 * (m - 4)
                        wd = NQ - off
                        moff = off
                        msk = maskc
                    else:
                        off = 0
                        wd = 128 * (m + 1)
                        moff = wd - 128
                        msk = maskw
                    isc, bsc = ctx.banks.acquire(p)
                    p.op("tensor", "matmul", bsc[0:128, off:off + wd],
                         KB2[base:base + 64, ch * S + jglob * 128:
                             ch * S + jglob * 128 + 128],
                         QB2[base:base + 64, ch * S + nq0 + off:
                             ch * S + nq0 + off + wd],
                         start=True, stop=False, tile_position=tp)
                    tms = p.op("tensor", "matmul",
                               bsc[0:128, moff:moff + 128], identb, msk,
                               start=False, stop=True, inc="mm")
                    sl = g["ptrr"] = (g.get("ptrr") or 0) + 1
                    sl = sl % 4
                    p.wait("scalar", tms)
                    p.wait("scalar", g["slot"][sl])
                    p.wait("scalar", g["xb_mm"] if sl < 2 else g["xsq_mm"])
                    tex = p.op("scalar", "activation", PT[sl][:, off:off + wd],
                               bsc[0:128, off:off + wd], AF.Exp,
                               bias=BCOL[0:128, 0:1], inc="sc")
                    ctx.banks.release(isc, tex)
                    p.wait("tensor", tex)
                    tvv = p.op("tensor", "matmul", obank[0:65, off:off + wd],
                               VT2[:, jglob * 520 + h * 65:
                                   jglob * 520 + h * 65 + 65],
                               PT[sl][:, off:off + wd],
                               start=(mi == 0), stop=(mi == nm - 1), inc="mm")
                    g["slot"][sl] = tvv
                p.wait("scalar", tvv)
                p.wait_all("scalar", g["den_w"])
                td1 = p.op("scalar", "activation", DEN[0:1, 0:NQ],
                           obank[64:65, 0:NQ], AF.Copy, bias=1e-30, inc="sc")
                p.wait("scalar", g.get("deni_r"))
                tlx = p.op("scalar", "activation", DENI[0:1, 0:NQ],
                           DEN[0:1, 0:NQ], AF.Ln, inc="sc")
                p.wait("scalar", tlx)
                tri = p.op("scalar", "activation", DENI[0:1, 0:NQ],
                           DENI[0:1, 0:NQ], AF.Exp, scale=-1.0, inc="sc")
                ir2, brb = ctx.banks.acquire(p)
                p.wait("tensor", tri)
                trb = p.op("tensor", "matmul", brb[0:128, 0:NQ],
                           ctx.cfv("ones_row", w=128, rn=1), DENI[0:1, 0:NQ],
                           start=True, stop=True, inc="mm")
                g["deni_r"] = trb
                p.wait("vector", trb)
                trbe = p.op("vector", "tensor_copy", DEN, brb[0:128, 0:NQ],
                            inc="ve")
                ctx.banks.release(ir2, trbe)
                p.wait("vector", trbe)
                p.wait("vector", g.get("at2_r"))
                ta1 = p.op("vector", "tensor_mul",
                           AT2[base:base + 64, ch * NQ:(ch + 1) * NQ],
                           obank[0:64, 0:NQ], DEN[base:base + 64, 0:NQ],
                           inc="ve")
                ctx.banks.release(io_, [td1, ta1])
                g["den_w"] = [ta1, tri]
                at_toks.append(ta1)
            p.wait_all("sync", g.get("xw_w", []))
            txw = p.op("sync", "dma_start", inc="ld", incn=16,
                       out=XW.rearrange("p (c w) -> p c w", w=NQ),
                       in_=ell[:, :, nq0:nq0 + NQ].rearrange("c p w -> p c w"))
            p.wait_all("tensor", at_toks)
            wo_toks = []
            for m in range(NCH):
                slot = m % 2
                two = wload_cols(WOS2[slot], "l1f_wo", li, m * 128, 128, 4,
                                 wait=g.get(f"wos_{slot}"))
                ib, bk = ctx.banks.acquire(p)
                p.wait("tensor", two)
                tok = None
                for kc in range(4):
                    tok = p.op("tensor", "matmul", bk[0:128, 0:NQ],
                               WOS2[slot][:, kc * 128:kc * 128 + 128],
                               AT2[:, kc * NQ:(kc + 1) * NQ],
                               start=(kc == 0), stop=(kc == 3),
                               inc="mm" if kc == 3 else None)
                g[f"wos_{slot}"] = tok
                p.wait("vector", tok)
                p.wait("vector", txw)
                txx = p.op("vector", "tensor_add",
                           XW[:, m * NQ:(m + 1) * NQ],
                           XW[:, m * NQ:(m + 1) * NQ], bk[0:128, 0:NQ],
                           inc="ve")
                wo_toks.append(txx)
                ctx.banks.release(ib, txx)
            p.wait_all("sync", wo_toks)
            tso = p.op("sync", "dma_start", inc="st", incn=16,
                       out=ell[:, :, nq0:nq0 + NQ].rearrange("c p w -> p c w"),
                       in_=XW.rearrange("p (c w) -> p c w", w=NQ))
            g["xw_w"] = [tso] + wo_toks
            g["at2_r"] = ("mm", p.counts["mm"])
        g["qb2_r"] = ("mm", p.counts["mm"])
        g["vt2_r"] = ("mm", p.counts["mm"])
        g["hid2_r"] = g["qb2_r"]
        g["hb2_r"] = g["qb2_r"]

        p.label = "l1f.ffn"
        hcar_z2 = p.op("vector", "memset", HCAR[:, 4:8], 0.0, inc="ve")
        final = li == 1
        for n in range(NG):
            nq0 = n * NQ
            p.wait_all("sync", g.get("xw_w", []))
            txw = p.op("sync", "dma_start", inc="ld", incn=16,
                       out=XW.rearrange("p (c w) -> p c w", w=NQ),
                       in_=ell[:, :, nq0:nq0 + NQ].rearrange("c p w -> p c w"))
            yn_toks = []
            for hh in range(HPN):
                c0 = nq0 + hh * NH
                p.wait("vector", txw)
                p.wait("vector", hcar_z2)
                for cc in range(NCH):
                    p.wait_all("vector", g["ao_w"][cc])
                sh_toks = []
                tcp = None
                for cc in range(NCH):
                    xs_ = XW[:, cc * NQ + hh * NH:cc * NQ + hh * NH + NH]
                    dst = AO[:, cc * NH:(cc + 1) * NH]
                    if cc < 4:
                        tcp = p.op("vector", "tensor_copy", dst, xs_, inc="ve")
                    else:
                        hcv = HCAR[:, cc:cc + 1]
                        tcp = p.op("vector", "tensor_copy", dst[:, 0:1], hcv,
                                   inc="ve")
                        tcp = p.op("vector", "tensor_copy", dst[:, 1:NH],
                                   xs_[:, 0:NH - 1], inc="ve")
                        p.wait("vector", tcp)
                        tcp = p.op("vector", "tensor_copy", hcv,
                                   xs_[:, NH - 1:NH], inc="ve")
                    sh_toks.append(tcp)
                rows = {"mu": R_mu, "var": R_var, "tmp": R_tmp, "iv": R_iv,
                        "c": R_ctot, "nmc": R_nmct}
                done, tcast, tsq = stats_and_coeffs(AO, NH, rows, None,
                                                    extra_cast_dep=sh_toks)
                ib1, bc1 = ctx.banks.acquire(p)
                ib2, bc2 = ctx.banks.acquire(p)
                tb1 = bcast_row(ctx, bc1, R_ctot, NH, dep=done)
                tb2 = bcast_row(ctx, bc2, R_nmct, NH, dep=done)
                p.wait_all("vector", [tb1, tb2, g.get("hb2_r")])
                tyn = None
                for cc in range(NCH):
                    ac = AO[:, cc * NH:(cc + 1) * NH]
                    s1 = SC1[:, cc * NH:(cc + 1) * NH]
                    p.wait_all("vector", g["sc1_w"])
                    p.op("vector", "tensor_mul", s1, ac, bc1[0:128, 0:NH])
                    p.op("vector", "tensor_add", s1, s1, bc2[0:128, 0:NH])
                    tyn = p.op("vector", "tensor_scalar",
                               HB[:, cc * NQ + hh * NH:cc * NQ + hh * NH + NH],
                               s1, lg[:, cc:cc + 1], lb[:, cc:cc + 1],
                               ALU.mult, ALU.add, inc="ve")
                yn_toks.append(tyn)
                g["ao_w"] = [[tyn] for _ in range(NCH)]
                g["sc1_w"] = [tyn]
                ctx.banks.release(ib1, tyn)
                ctx.banks.release(ib2, tyn)
            p.wait_all("tensor", yn_toks)
            hid_toks = []
            p.wait("vector", g.get("hid2_r"))
            for ma in range(22):
                slot = ma % 2
                slg = 2 + slot
                twa = wload_cols(W1S[slot], "l1f_w1p", li, ma * 128, 128, 8,
                                 wait=g.get(f"w1s_{slot}"))
                twg = wload_cols(W1S[slg], "l1f_w1p", li, (22 + ma) * 128,
                                 128, 8, wait=g.get(f"w1s_{slg}"))
                ib1_, bka = ctx.banks.acquire(p)
                p.wait("tensor", twa)
                toka = None
                for kc in range(NCH):
                    toka = p.op("tensor", "matmul", bka[0:128, 0:NQ],
                                W1S[slot][:, kc * 128:kc * 128 + 128],
                                HB[:, kc * NQ:(kc + 1) * NQ],
                                start=(kc == 0), stop=(kc == 7),
                                inc="mm" if kc == 7 else None)
                g[f"w1s_{slot}"] = toka
                ib2_, bkg = ctx.banks.acquire(p)
                p.wait("tensor", twg)
                tokg = None
                for kc in range(NCH):
                    tokg = p.op("tensor", "matmul", bkg[0:128, 0:NQ],
                                W1S[slg][:, kc * 128:kc * 128 + 128],
                                HB[:, kc * NQ:(kc + 1) * NQ],
                                start=(kc == 0), stop=(kc == 7),
                                inc="mm" if kc == 7 else None)
                g[f"w1s_{slg}"] = tokg
                p.wait("vector", toka)
                p.wait("vector", g.get("gel_r"))
                tab = p.op("vector", "tensor_scalar", ABF, bka[0:128, 0:NQ],
                           b1c[:, ma:ma + 1], None, ALU.add, inc="ve")
                ctx.banks.release(ib1_, tab)
                p.wait("scalar", tokg)
                p.wait("scalar", g.get("gel_r"))
                tge = p.op("scalar", "activation", GEL, bkg[0:128, 0:NQ],
                           AF.Gelu, bias=b1c[:, 22 + ma:23 + ma], inc="sc")
                ctx.banks.release(ib2_, tge)
                p.wait("vector", tab)
                p.wait("vector", tge)
                th = p.op("vector", "tensor_mul",
                          HID2[:, ma * NQ:(ma + 1) * NQ], ABF, GEL, inc="ve")
                g["gel_r"] = th
                hid_toks.append(th)
            p.wait_all("tensor", hid_toks)
            w2_toks = []
            for m in range(NCH):
                slot = m % 2
                tw2 = wload_cols(W2S[slot], "l1f_w2p", li, m * 128, 128, 22,
                                 wait=g.get(f"w2s_{slot}"))
                ib, bk = ctx.banks.acquire(p)
                p.wait("tensor", tw2)
                tok = None
                for kc in range(22):
                    tok = p.op("tensor", "matmul", bk[0:128, 0:NQ],
                               W2S[slot][:, kc * 128:kc * 128 + 128],
                               HID2[:, kc * NQ:(kc + 1) * NQ],
                               start=(kc == 0), stop=(kc == 21),
                               inc="mm" if kc == 21 else None)
                g[f"w2s_{slot}"] = tok
                p.wait("vector", tok)
                txx = p.op("vector", "scalar_tensor_tensor",
                           XW[:, m * NQ:(m + 1) * NQ], bk[0:128, 0:NQ],
                           b2c[:, m:m + 1], XW[:, m * NQ:(m + 1) * NQ],
                           ALU.add, ALU.add, inc="ve")
                w2_toks.append(txx)
                ctx.banks.release(ib, txx)
            g["hid2_r"] = ("mm", p.counts["mm"])
            g["hb2_r"] = ("mm", p.counts["mm"])
            if not final:
                p.wait_all("sync", w2_toks)
                tso = p.op("sync", "dma_start", inc="st", incn=16,
                           out=ell[:, :, nq0:nq0 + NQ].rearrange(
                               "c p w -> p c w"),
                           in_=XW.rearrange("p (c w) -> p c w", w=NQ))
                g["xw_w"] = [tso] + w2_toks
            else:
                for hh in range(HPN):
                    c0 = nq0 + hh * NH
                    p.wait_all("sync", g["xf_w"])
                    tx2 = halfload(x_in, c0, NH, XF)
                    rows = {"mu": R_mu, "var": R_var, "tmp": R_tmp,
                            "iv": R_iv, "c": R_ctot, "nmc": R_nmct}
                    xwf = XW.rearrange("p (c x) -> p c x", x=NQ)[
                        :, :, hh * NH:hh * NH + NH]
                    done, tcast, tsq = stats_and_coeffs(
                        xwf, NH, rows, None, extra_cast_dep=w2_toks)
                    ib1, bc1 = ctx.banks.acquire(p)
                    ib2, bc2 = ctx.banks.acquire(p)
                    tb1 = bcast_row(ctx, bc1, R_ctot, NH, dep=done)
                    tb2 = bcast_row(ctx, bc2, R_nmct, NH, dep=done)
                    p.wait_all("vector", [tb1, tb2, tx2] + g["sc1_w"])
                    xn_toks = []
                    exit_tt = None
                    for cc in range(NCH):
                        xwc = XW[:, cc * NQ + hh * NH:cc * NQ + hh * NH + NH]
                        s1 = SC1[:, cc * NH:(cc + 1) * NH]
                        xc = XF[:, cc * NH:(cc + 1) * NH]
                        p.op("vector", "tensor_mul", s1, xwc,
                             bc1[0:128, 0:NH])
                        exit_tt = p.op("vector", "tensor_add", s1, s1,
                                       bc2[0:128, 0:NH], inc="ve")
                        p.wait("gpsimd", exit_tt)
                        tg_ = p.op("gpsimd", "tensor_add", s1, s1, xc,
                                   inc="gp")
                        xn_toks.append(tg_)
                    ctx.banks.release(ib1, exit_tt)
                    ctx.banks.release(ib2, exit_tt)
                    p.wait_all("sync", xn_toks)
                    tst = p.op("sync", "dma_start", inc="st", incn=16,
                               out=x_out[:, :, c0:c0 + NH].rearrange(
                                   "c p w -> p c w"),
                               in_=SC1.rearrange("p (c w) -> p c w", w=NH))
                    g["xf_w"] = [xn_toks[-1], exit_tt, tcast, tsq]
                    g["sc1_w"] = [tst] + xn_toks
                g["xw_w"] = w2_toks + [("ve", p.counts["ve"])]

        g["qb2_r"] = ("mm", p.counts["mm"])

    ba.pop(); fa.pop()
    return tst


def build_consts_l1f(c, inputs):
    P = np.zeros((128, 128), np.float32)
    for b in (0, 64):
        for dd in range(32):
            P[b + dd + 32, b + dd] = -1.0
            P[b + dd, b + dd + 32] = 1.0
    c.add_bf("perm_rothalf", P)
    mc = np.where(np.arange(128)[None, :] < np.arange(128)[:, None],
                  -1e9, 0.0).astype(np.float32)
    mw = np.where(np.arange(128)[None, :] > np.arange(128)[:, None],
                  -1e9, 0.0).astype(np.float32)
    c.add_bf("l1f_maskc", mc)
    c.add_bf("l1f_maskw", mw)
    for mq in range(4):
        s = np.zeros((128, 128), np.float32)
        s[2 * mq, 0:64] = 1.0
        s[2 * mq + 1, 64:128] = 1.0
        c.add_f32(f"sel8_{mq}", s)
    for li in range(2):
        qs = np.asarray(inputs["L1_ffn_q_scale"][li], np.float32)
        ks = np.asarray(inputs["L1_ffn_k_scale"][li], np.float32)
        c.add_f32(f"l1f_qs{li}", np.tile(qs, 2)[:, None] * 8.0)
        c.add_f32(f"l1f_ks{li}", np.tile(ks, 2)[:, None])
        c.add_f32(f"l1f_lg{li}", col_chunks(inputs["L1_ffn_ln_g"][li]))
        c.add_f32(f"l1f_lb{li}", col_chunks(inputs["L1_ffn_ln_b"][li]))
        b1 = np.asarray(inputs["L1_ffn_b1"][li], np.float32)
        b1p = np.zeros(44 * 128, np.float32)
        b1p[0:2730] = b1[0:2730]
        b1p[22 * 128:22 * 128 + 2730] = b1[2730:5460]
        c.add_f32(f"l1f_b1_{li}", col_chunks(b1p))
        c.add_f32(f"l1f_b2_{li}", col_chunks(inputs["L1_ffn_b2"][li]))


def xpos_tables_host():
    dim_head, window, n = 64, 512, S
    half = np.arange(0, dim_head, 2, dtype=np.float32)
    inv_freq = 1.0 / 10000.0 ** (half / dim_head)
    t = np.arange(n, dtype=np.float32)
    fr = inv_freq[:, None] * t[None, :]
    freqs = np.concatenate([fr, fr], axis=0)
    base = (half + 0.4 * dim_head) / (1.4 * dim_head)
    power = (t - n // 2) / float(window // 2)
    scale = base[:, None] ** power[None, :]
    scale = np.concatenate([scale, scale], axis=0)
    cos, sin = np.cos(freqs), np.sin(freqs)
    Cq = cos * scale
    Sq = sin * scale
    Ck = cos / scale
    Sk = sin / scale
    out = np.concatenate([np.tile(T, (2, 1)) for T in (Cq, Sq, Ck, Sk)],
                         axis=0)
    return np.ascontiguousarray(out)


def host_weights_l1f(inputs):
    A = lambda a: np.ascontiguousarray(np.asarray(a, np.float32))
    w1 = np.asarray(inputs["L1_ffn_w1"], np.float32)
    w1p = np.zeros((2, 1024, 44 * 128), np.float32)
    w1p[:, :, 0:2730] = w1[:, :, 0:2730]
    w1p[:, :, 22 * 128:22 * 128 + 2730] = w1[:, :, 2730:5460]
    w2 = np.asarray(inputs["L1_ffn_w2"], np.float32)
    w2p = np.zeros((2, 22 * 128, 1024), np.float32)
    w2p[:, 0:2730, :] = w2
    return {
        "l1f_wqkv": A(inputs["L1_ffn_wqkv"]),
        "l1f_wo": A(inputs["L1_ffn_wo"]),
        "l1f_w1p": np.ascontiguousarray(w1p),
        "l1f_w2p": np.ascontiguousarray(w2p),
        "l1f_xpos": xpos_tables_host(),
    }


L1F_WSHAPES = {"l1f_wqkv": [2, 1024, 1536], "l1f_wo": [2, 512, 1024],
               "l1f_w1p": [2, 1024, 5632], "l1f_w2p": [2, 2816, 1024],
               "l1f_xpos": [512, 2048]}



def host_weights_l0a(inputs):
    T = lambda a: np.ascontiguousarray(np.asarray(a, np.float32).T)
    return {
        "wdqT": T(inputs["L0_attn_w_down_q"]),
        "wdkvT": T(inputs["L0_attn_w_down_kv"]),
        "wupqT": T(inputs["L0_attn_w_up_q"]),
        "wupkT": T(inputs["L0_attn_w_up_k"]),
        "wupvT": T(inputs["L0_attn_w_up_v"]),
        "woutT": T(inputs["L0_attn_w_out"]),
    }


def declare_weights(nc, wdict):
    out = {}
    for k, v in wdict.items():
        out[k] = nc.dram_tensor(k, list(v.shape), F32, kind="ExternalInput").ap()
    return out


BLOCK_WSHAPES = {
    "l0a": {"wdqT": [1024, 512], "wdkvT": [1024, 512],
            "wupqT": [512, 1024], "wupkT": [512, 1024],
            "wupvT": [512, 1024], "woutT": [1024, 1024]},
    "l0f": L0F_WSHAPES,
}

BLOCK_EMIT = {}


def _register_blocks():
    BLOCK_EMIT["l0a"] = emit_l0a
    BLOCK_EMIT["l0f"] = emit_l0f
    BLOCK_EMIT["l1a"] = emit_l1a
    BLOCK_WSHAPES["l1a"] = L1A_WSHAPES
    BLOCK_EMIT["l1f"] = emit_l1f
    BLOCK_WSHAPES["l1f"] = L1F_WSHAPES


def build_chain_program(consts, blocks, nrows=NROWS, dbg_shapes=None):
    _register_blocks()
    nc = bass.Bass("TRN2", target_bir_lowering=False)
    ctx = Ctx(nc, consts)
    if dbg_shapes:
        ctx.dbgmap = {k: nc.dram_tensor("dbg_" + k, v[0], v[1],
                                        kind="ExternalOutput").ap()
                      for k, v in dbg_shapes.items()}
    shape = [nrows, NCH, 128, S]
    x_in = nc.dram_tensor("x_in", shape, F32, kind="ExternalInput").ap()
    x_out = nc.dram_tensor("x_out", shape, F32, kind="ExternalOutput").ap()
    ios = [x_in]
    for i in range(len(blocks) - 1):
        ios.append(nc.dram_tensor(f"x_mid{i}", shape, F32, kind="Internal").ap())
    ios.append(x_out)
    ctx.ell = nc.dram_tensor("ell_scr", [NCH, 128, S], F32, kind="Internal").ap()
    W = {}
    ctx.wtok = {}
    worder = []
    for b in blocks:
        for k, v in BLOCK_WSHAPES[b].items():
            if k not in W:
                wf = nc.dram_tensor(k, v, F32, kind="ExternalInput").ap()
                wb = nc.dram_tensor(k + "__bf", v, BF16, kind="Internal").ap()
                W[k] = wb
                worder.append((k, wf, wb))
    ctx.load_consts()
    for k, wf, wb in worder:
        ctx.wtok[k] = ctx.p.op("gpsimd", "dma_start", inc="gd", incn=16,
                               out=wb, in_=wf)
    for r in range(nrows):
        for bi, b in enumerate(blocks):
            BLOCK_EMIT[b](ctx, ios[bi][r], ios[bi + 1][r], W)
            ctx.barrier()
    ctx.p.wait("sync", ("st", ctx.p.counts["st"]))
    ctx.p.build()
    return nc


def build_l0a_program(consts, nrows=NROWS):
    return build_chain_program(consts, ["l0a"], nrows)


def to_fm(x_rows):
    r = x_rows.shape[0]
    return np.ascontiguousarray(
        x_rows.transpose(0, 2, 1).reshape(r, NCH, 128, S))


def from_fm(x_fm):
    r = x_fm.shape[0]
    return np.ascontiguousarray(
        x_fm.reshape(r, D, S).transpose(0, 2, 1))


ALL_BLOCKS = ["l0a", "l0f", "l1a", "l1f"]


def kernel(**inputs):
    consts = Consts()
    build_consts_common(consts)
    build_consts_l0a(consts, inputs)
    build_consts_l0f(consts)
    build_consts_l1a(consts, inputs)
    build_consts_l1f(consts, inputs)
    cf, cb = consts.finish()
    wmap = {}
    wmap.update(host_weights_l0a(inputs))
    wmap.update(host_weights_l0f(inputs))
    wmap.update(host_weights_l1a(inputs))
    wmap.update(host_weights_l1f(inputs))

    x = np.asarray(inputs["L0_attn_x"], np.float32)
    x_fm = to_fm(x)

    nc = build_chain_program(consts, ALL_BLOCKS, nrows=NROWS)
    in_maps = []
    for c in range(NCORES):
        m = dict(wmap)
        m["c_f32"] = cf
        m["c_bf"] = cb
        m["x_in"] = np.ascontiguousarray(x_fm[c * NROWS:(c + 1) * NROWS])
        in_maps.append(m)
    res = run_bass_kernel_spmd(nc, in_maps, core_ids=list(range(NCORES)))
    out = np.concatenate([from_fm(r["x_out"]) for r in res.results], axis=0)
    return np.ascontiguousarray(out.astype(np.float32))


# version 05
# speedup vs baseline: 1.3631x; 1.0220x over previous
import numpy as np
import ml_dtypes
import concourse.bass as bass
import concourse.mybir as mybir
from concourse.bass_utils import run_bass_kernel_spmd

F32 = mybir.dt.float32
BF16 = mybir.dt.bfloat16
AF = mybir.ActivationFunctionType
ALU = mybir.AluOpType
AX = mybir.AxisListType
BF = ml_dtypes.bfloat16

NCORES = 8
NROWS = 4
S = 2048
D = 1024
NCH = 8
NG = 4
NQ = 512
NH = 128
HPN = NQ // NH
EPS1 = 1e-5


ENGINES = ("tensor", "vector", "scalar", "gpsimd", "sync")


class Plan:
    def __init__(self, nc):
        self.nc = nc
        self.ops = {e: [] for e in ENGINES}
        self.sems = {}
        self.counts = {}
        self.waited = {e: {} for e in ENGINES}
        self.label = ""
        self.labels = {e: [] for e in ENGINES}

    def _sem(self, name):
        if name not in self.sems:
            self.sems[name] = self.nc.alloc_semaphore(name)
            self.counts[name] = 0
        return self.sems[name]

    def op(self, eng, meth, *args, inc=None, incn=1, **kwargs):
        tok = None
        if inc is not None:
            self._sem(inc)
            self.counts[inc] += incn
            tok = (inc, self.counts[inc])
        self.ops[eng].append((meth, args, kwargs, inc, incn))
        self.labels[eng].append(self.label)
        return tok

    def wait(self, eng, tok):
        if tok is None:
            return
        sname, v = tok
        self._sem(sname)
        if self.waited[eng].get(sname, 0) >= v:
            return
        self.waited[eng][sname] = v
        self.ops[eng].append(("wait_ge", (sname, v), {}, None, 0))

    def wait_all(self, eng, toks):
        for t in toks:
            self.wait(eng, t)

    def build(self):
        plan = self

        def runner(engname):
            def run(e):
                for meth, args, kwargs, inc, incn in plan.ops[engname]:
                    if meth == "wait_ge":
                        e.wait_ge(plan.sems[args[0]], args[1])
                        continue
                    inst = getattr(e, meth)(*args, **kwargs)
                    if inc is not None:
                        inst.then_inc(plan.sems[inc], incn)
            return run

        with plan.nc.Block() as block:
            block.tensor(runner("tensor"))
            block.vector(runner("vector"))
            block.scalar(runner("scalar"))
            block.gpsimd(runner("gpsimd"))
            block.sync(runner("sync"))


class Banks:
    def __init__(self, nc, n=8):
        self.banks = [nc.alloc_psum_tensor(f"bank{i}", [128, 512], F32)
                      for i in range(n)]
        self.free_tok = [None] * n
        self.order = list(range(n))

    def acquire(self, p):
        i = self.order.pop(0)
        ft = self.free_tok[i]
        if ft is not None:
            for t in (ft if isinstance(ft, list) else [ft]):
                p.wait("tensor", t)
        return i, self.banks[i]

    def release(self, i, consumer_tok):
        self.free_tok[i] = consumer_tok
        self.order.append(i)


class Arena:
    def __init__(self, nc, name, cols, dtype):
        self.t = nc.alloc_sbuf_tensor(name, [128, cols], dtype)
        self.name = name
        self.cols = cols
        self.off = 0
        self.marks = []

    def alloc(self, w):
        assert self.off + w <= self.cols, \
            f"Arena {self.name} overflow: {self.off}+{w} > {self.cols}"
        v = self.t[:, self.off:self.off + w]
        self.off += w
        return v

    def push(self):
        self.marks.append(self.off)

    def pop(self):
        self.off = self.marks.pop()



def rope_tables_l0(s, d_rope):
    inv_freq = 1.0 / 10000.0 ** (np.arange(0, d_rope, 2, dtype=np.float32) / d_rope)
    half = d_rope // 2
    idx = [(2 * i) % half for i in range(half)]
    t = np.arange(s, dtype=np.float32)
    fr = np.outer(inv_freq[idx], t)
    c, sn = np.cos(fr), np.sin(fr)
    C = np.empty((d_rope, s), np.float32)
    Sg = np.empty((d_rope, s), np.float32)
    C[0::2] = c; C[1::2] = c
    Sg[0::2] = -sn; Sg[1::2] = sn
    return C, Sg


def col_chunks(vec, pad_to=None):
    vec = np.asarray(vec, np.float32).ravel()
    n = vec.shape[0]
    if pad_to is not None and n < pad_to:
        vec = np.concatenate([vec, np.zeros(pad_to - n, np.float32)])
        n = pad_to
    assert n % 128 == 0
    return np.ascontiguousarray(vec.reshape(n // 128, 128).T)


class Consts:
    def __init__(self):
        self.f32, self.bfl = [], []
        self.f32_off, self.bf_off = {}, {}
        self.nf, self.nb = 0, 0

    def add_f32(self, name, arr):
        arr = np.asarray(arr, np.float32)
        assert arr.shape[0] == 128, name
        self.f32_off[name] = self.nf
        self.nf += arr.shape[1]
        self.f32.append(arr)

    def add_bf(self, name, arr):
        arr = np.asarray(arr, np.float32)
        assert arr.shape[0] == 128, name
        self.bf_off[name] = self.nb
        self.nb += arr.shape[1]
        self.bfl.append(arr.astype(BF))

    def finish(self):
        cf = np.ascontiguousarray(np.concatenate(self.f32, axis=1))
        cb = np.ascontiguousarray(
            np.concatenate(self.bfl, axis=1).astype(np.float32))
        return cf, cb


def build_consts_common(c):
    eps = np.array([1e-5, 1e-5 + 1e-5 * 1e-5, 1e-5 + 1e-5 * 1e-6,
                    1e-24, 1e-38], np.float32)
    c.add_f32("epsc", np.tile(eps[None, :], (128, 1)))
    c.add_bf("ones_col", np.ones((128, 1), np.float32))
    oner = np.zeros((128, 128), np.float32)
    oner[0, :] = 1.0
    oner[32, :] = 1.0
    oner[64, :] = 1.0
    c.add_f32("ones_row", oner)
    c.add_f32("ident_f32", np.eye(128, dtype=np.float32))
    c.add_bf("ident_bf", np.eye(128, dtype=np.float32))
    seld = np.zeros((128, 128), np.float32)
    seld[0, 0:64] = 1.0
    seld[64, 64:128] = 1.0
    c.add_f32("sel_den", seld)


def build_consts_l0a(c, inputs):
    c.add_f32("l0a_gamma", col_chunks(inputs["L0_attn_ln_scale"]))
    c.add_f32("l0a_beta", col_chunks(inputs["L0_attn_ln_bias"]))
    C0, S0 = rope_tables_l0(S, 64)
    c.add_bf("l0_ropeC", np.concatenate([C0, C0], axis=0))
    c.add_bf("l0_ropeS", np.concatenate([S0, S0], axis=0))
    P = np.zeros((128, 128), np.float32)
    for i in range(64):
        P[2 * i + 1, 2 * i] = 1.0
        P[2 * i, 2 * i + 1] = 1.0
    c.add_bf("perm_pairswap", P)
    for m in range(8):
        sel = np.zeros((128, 16), np.float32)
        sel[0:64, 2 * m] = 1.0
        sel[64:128, 2 * m + 1] = 1.0
        c.add_bf(f"sel_qk_{m}", sel)



class Ctx:
    def __init__(self, nc, consts, fa_cols=16064, ba_cols=65792):
        self.nc = nc
        self.p = Plan(nc)
        self.banks = Banks(nc)
        self.consts = consts
        self.cf_dram = nc.dram_tensor("c_f32", [128, consts.nf], F32,
                                      kind="ExternalInput").ap()
        self.cb_dram = nc.dram_tensor("c_bf", [128, consts.nb], F32,
                                      kind="ExternalInput").ap()
        self.cf = nc.alloc_sbuf_tensor("cf", [128, consts.nf], F32)
        self.cb = nc.alloc_sbuf_tensor("cb", [128, consts.nb], BF16)
        self.fa = Arena(nc, "fa", fa_cols, F32)
        self.ba = Arena(nc, "ba", ba_cols, BF16)
        self.dum = self.fa.alloc(4)
        fa, ba = self.fa, self.ba
        cm = {}
        cm["CH"] = [fa.alloc(NCH * NH) for _ in range(5)]
        cm["TIN"] = fa.alloc(128)
        cm["DEN"] = fa.alloc(NQ)
        cm["DENI"] = fa.alloc(NQ)
        cm["BROW"] = fa.alloc(16)
        cm["BCOL"] = fa.alloc(16)
        cm["SMV"] = fa.alloc(16)
        cm["SMB"] = fa.alloc(2 * S + 8 * NH)
        pt_off = ba.off
        cm["PT"] = [ba.alloc(NQ) for _ in range(4)]
        cm["XB"] = ba.t[:, pt_off:pt_off + NCH * NH]
        cm["XSQ"] = ba.t[:, pt_off + NCH * NH:pt_off + 2 * NCH * NH]
        cm["TBa"] = ba.alloc(NQ)
        cm["TBb"] = ba.alloc(NQ)
        cm["TBr"] = ba.alloc(NQ)
        cm["SQT"] = ba.alloc(NQ)
        self.cm = cm
        SMB = cm["SMB"]
        r = {}
        r["CROW"] = SMB[0:1, 0:S]
        r["NMCROW"] = SMB[0:1, S:2 * S]
        _r0 = 2 * S
        names9 = ("mu", "var", "tmp", "iv", "ctot", "nmct", "c2", "nmc2")
        self.row_sets = []
        for b in (0, 32, 64):
            rs = {"base": b}
            for i, nm in enumerate(names9):
                rs[nm] = SMB[b:b + 1, _r0 + i * NH:_r0 + (i + 1) * NH]
            rs["cw"] = rs["c2"]
            self.row_sets.append(rs)
        for i, nm in enumerate(names9):
            r[nm] = SMB[0:1, _r0 + i * NH:_r0 + (i + 1) * NH]
        self.rows = r
        self.rs_idx = 0
        self.g = {
            "rows_r": {0: None, 32: None, 64: None},
            "xb_mm": None, "xsq_mm": None,
            "tbr_rot": None, "tbr_ta": None,
            "den_w": [], "deni_r": None, "tin_r": None,
            "qn_r": None, "att_r": None,
            "slot": [None] * 4, "hf_r": None,
        }
        self.ch_guard = [[] for _ in range(5)]
        self.ch_idx = 0

    def take(self, eng):
        i = self.ch_idx % 5
        self.ch_idx += 1
        for t in self.ch_guard[i]:
            self.p.wait(eng, t)
        return i, self.cm["CH"][i]

    def give(self, i, toks):
        self.ch_guard[i] = [t for t in toks if t is not None]

    def take_rows(self):
        rs = self.row_sets[self.rs_idx % 3]
        self.rs_idx += 1
        for t in self.g["rows_r"].get(rs["base"]) or []:
            self.p.wait("vector", t)
            self.p.wait("scalar", t)
        return rs

    def give_rows(self, rs, toks):
        if not isinstance(toks, (list, tuple)):
            toks = [toks]
        self.g["rows_r"][rs["base"]] = [t for t in toks if t is not None]

    def load_consts(self):
        p = self.p
        t1 = p.op("sync", "dma_start", inc="ld", incn=16,
                  out=self.cf[:], in_=self.cf_dram)
        t2 = p.op("gpsimd", "dma_start", inc="gd", incn=16,
                  out=self.cb[:], in_=self.cb_dram)
        for e in ENGINES:
            p.wait(e, t1)
            p.wait(e, t2)

    def cfv(self, name, w=1, r0=0, rn=128):
        off = self.consts.f32_off[name]
        return self.cf[r0:r0 + rn, off:off + w]

    def cbv(self, name, w=1, r0=0, rn=128):
        off = self.consts.bf_off[name]
        return self.cb[r0:r0 + rn, off:off + w]

    dbgmap = None

    def dump(self, name, view):
        if not self.dbgmap or name not in self.dbgmap:
            return
        self.barrier()
        t = self.p.op("sync", "dma_start", inc="st", incn=16,
                      out=self.dbgmap[name], in_=view)
        self.p.wait("sync", t)
        self.barrier()

    def barrier(self):
        p = self.p
        toks = [
            p.op("vector", "memset", self.dum[0:1, 0:1], 0.0, inc="ve"),
            p.op("scalar", "memzero", self.dum[0:1, 1:2], inc="sc"),
            p.op("gpsimd", "memset", self.dum[0:1, 2:3], 0.0, inc="gp"),
        ]
        for sname in ("mm", "ld", "st", "gd"):
            if self.p.counts.get(sname, 0) > 0:
                toks.append((sname, self.p.counts[sname]))
        for e in ENGINES:
            p.wait_all(e, toks)


def colsum_mms(ctx, bank, src, n_w, dep=(), rows=1, lhsT=None, nch=NCH,
               r0=0):
    p = ctx.p
    if lhsT is None:
        lhsT = ctx.cbv("ones_col")
    p.wait_all("tensor", dep)
    tok = None
    for cc in range(nch):
        last = cc == nch - 1
        tok = p.op("tensor", "matmul", bank[r0:r0 + rows, 0:n_w], lhsT,
                   src[:, cc * n_w:(cc + 1) * n_w],
                   start=(cc == 0), stop=last, inc="mm" if last else None)
    return tok


def bcast_row(ctx, bank, row_view, n_w, m=128, dep=()):
    p = ctx.p
    p.wait_all("tensor", dep)
    base = getattr(row_view, "base_partition", None)
    if callable(base):
        base = base()
    if base is None:
        base = 0
    ones = ctx.cfv("ones_row", w=m, rn=1, r0=base)
    tok = p.op("tensor", "matmul", bank[0:m, 0:n_w], ones, row_view,
               start=True, stop=True, inc="mm")
    return tok


def ln_coeffs(ctx, b_sx, b_sq, rows, n_w, mm_toks, want_outer=True,
              fused_eps2=None):
    p = ctx.p
    v = "vector"
    p.wait_all(v, mm_toks)
    b = rows.get("base", 0)
    sx = b_sx[b:b + 1, 0:n_w]
    sq = b_sq[b:b + 1, 0:n_w]
    tmu = p.op(v, "tensor_scalar_mul", rows["mu"], sx, 1.0 / D, inc="ve")
    p.wait(v, tmu)
    t1 = p.op(v, "scalar_tensor_tensor", rows["tmp"], rows["mu"], -1.0,
              rows["mu"], ALU.mult, ALU.mult, inc="ve")
    p.wait(v, t1)
    tvar = p.op(v, "scalar_tensor_tensor", rows["var"], sq, 1.0 / D,
                rows["tmp"], ALU.mult, ALU.add, inc="ve")
    done = []
    free = [tmu, tvar]
    p.wait("scalar", tvar)
    epsc = ctx.cfv("epsc", w=5, rn=1)
    if want_outer:
        tl = p.op("scalar", "activation", rows["iv"], rows["var"], AF.Ln,
                  bias=epsc[:, 0:1], inc="sc")
        p.wait("scalar", tl)
        if rows.get("cross"):
            tc = p.op("scalar", "activation", rows["cw"], rows["iv"], AF.Exp,
                      scale=-0.5, inc="sc")
            p.wait(v, tc)
            tn = p.op(v, "scalar_tensor_tensor", rows["nmc"], rows["mu"],
                      -1.0, rows["cw"], ALU.mult, ALU.mult, inc="ve")
            p.wait(v, tn)
            tcp = p.op(v, "tensor_copy", rows["c"], rows["cw"], inc="ve")
            done += [tn, tcp]
        else:
            tc = p.op("scalar", "activation", rows["c"], rows["iv"], AF.Exp,
                      scale=-0.5, inc="sc")
            p.wait(v, tc)
            tn = p.op(v, "scalar_tensor_tensor", rows["nmc"], rows["mu"],
                      -1.0, rows["c"], ALU.mult, ALU.mult, inc="ve")
            done += [tc, tn]
    if fused_eps2 is not None:
        e2 = fused_eps2
        ecol = 1 if e2 == 1e-5 else 2
        tl = p.op("scalar", "activation", rows["tmp"], rows["var"], AF.Ln,
                  scale=1.0 + e2, bias=epsc[:, ecol:ecol + 1], inc="sc")
        p.wait("scalar", tl)
        tc = p.op("scalar", "activation", rows["ctot"], rows["tmp"], AF.Exp,
                  scale=-0.5, inc="sc")
        p.wait(v, tc)
        tn = p.op(v, "scalar_tensor_tensor", rows["nmct"], rows["mu"], -1.0,
                  rows["ctot"], ALU.mult, ALU.mult, inc="ve")
        done += [tc, tn]
    return done, free



def emit_l0a(ctx, x_in, x_out, W):
    p = ctx.p
    ba, fa = ctx.ba, ctx.fa
    ba.push(); fa.push()
    cm, g, rows_ = ctx.cm, ctx.g, ctx.rows

    HF = ba.alloc(NCH * NQ)
    KB = ba.alloc(NCH * S)
    _cq_off = ba.off
    CQ = ba.alloc(4 * NQ)
    CKV = ba.alloc(4 * NQ)
    ATT = ba.alloc(NCH * NQ)
    VT = ba.alloc(16 * 1040)
    W12 = ba.alloc(NCH * 1024)
    W4B = ba.alloc(4 * 1024)
    W5B = ba.alloc(4 * 1024)
    W1 = W12[:, 0:4096]
    W2 = W12[:, 4096:8192]
    W6 = W12
    W3 = HF
    QN = ba.t[:, _cq_off:_cq_off + NCH * NQ]
    CQL = ATT[:, 0:4 * NQ]

    PT = cm["PT"]
    TBa, TBb, TBr, SQT = cm["TBa"], cm["TBb"], cm["TBr"], cm["SQT"]
    XB, XSQ = cm["XB"], cm["XSQ"]
    TIN, DEN, DENI = cm["TIN"], cm["DEN"], cm["DENI"]
    BROW, BCOL, SMV = cm["BROW"], cm["BCOL"], cm["SMV"]
    CROW, NMCROW = rows_["CROW"], rows_["NMCROW"]
    R_mu, R_var, R_tmp, R_iv = rows_["mu"], rows_["var"], rows_["tmp"], rows_["iv"]
    R_ctot, R_nmct = rows_["ctot"], rows_["nmct"]
    R_c2, R_nmc2 = rows_["c2"], rows_["nmc2"]

    QMX = SMV[0:16, 0:1]
    KMX = SMV[0:16, 1:2]
    QSQ = SMV[0:16, 2:3]
    KSQ = SMV[0:16, 3:4]
    NB = SMV[0:16, 4:5]
    KMC = SMV[0:16, 5:9]

    gamma = ctx.cfv("l0a_gamma", w=NCH)
    beta = ctx.cfv("l0a_beta", w=NCH)
    ropeC = ctx.cbv("l0_ropeC", w=S)
    ropeS = ctx.cbv("l0_ropeS", w=S)
    perm = ctx.cbv("perm_pairswap", w=128)
    ident = ctx.cfv("ident_f32", w=128)

    def wload(dst, wname, mcols, wait=None):
        p.wait("sync", ctx.wtok.get(wname))
        p.wait("sync", wait)
        return p.op("sync", "dma_start", inc="ld", incn=16,
                    out=dst.rearrange("p (c m) -> p c m", m=mcols),
                    in_=W[wname].rearrange("(c p) m -> p c m", p=128))

    def halfload(dram, c0, width, dst):
        return p.op("sync", "dma_start", inc="ld", incn=16,
                    out=dst.rearrange("p (c w) -> p c w", w=width),
                    in_=dram[:, :, c0:c0 + width].rearrange("c p w -> p c w"))

    tw1 = wload(W1, "wdqT", 512)
    tw2 = wload(W2, "wdkvT", 512)
    tw4 = wload(W4B, "wupkT", 1024)
    tw5 = wload(W5B, "wupvT", 1024)
    t_vones = p.op("vector", "memset", VT[:, 64:16 * 1040:65], 1.0, inc="ve")
    t_tin0 = p.op("vector", "memset", TIN, 0.0, inc="ve")
    p.op("vector", "memset", DEN, 0.0)
    t_den0 = p.op("vector", "memset", DENI, 0.0, inc="ve")
    p.wait("scalar", t_den0)
    p.wait("tensor", t_den0)

    def stats_and_coeffs(srcf, width, rows, fused_eps2, extra_cast_dep=()):
        p.wait("vector", g["xb_mm"])
        p.wait_all("vector", [g["slot"][0], g["slot"][1]])
        p.wait_all("vector", extra_cast_dep)
        tcast = p.op("vector", "tensor_copy", XB[:, 0:NCH * width], srcf,
                     inc="ve")
        g["slot"][0] = tcast
        g["slot"][1] = tcast
        p.wait_all("scalar", [g["slot"][2], g["slot"][3]])
        p.wait("scalar", g["xsq_mm"])
        p.wait_all("scalar", extra_cast_dep)
        tsq = p.op("scalar", "activation", XSQ[:, 0:NCH * width], srcf,
                   AF.Square, inc="sc")
        g["slot"][2] = tsq
        g["slot"][3] = tsq
        isx, bsx = ctx.banks.acquire(p)
        isq, bsq = ctx.banks.acquire(p)
        t1 = colsum_mms(ctx, bsx, XB[:, 0:NCH * width], width, dep=(tcast,),
                        r0=rows.get("base", 0))
        t2 = colsum_mms(ctx, bsq, XSQ[:, 0:NCH * width], width, dep=(tsq,),
                        r0=rows.get("base", 0))
        g["xb_mm"], g["xsq_mm"] = t1, t2
        done, tfree = ln_coeffs(ctx, bsx, bsq, rows, width, (t1, t2),
                                want_outer=True, fused_eps2=fused_eps2)
        ctx.banks.release(isx, tfree)
        ctx.banks.release(isq, tfree)
        return done, tcast, tsq

    cq_st = None
    kb_last = None
    vt_last = None
    for n in range(NG):
        nq0 = n * NQ
        hf_half_toks = []
        for hh in range(HPN):
            c0 = nq0 + hh * NH
            ixf, XF = ctx.take("sync")
            tx = halfload(x_in, c0, NH, XF)
            p.wait("vector", tx)
            p.wait("scalar", tx)
            rs = ctx.take_rows()
            rows = dict(rs)
            rows["c"] = CROW[:, c0:c0 + NH]
            rows["nmc"] = NMCROW[:, c0:c0 + NH]
            rows["cross"] = rs["base"] != 0
            done, tcast, tsq = stats_and_coeffs(XF, NH, rows, 1e-5)
            ibc1, bc1 = ctx.banks.acquire(p)
            ibc2, bc2 = ctx.banks.acquire(p)
            tb1 = bcast_row(ctx, bc1, rows["ctot"], NH, dep=done)
            tb2 = bcast_row(ctx, bc2, rows["nmct"], NH, dep=done)
            ctx.give_rows(rs, [tb1, tb2])
            isc, SCv = ctx.take("vector")
            p.wait_all("vector", [tb1, tb2])
            last_tt = None
            th = None
            for cc in range(NCH):
                xc = XF[:, cc * NH:(cc + 1) * NH]
                s1 = SCv[:, cc * NH:(cc + 1) * NH]
                p.op("vector", "tensor_mul", s1, xc, bc1[0:128, 0:NH])
                last_tt = p.op("vector", "tensor_add", s1, s1,
                               bc2[0:128, 0:NH], inc="ve")
                p.wait("vector", g["hf_r"])
                p.wait("vector", last_tt)
                th = p.op("vector", "tensor_scalar",
                          HF[:, cc * NQ + hh * NH:cc * NQ + hh * NH + NH],
                          s1, gamma[:, cc:cc + 1], beta[:, cc:cc + 1],
                          ALU.mult, ALU.add, inc="ve")
            hf_half_toks.append(th)
            ctx.give(ixf, [last_tt, tcast, tsq])
            ctx.give(isc, [th])
            ctx.banks.release(ibc1, last_tt)
            ctx.banks.release(ibc2, last_tt)
        if n == 0:
            ctx.dump("hf0", HF)
        p.wait_all("tensor", hf_half_toks + [tw1, tw2])
        p.wait("tensor", g.get("cq_ld"))
        cq_ev = []
        for dst, wbuf in ((CQ, W1), (CKV, W2)):
            for m in range(4):
                ib, bk = ctx.banks.acquire(p)
                tok = None
                for kc in range(NCH):
                    tok = p.op("tensor", "matmul", bk[0:128, 0:NQ],
                               wbuf[:, kc * 512 + m * 128:kc * 512 + m * 128 + 128],
                               HF[:, kc * NQ:(kc + 1) * NQ],
                               start=(kc == 0), stop=(kc == 7),
                               inc="mm" if kc == 7 else None)
                p.wait("scalar", tok)
                p.wait("scalar", g.get("cq_r"))
                tev = p.op("scalar", "activation",
                           dst[:, m * NQ:(m + 1) * NQ],
                           bk[0:128, 0:NQ], AF.Copy, inc="sc")
                cq_ev.append(tev)
                ctx.banks.release(ib, tev)
        g["hf_r"] = ("mm", p.counts["mm"])
        p.wait_all("sync", cq_ev[0:4])
        p.wait("sync", g.get("cqs_r"))
        cq_st = p.op("sync", "dma_start", inc="st", incn=16,
                     out=ctx.cqs[n].rearrange("c p w -> p c w"),
                     in_=CQ.rearrange("p (c w) -> p c w", w=NQ))
        if n == 0:
            ctx.dump("cq0", CQ)
            ctx.dump("ckv0", CKV)
        p.wait("tensor", cq_ev[-1])
        p.wait("tensor", tw4)
        ink, bnk = ctx.banks.acquire(p)
        sel_mm = None
        for m in range(NCH):
            ib, bk = ctx.banks.acquire(p)
            tok = None
            for kc in range(4):
                tok = p.op("tensor", "matmul", bk[0:128, 0:NQ],
                           W4B[:, kc * 1024 + m * 128:kc * 1024 + m * 128 + 128],
                           CKV[:, kc * NQ:(kc + 1) * NQ],
                           start=(kc == 0), stop=(kc == 3),
                           inc="mm" if kc == 3 else None)
            p.wait("scalar", tok)
            p.wait("scalar", g["tbr_rot"])
            p.wait("scalar", g["tbr_ta"])
            tkraw = p.op("scalar", "activation", TBr, bk[0:128, 0:NQ],
                         AF.Copy, inc="sc")
            ctx.banks.release(ib, tkraw)
            ir, br = ctx.banks.acquire(p)
            p.wait("tensor", tkraw)
            trot = p.op("tensor", "matmul", br[0:128, 0:NQ], perm, TBr,
                        start=True, stop=True, inc="mm")
            g["tbr_rot"] = trot
            p.wait("vector", tkraw)
            ta = p.op("vector", "tensor_mul", TBa, TBr,
                      ropeC[:, nq0:nq0 + NQ], inc="ve")
            g["tbr_ta"] = ta
            p.wait("vector", trot)
            tb = p.op("vector", "tensor_mul", TBb, br[0:128, 0:NQ],
                      ropeS[:, nq0:nq0 + NQ], inc="ve")
            ctx.banks.release(ir, tb)
            p.wait("vector", ta)
            p.wait("vector", tb)
            kdst = KB[:, m * S + nq0:m * S + nq0 + NQ]
            kb_last = p.op("vector", "tensor_add", kdst, TBa, TBb, inc="ve")
            p.wait("scalar", kb_last)
            p.wait("scalar", sel_mm)
            p.wait("scalar", g.get("sqt_r"))
            tsqk = p.op("scalar", "activation", SQT, kdst, AF.Square,
                        inc="sc")
            p.wait("tensor", tsqk)
            sel_mm = p.op("tensor", "matmul", bnk[0:16, 0:NQ],
                          ctx.cbv(f"sel_qk_{m}", w=16), SQT,
                          start=(m == 0), stop=(m == 7), inc="mm")
            g["sqt_r"] = sel_mm
        p.wait("vector", sel_mm)
        tkmc = p.op("vector", "reduce_max", KMC[:, n:n + 1],
                    bnk[0:16, 0:NQ], axis=AX.X, inc="ve")
        ctx.banks.release(ink, tkmc)
        p.wait_all("tensor", [tw5, t_vones])
        for tt4 in range(4):
            tt = n * 4 + tt4
            for dn in range(2):
                ib, bk = ctx.banks.acquire(p)
                tok = None
                for kc in range(4):
                    tok = p.op("tensor", "matmul", bk[0:128, 0:NQ],
                               CKV[:, kc * NQ + tt4 * 128:kc * NQ + tt4 * 128 + 128],
                               W5B[:, kc * 1024 + dn * 512:kc * 1024 + dn * 512 + 512],
                               start=(kc == 0), stop=(kc == 3),
                               inc="mm" if kc == 3 else None)
                p.wait("vector", tok)
                base = tt * 1040 + dn * 520
                dst = VT[:, base:base + 520].rearrange(
                    "p (h dd) -> p h dd", dd=65)[:, :, 0:64]
                vt_last = p.op("vector", "tensor_copy", dst,
                               bk[0:128, 0:NQ].rearrange(
                                   "p (h d) -> p h d", d=64), inc="ve")
                ctx.banks.release(ib, vt_last)
        g["cq_r"] = ("mm", p.counts["mm"])
    ctx.dump("kb", KB)
    ctx.dump("vt", VT)
    p.wait("vector", tkmc)
    tkm = p.op("vector", "reduce_max", KMX, KMC, axis=AX.X, inc="ve")
    p.wait("vector", tkm)
    tks = p.op("vector", "tensor_copy", KSQ, KMX, inc="ve")

    tw6 = wload(W6, "woutT", 1024, wait=g["hf_r"])
    tw3 = wload(W3, "wupqT", 1024, wait=g["hf_r"])

    p.wait_all("tensor", [tw3, tw6])
    tst = None
    for n in range(NG):
        nq0 = n * NQ
        p.wait("sync", cq_st)
        p.wait("sync", g.get("att_r"))
        tcql = p.op("sync", "dma_start", inc="ld", incn=16,
                    out=CQL.rearrange("p (c w) -> p c w", w=NQ),
                    in_=ctx.cqs[n].rearrange("c p w -> p c w"))
        inq, bnq = ctx.banks.acquire(p)
        sel_mm = None
        qn_last = None
        p.wait("vector", g["qn_r"])
        p.wait("tensor", tcql)
        for m in range(NCH):
            ib, bk = ctx.banks.acquire(p)
            tok = None
            for kc in range(4):
                tok = p.op("tensor", "matmul", bk[0:128, 0:NQ],
                           W3[:, kc * 1024 + m * 128:kc * 1024 + m * 128 + 128],
                           CQL[:, kc * NQ:(kc + 1) * NQ],
                           start=(kc == 0), stop=(kc == 3),
                           inc="mm" if kc == 3 else None)
            p.wait("scalar", tok)
            p.wait("scalar", g["tbr_rot"])
            p.wait("scalar", g["tbr_ta"])
            tqraw = p.op("scalar", "activation", TBr, bk[0:128, 0:NQ],
                         AF.Copy, inc="sc")
            ctx.banks.release(ib, tqraw)
            ir, br = ctx.banks.acquire(p)
            p.wait("tensor", tqraw)
            trot = p.op("tensor", "matmul", br[0:128, 0:NQ], perm, TBr,
                        start=True, stop=True, inc="mm")
            g["tbr_rot"] = trot
            p.wait("vector", tqraw)
            ta = p.op("vector", "tensor_mul", TBa, TBr,
                      ropeC[:, nq0:nq0 + NQ], inc="ve")
            g["tbr_ta"] = ta
            p.wait("vector", trot)
            tb = p.op("vector", "tensor_mul", TBb, br[0:128, 0:NQ],
                      ropeS[:, nq0:nq0 + NQ], inc="ve")
            ctx.banks.release(ir, tb)
            p.wait("vector", ta)
            p.wait("vector", tb)
            qdst = QN[:, m * NQ:(m + 1) * NQ]
            qn_last = p.op("vector", "tensor_add", qdst, TBa, TBb, inc="ve")
            p.wait("scalar", qn_last)
            p.wait("scalar", sel_mm)
            p.wait("scalar", g.get("sqt_r"))
            tsqq = p.op("scalar", "activation", SQT, qdst, AF.Square,
                        inc="sc")
            p.wait("tensor", tsqq)
            sel_mm = p.op("tensor", "matmul", bnq[0:16, 0:NQ],
                          ctx.cbv(f"sel_qk_{m}", w=16), SQT,
                          start=(m == 0), stop=(m == 7), inc="mm")
            g["sqt_r"] = sel_mm
        if n == 0:
            ctx.dump("qn0", QN)
        p.wait("vector", sel_mm)
        tqm = p.op("vector", "reduce_max", QMX, bnq[0:16, 0:NQ], axis=AX.X,
                   inc="ve")
        ctx.banks.release(inq, tqm)
        p.wait("vector", tqm)
        p.wait("vector", tks)
        tqk2 = p.op("vector", "scalar_tensor_tensor", QSQ, QMX, 1.0, KSQ,
                    ALU.mult, ALU.mult, inc="ve")
        p.wait("scalar", tqk2)
        tln = p.op("scalar", "activation", NB, QSQ, AF.Ln,
                   bias=ctx.cfv("epsc", w=5, rn=16)[:, 4:5], inc="sc")
        p.wait("scalar", tln)
        tex = p.op("scalar", "activation", NB, NB, AF.Exp, scale=0.5,
                   inc="sc")
        p.wait("vector", tex)
        p.wait("vector", g["tin_r"])
        tnb = p.op("vector", "tensor_scalar_mul", NB, NB, -0.125, inc="ve")
        p.wait("vector", tnb)
        ttin = p.op("vector", "tensor_copy", TIN[0:16, 0:1], NB, inc="ve")
        it, btr = ctx.banks.acquire(p)
        p.wait("tensor", ttin)
        ttr = p.op("tensor", "transpose", btr[0:128, 0:128], TIN, ident,
                   inc="mm")
        g["tin_r"] = ttr
        p.wait("scalar", ttr)
        tbrow = p.op("scalar", "activation", BROW[0:1, 0:16],
                     btr[0:1, 0:16], AF.Copy, inc="sc")
        ctx.banks.release(it, tbrow)
        ibc, bbc = ctx.banks.acquire(p)
        tbc = bcast_row(ctx, bbc, BROW[0:1, 0:16], 16, dep=(tbrow,))
        p.wait("scalar", tbc)
        tbcol = p.op("scalar", "activation", BCOL, bbc[0:128, 0:16],
                     AF.Copy, inc="sc")
        ctx.banks.release(ibc, tbcol)

        if n == 0:
            ctx.dump("bcol0", BCOL)
        p.wait_all("tensor", [qn_last, kb_last, vt_last, tbcol])
        att_last = None
        for hp in range(8):
            ia, boa = ctx.banks.acquire(p)
            ib2, bob = ctx.banks.acquire(p)
            tva = tvb = None
            for j in range(16):
                isa, bsa = ctx.banks.acquire(p)
                isb, bsb = ctx.banks.acquire(p)
                kc0 = hp * S + j * 128
                tsa = p.op("tensor", "matmul", bsa[0:128, 0:NQ],
                           KB[0:64, kc0:kc0 + 128],
                           QN[0:64, hp * NQ:(hp + 1) * NQ],
                           start=True, stop=True, tile_position=(0, 0),
                           inc="mm")
                tsb = p.op("tensor", "matmul", bsb[0:128, 0:NQ],
                           KB[64:128, kc0:kc0 + 128],
                           QN[64:128, hp * NQ:(hp + 1) * NQ],
                           start=True, stop=True, tile_position=(64, 0),
                           inc="mm")
                sa = (2 * j) % 4
                sb = (2 * j + 1) % 4
                p.wait("scalar", tsa)
                p.wait("scalar", g["slot"][sa])
                p.wait("scalar", g["xb_mm"] if sa < 2 else g["xsq_mm"])
                tea = p.op("scalar", "activation", PT[sa],
                           bsa[0:128, 0:NQ], AF.Exp,
                           bias=BCOL[:, 2 * hp:2 * hp + 1], scale=0.125,
                           inc="sc")
                ctx.banks.release(isa, tea)
                p.wait("scalar", tsb)
                p.wait("scalar", g["slot"][sb])
                p.wait("scalar", g["xb_mm"] if sb < 2 else g["xsq_mm"])
                teb = p.op("scalar", "activation", PT[sb],
                           bsb[0:128, 0:NQ], AF.Exp,
                           bias=BCOL[:, 2 * hp + 1:2 * hp + 2], scale=0.125,
                           inc="sc")
                ctx.banks.release(isb, teb)
                p.wait("tensor", tea)
                tva = p.op("tensor", "matmul", boa[0:65, 0:NQ],
                           VT[:, j * 1040 + (2 * hp) * 65:
                              j * 1040 + (2 * hp) * 65 + 65],
                           PT[sa], start=(j == 0), stop=(j == 15), inc="mm")
                g["slot"][sa] = tva
                p.wait("tensor", teb)
                tvb = p.op("tensor", "matmul", bob[0:65, 0:NQ],
                           VT[:, j * 1040 + (2 * hp + 1) * 65:
                              j * 1040 + (2 * hp + 1) * 65 + 65],
                           PT[sb], start=(j == 0), stop=(j == 15), inc="mm")
                g["slot"][sb] = tvb
            p.wait("scalar", tva)
            p.wait("scalar", tvb)
            p.wait_all("scalar", g["den_w"])
            td1 = p.op("scalar", "activation", DEN[0:1, 0:NQ],
                       boa[64:65, 0:NQ], AF.Copy, bias=1e-30, inc="sc")
            td2 = p.op("scalar", "activation", DEN[64:65, 0:NQ],
                       bob[64:65, 0:NQ], AF.Copy, bias=1e-30, inc="sc")
            p.wait("scalar", g["deni_r"])
            p.wait("scalar", td2)
            p.op("scalar", "activation", DENI[0:1, 0:NQ], DEN[0:1, 0:NQ],
                 AF.Ln)
            tlx = p.op("scalar", "activation", DENI[64:65, 0:NQ],
                       DEN[64:65, 0:NQ], AF.Ln, inc="sc")
            p.wait("scalar", tlx)
            p.op("scalar", "activation", DENI[0:1, 0:NQ], DENI[0:1, 0:NQ],
                 AF.Exp, scale=-1.0)
            tri = p.op("scalar", "activation", DENI[64:65, 0:NQ],
                       DENI[64:65, 0:NQ], AF.Exp, scale=-1.0, inc="sc")
            g["den_w"] = [tri]
            ir2, brb = ctx.banks.acquire(p)
            p.wait("tensor", tri)
            trb = p.op("tensor", "matmul", brb[0:128, 0:NQ],
                       ctx.cfv("sel_den", w=128, rn=65), DENI[0:65, 0:NQ],
                       start=True, stop=True, inc="mm")
            g["deni_r"] = trb
            p.wait("vector", trb)
            p.wait_all("vector", g["den_w"])
            trbe = p.op("vector", "tensor_copy", DEN, brb[0:128, 0:NQ],
                        inc="ve")
            ctx.banks.release(ir2, trbe)
            p.wait("vector", trbe)
            p.wait("vector", g["att_r"])
            ta1 = p.op("vector", "tensor_mul",
                       ATT[0:64, hp * NQ:(hp + 1) * NQ],
                       boa[0:64, 0:NQ], DEN[0:64, :], inc="ve")
            ctx.banks.release(ia, [td1, ta1])
            ta2 = p.op("vector", "tensor_mul",
                       ATT[64:128, hp * NQ:(hp + 1) * NQ],
                       bob[0:64, 0:NQ], DEN[64:128, :], inc="ve")
            ctx.banks.release(ib2, [td2, ta2])
            g["den_w"] = [ta1, ta2, tri]
            att_last = ta2
        g["qn_r"] = ("mm", p.counts["mm"])

        if n == 0:
            ctx.dump("att0", ATT)
        p.wait("tensor", att_last)
        for hh in range(HPN):
            c0 = nq0 + hh * NH
            ixf, XF = ctx.take("sync")
            tx2 = halfload(x_in, c0, NH, XF)
            iao, AO = ctx.take("scalar")
            ao_toks = []
            for m in range(NCH):
                ib, bk = ctx.banks.acquire(p)
                tok = None
                for kc in range(NCH):
                    tok = p.op("tensor", "matmul", bk[0:128, 0:NH],
                               W6[:, kc * 1024 + m * 128:kc * 1024 + m * 128 + 128],
                               ATT[:, kc * NQ + hh * NH:kc * NQ + hh * NH + NH],
                               start=(kc == 0), stop=(kc == 7),
                               inc="mm" if kc == 7 else None)
                p.wait("scalar", tok)
                tev = p.op("scalar", "activation", AO[:, m * NH:(m + 1) * NH],
                           bk[0:128, 0:NH], AF.Copy, inc="sc")
                ao_toks.append(tev)
                ctx.banks.release(ib, tev)
            g["att_r"] = ("mm", p.counts["mm"])
            ibl1, bl1 = ctx.banks.acquire(p)
            ibl2, bl2 = ctx.banks.acquire(p)
            tb1 = bcast_row(ctx, bl1, CROW[:, c0:c0 + NH], NH)
            tb2 = bcast_row(ctx, bl2, NMCROW[:, c0:c0 + NH], NH)
            isc, SCv = ctx.take("vector")
            p.wait_all("vector", [tb1, tb2, tx2])
            ell_toks = []
            last_tt = None
            for cc in range(NCH):
                xc = XF[:, cc * NH:(cc + 1) * NH]
                s1 = SCv[:, cc * NH:(cc + 1) * NH]
                aoc = AO[:, cc * NH:(cc + 1) * NH]
                p.op("vector", "tensor_mul", s1, xc, bl1[0:128, 0:NH])
                last_tt = p.op("vector", "tensor_add", s1, s1,
                               bl2[0:128, 0:NH], inc="ve")
                p.wait("gpsimd", last_tt)
                p.wait("gpsimd", ao_toks[cc])
                tg = p.op("gpsimd", "tensor_add", aoc, aoc, s1, inc="gp")
                ell_toks.append(tg)
            ctx.banks.release(ibl1, last_tt)
            ctx.banks.release(ibl2, last_tt)
            rs = ctx.take_rows()
            rows = dict(rs)
            rows["c"] = rows["c2"]
            rows["nmc"] = rows["nmc2"]
            done, tcast, tsq = stats_and_coeffs(AO, NH, rows, None,
                                                extra_cast_dep=ell_toks)
            ibc1, bc1 = ctx.banks.acquire(p)
            ibc2, bc2 = ctx.banks.acquire(p)
            tb1 = bcast_row(ctx, bc1, rows["c2"], NH, dep=done)
            tb2 = bcast_row(ctx, bc2, rows["nmc2"], NH, dep=done)
            ctx.give_rows(rs, [tb1, tb2])
            p.wait_all("vector", [tb1, tb2])
            xn_toks = []
            exit_tt = None
            for cc in range(NCH):
                aoc = AO[:, cc * NH:(cc + 1) * NH]
                xc = XF[:, cc * NH:(cc + 1) * NH]
                p.op("vector", "tensor_mul", aoc, aoc, bc1[0:128, 0:NH])
                exit_tt = p.op("vector", "tensor_add", aoc, aoc,
                               bc2[0:128, 0:NH], inc="ve")
                p.wait("gpsimd", exit_tt)
                tg = p.op("gpsimd", "tensor_add", aoc, aoc, xc, inc="gp")
                xn_toks.append(tg)
            ctx.banks.release(ibc1, exit_tt)
            ctx.banks.release(ibc2, exit_tt)
            p.wait_all("sync", xn_toks)
            tst = p.op("sync", "dma_start", inc="st", incn=16,
                       out=x_out[:, :, c0:c0 + NH].rearrange("c p w -> p c w"),
                       in_=AO.rearrange("p (c w) -> p c w", w=NH))
            ctx.give(ixf, [xn_toks[-1], exit_tt, last_tt, tcast, tsq])
            ctx.give(iao, xn_toks + [tst, tcast, tsq])
            ctx.give(isc, ell_toks)
    g["cqs_r"] = tcql

    ba.pop(); fa.pop()
    return tst



def emit_l0f(ctx, x_in, x_out, W):
    p = ctx.p
    ba, fa = ctx.ba, ctx.fa
    ba.push(); fa.push()
    cm, g, rows_ = ctx.cm, ctx.g, ctx.rows

    HF = ba.alloc(NCH * S)
    WT = ba.alloc(NCH * 1024)
    WG = ba.alloc(NCH * 8)
    WU = ba.alloc(NCH * 128)
    WV = ba.alloc(1024)
    TM = ba.alloc(NH)
    GTB = ba.alloc(NH)

    XB, XSQ, SQT = cm["XB"], cm["XSQ"], cm["SQT"]
    TIN, DEN, DENI = cm["TIN"], cm["DEN"], cm["DENI"]
    BROW, BCOL, SMV = cm["BROW"], cm["BCOL"], cm["SMV"]
    R_mu, R_var, R_tmp, R_iv = rows_["mu"], rows_["var"], rows_["tmp"], rows_["iv"]
    R_ctot, R_nmct = rows_["ctot"], rows_["nmct"]
    R_c2, R_nmc2 = rows_["c2"], rows_["nmc2"]
    GROW = rows_["CROW"][0:1, 0:NH]
    ident = ctx.cfv("ident_f32", w=128)
    selg = ctx.cfv("selg", w=128, rn=8)

    def wload(dst, wname, mcols, wait=None):
        p.wait("sync", ctx.wtok.get(wname))
        p.wait("sync", wait)
        return p.op("sync", "dma_start", inc="ld", incn=16,
                    out=dst.rearrange("p (c m) -> p c m", m=mcols),
                    in_=W[wname].rearrange("(c p) m -> p c m", p=128))

    def halfload(dram, c0, width, dst):
        return p.op("sync", "dma_start", inc="ld", incn=16,
                    out=dst.rearrange("p (c w) -> p c w", w=width),
                    in_=dram[:, :, c0:c0 + width].rearrange("c p w -> p c w"))

    twt = wload(WT, "l0f_wT", 1024)
    twg = wload(WG, "l0f_gwT", 8)
    twu = wload(WU, "l0f_uT", 128)
    p.wait("sync", ctx.wtok.get("l0f_vall"))
    twv = p.op("sync", "dma_start", inc="ld", incn=16, out=WV,
               in_=W["l0f_vall"])

    def stats_and_coeffs(srcf, width, rows, fused_eps2, extra_cast_dep=()):
        p.wait("vector", g["xb_mm"])
        p.wait_all("vector", [g["slot"][0], g["slot"][1]])
        p.wait_all("vector", extra_cast_dep)
        tcast = p.op("vector", "tensor_copy", XB[:, 0:NCH * width], srcf,
                     inc="ve")
        g["slot"][0] = tcast
        g["slot"][1] = tcast
        p.wait_all("scalar", [g["slot"][2], g["slot"][3]])
        p.wait("scalar", g["xsq_mm"])
        p.wait_all("scalar", extra_cast_dep)
        tsq = p.op("scalar", "activation", XSQ[:, 0:NCH * width], srcf,
                   AF.Square, inc="sc")
        g["slot"][2] = tsq
        g["slot"][3] = tsq
        isx, bsx = ctx.banks.acquire(p)
        isq, bsq = ctx.banks.acquire(p)
        t1 = colsum_mms(ctx, bsx, XB[:, 0:NCH * width], width, dep=(tcast,),
                        r0=rows.get("base", 0))
        t2 = colsum_mms(ctx, bsq, XSQ[:, 0:NCH * width], width, dep=(tsq,),
                        r0=rows.get("base", 0))
        g["xb_mm"], g["xsq_mm"] = t1, t2
        done, tfree = ln_coeffs(ctx, bsx, bsq, rows, width, (t1, t2),
                                want_outer=True, fused_eps2=fused_eps2)
        ctx.banks.release(isx, tfree)
        ctx.banks.release(isq, tfree)
        return done, tcast, tsq

    hf_toks = []
    for n in range(NG):
        for hh in range(HPN):
            c0 = n * NQ + hh * NH
            ixf, XF = ctx.take("sync")
            tx = halfload(x_in, c0, NH, XF)
            p.wait("vector", tx)
            p.wait("scalar", tx)
            rs = ctx.take_rows()
            rows = dict(rs)
            rows["c"] = rows["ctot"]
            rows["nmc"] = rows["nmct"]
            done, tcast, tsq = stats_and_coeffs(XF, NH, rows, None)
            ibc1, bc1 = ctx.banks.acquire(p)
            ibc2, bc2 = ctx.banks.acquire(p)
            tb1 = bcast_row(ctx, bc1, rows["ctot"], NH, dep=done)
            tb2 = bcast_row(ctx, bc2, rows["nmct"], NH, dep=done)
            ctx.give_rows(rs, [tb1, tb2])
            isc, SCv = ctx.take("vector")
            p.wait_all("vector", [tb1, tb2])
            last_l = None
            for cc in range(NCH):
                xc = XF[:, cc * NH:(cc + 1) * NH]
                s1 = SCv[:, cc * NH:(cc + 1) * NH]
                tmm = p.op("vector", "tensor_mul", s1, xc, bc1[0:128, 0:NH],
                           inc="ve")
                p.wait("vector", tmm)
                last_l = p.op("vector", "tensor_add",
                              HF[:, cc * S + c0:cc * S + c0 + NH], s1,
                              bc2[0:128, 0:NH], inc="ve")
            hf_toks.append(last_l)
            ctx.give(ixf, [last_l, tsq, tcast])
            ctx.give(isc, [last_l])
            ctx.banks.release(ibc1, last_l)
            ctx.banks.release(ibc2, last_l)

    p.wait_all("tensor", hf_toks + [twg])
    for n in range(NG):
        nq0 = n * NQ
        ib, bk = ctx.banks.acquire(p)
        tok = None
        for kc in range(NCH):
            tok = p.op("tensor", "matmul", bk[0:8, 0:NQ],
                       WG[:, kc * 8:(kc + 1) * 8],
                       HF[:, kc * S + nq0:kc * S + nq0 + NQ],
                       start=(kc == 0), stop=(kc == 7),
                       inc="mm" if kc == 7 else None)
        p.wait("vector", tok)
        tmx = p.op("vector", "reduce_max", SMV[0:8, 5 + n:6 + n],
                   bk[0:8, 0:NQ], axis=AX.X, inc="ve")
        ctx.banks.release(ib, tmx)
    p.wait("vector", tmx)
    tg8 = p.op("vector", "reduce_max", SMV[0:8, 0:1], SMV[0:8, 5:9],
               axis=AX.X, inc="ve")
    p.wait("vector", g["tin_r"])
    p.wait("vector", tg8)
    ttin = p.op("vector", "tensor_copy", TIN[0:8, 0:1], SMV[0:8, 0:1],
                inc="ve")
    it, btr = ctx.banks.acquire(p)
    p.wait("tensor", ttin)
    ttr = p.op("tensor", "transpose", btr[0:128, 0:128], TIN, ident, inc="mm")
    g["tin_r"] = ttr
    p.wait("scalar", ttr)
    tbrow = p.op("scalar", "activation", BROW[0:1, 0:8], btr[0:1, 0:8],
                 AF.Copy, inc="sc")
    ctx.banks.release(it, tbrow)
    p.wait("vector", tbrow)
    tbg = p.op("vector", "reduce_max", SMV[0:1, 1:2], BROW[0:1, 0:8],
               axis=AX.X, inc="ve")
    p.wait("vector", tbg)
    tnbg = p.op("vector", "tensor_scalar_mul", SMV[0:1, 2:3], SMV[0:1, 1:2],
                -1.0, inc="ve")
    ibg, bbg = ctx.banks.acquire(p)
    p.wait("tensor", tnbg)
    tbgm = p.op("tensor", "matmul", bbg[0:8, 0:1],
                ctx.cfv("ones_row", w=8, rn=1), SMV[0:1, 2:3],
                start=True, stop=True, inc="mm")
    p.wait("scalar", tbgm)
    tbgc = p.op("scalar", "activation", BCOL[0:8, 0:1], bbg[0:8, 0:1],
                AF.Copy, inc="sc")
    ctx.banks.release(ibg, tbgc)

    p.wait_all("tensor", [twt, twu, twv, tbgc])
    tst = None
    den_g = []
    for n in range(NG):
        for hh in range(HPN):
            c0 = n * NQ + hh * NH
            ixf, XF = ctx.take("sync")
            tx2 = halfload(x_in, c0, NH, XF)
            igl, bgl = ctx.banks.acquire(p)
            tok = None
            for kc in range(NCH):
                tok = p.op("tensor", "matmul", bgl[0:8, 0:NH],
                           WG[:, kc * 8:(kc + 1) * 8],
                           HF[:, kc * S + c0:kc * S + c0 + NH],
                           start=(kc == 0), stop=(kc == 7),
                           inc="mm" if kc == 7 else None)
            p.wait_all("scalar", [tok, tbgc] + den_g)
            tex = p.op("scalar", "activation", DEN[0:8, 0:NH],
                       bgl[0:8, 0:NH], AF.Exp, bias=BCOL[0:8, 0:1], inc="sc")
            ctx.banks.release(igl, tex)
            p.wait("vector", tex)
            texb = p.op("vector", "tensor_copy", SQT[0:8, 0:NH],
                        DEN[0:8, 0:NH], inc="ve")
            idn, bdn = ctx.banks.acquire(p)
            p.wait("tensor", texb)
            tdn = p.op("tensor", "matmul", bdn[0:1, 0:NH],
                       ctx.cbv("ones_col", rn=8), SQT[0:8, 0:NH],
                       start=True, stop=True, inc="mm")
            p.wait("scalar", tdn)
            p.wait("scalar", g.get("grow_r"))
            trg0 = p.op("scalar", "activation", GROW, bdn[0:1, 0:NH], AF.Ln,
                        inc="sc")
            ctx.banks.release(idn, trg0)
            p.wait("scalar", trg0)
            trg = p.op("scalar", "activation", GROW, GROW, AF.Exp,
                       scale=-1.0, inc="sc")
            irb, brb = ctx.banks.acquire(p)
            p.wait("tensor", trg)
            trb = p.op("tensor", "matmul", brb[0:8, 0:NH],
                       ctx.cfv("ones_row", w=8, rn=1), GROW,
                       start=True, stop=True, inc="mm")
            g["grow_r"] = trb
            p.wait("vector", trb)
            p.wait("vector", g.get("deni_r"))
            tegn = p.op("vector", "tensor_mul", DENI[0:8, 0:NH],
                        DEN[0:8, 0:NH], brb[0:8, 0:NH], inc="ve")
            ctx.banks.release(irb, tegn)
            den_g = [tegn, texb]
            ig1, bg1 = ctx.banks.acquire(p)
            p.wait("tensor", tegn)
            tg128 = p.op("tensor", "matmul", bg1[0:128, 0:NH], selg,
                         DENI[0:8, 0:NH], start=True, stop=True, inc="mm")
            g["deni_r"] = tg128
            it2, bt2 = ctx.banks.acquire(p)
            tok = None
            for kc in range(NCH):
                tok = p.op("tensor", "matmul", bt2[0:128, 0:NH],
                           WU[:, kc * 128:(kc + 1) * 128],
                           HF[:, kc * S + c0:kc * S + c0 + NH],
                           start=(kc == 0), stop=(kc == 7),
                           inc="mm" if kc == 7 else None)
            p.wait("scalar", tok)
            p.wait("scalar", g.get("tm_r"))
            ttm = p.op("scalar", "activation", TM, bt2[0:128, 0:NH],
                       AF.Copy, inc="sc")
            ctx.banks.release(it2, ttm)
            p.wait("vector", ttm)
            p.wait("vector", tg128)
            p.wait("vector", g.get("gtb_r"))
            tgt = p.op("vector", "tensor_mul", GTB, TM, bg1[0:128, 0:NH],
                       inc="ve")
            ctx.banks.release(ig1, tgt)
            g["tm_r"] = tgt
            iao, AO = ctx.take("scalar")
            p.wait("tensor", tgt)
            ao_toks = []
            for m in range(NCH):
                ib, bk = ctx.banks.acquire(p)
                for kc in range(NCH):
                    p.op("tensor", "matmul", bk[0:128, 0:NH],
                         WT[:, kc * 1024 + m * 128:kc * 1024 + m * 128 + 128],
                         HF[:, kc * S + c0:kc * S + c0 + NH],
                         start=(kc == 0), stop=False)
                tok = p.op("tensor", "matmul", bk[0:128, 0:NH],
                           WV[:, m * 128:(m + 1) * 128], GTB,
                           start=False, stop=True, inc="mm")
                g["gtb_r"] = tok
                p.wait("scalar", tok)
                tev = p.op("scalar", "activation", AO[:, m * NH:(m + 1) * NH],
                           bk[0:128, 0:NH], AF.Copy, inc="sc")
                ao_toks.append(tev)
                ctx.banks.release(ib, tev)
            rs = ctx.take_rows()
            rows = dict(rs)
            rows["c"] = rows["c2"]
            rows["nmc"] = rows["nmc2"]
            done, tcast, tsq = stats_and_coeffs(AO, NH, rows, None,
                                                extra_cast_dep=ao_toks)
            ibc1, bc1 = ctx.banks.acquire(p)
            ibc2, bc2 = ctx.banks.acquire(p)
            tb1 = bcast_row(ctx, bc1, rows["c2"], NH, dep=done)
            tb2 = bcast_row(ctx, bc2, rows["nmc2"], NH, dep=done)
            ctx.give_rows(rs, [tb1, tb2])
            p.wait_all("vector", [tb1, tb2, tx2])
            xn_toks = []
            exit_tt = None
            for cc in range(NCH):
                aoc = AO[:, cc * NH:(cc + 1) * NH]
                xc = XF[:, cc * NH:(cc + 1) * NH]
                p.op("vector", "tensor_mul", aoc, aoc, bc1[0:128, 0:NH])
                exit_tt = p.op("vector", "tensor_add", aoc, aoc,
                               bc2[0:128, 0:NH], inc="ve")
                p.wait("gpsimd", exit_tt)
                tg_ = p.op("gpsimd", "tensor_add", aoc, aoc, xc, inc="gp")
                xn_toks.append(tg_)
            ctx.banks.release(ibc1, exit_tt)
            ctx.banks.release(ibc2, exit_tt)
            p.wait_all("sync", xn_toks)
            tst = p.op("sync", "dma_start", inc="st", incn=16,
                       out=x_out[:, :, c0:c0 + NH].rearrange("c p w -> p c w"),
                       in_=AO.rearrange("p (c w) -> p c w", w=NH))
            ctx.give(ixf, [xn_toks[-1], exit_tt, tcast, tsq])
            ctx.give(iao, xn_toks + [tst, tcast, tsq])

    ba.pop(); fa.pop()
    return tst


def host_weights_l0f(inputs):
    T = lambda a: np.ascontiguousarray(np.asarray(a, np.float32).T)
    uallT = np.ascontiguousarray(
        np.transpose(np.asarray(inputs["L0_ffn_lora_U"], np.float32),
                     (2, 0, 1)).reshape(1024, 128))
    vall = np.ascontiguousarray(
        np.transpose(np.asarray(inputs["L0_ffn_lora_V"], np.float32),
                     (0, 2, 1)).reshape(128, 1024))
    return {
        "l0f_wT": T(inputs["L0_ffn_weight"]),
        "l0f_gwT": T(inputs["L0_ffn_gate_w"]),
        "l0f_uT": uallT,
        "l0f_vall": vall,
    }


L0F_WSHAPES = {"l0f_wT": [1024, 1024], "l0f_gwT": [1024, 8],
               "l0f_uT": [1024, 128], "l0f_vall": [128, 1024]}


def build_consts_l0f(c):
    selg = np.zeros((128, 128), np.float32)
    for e in range(8):
        selg[e, 16 * e:16 * e + 16] = 1.0 / 16.0
    c.add_f32("selg", selg)



def emit_l1a(ctx, x_in, x_out, W):
    p = ctx.p
    ba, fa = ctx.ba, ctx.fa
    ba.push(); fa.push()
    cm, g, rows_ = ctx.cm, ctx.g, ctx.rows
    ell = ctx.ell

    XN = ba.alloc(NCH * S)
    HID = ba.alloc(24 * NQ)
    WM3 = [ba.alloc(3 * 1024) for _ in range(2)]
    WFF = [ba.alloc(2 * 1024) for _ in range(2)]
    WOS = [ba.alloc(3 * 1024) for _ in range(2)]
    SGB = ba.alloc(NQ)
    UPB = ba.alloc(NQ)

    fo_off = fa.off
    SCQ = fa.alloc(4 * NQ)
    RC2B = fa.alloc(2048)
    FO = fa.t[:, fo_off:fo_off + NCH * NQ]
    XSA = SCQ[:, 0:NQ]
    TMP = SCQ[:, NQ:2 * NQ]
    LM = SCQ[:, 2 * NQ:3 * NQ]
    SCO = SCQ[:, 3 * NQ:4 * NQ]
    RC2ROW = RC2B[0:1, 0:S]
    SCSTB = fa.alloc(1)
    SCST = SCSTB[0:128, 0:1]

    XB, XSQ, SQT = cm["XB"], cm["XSQ"], cm["SQT"]
    R_mu, R_var, R_tmp, R_iv = rows_["mu"], rows_["var"], rows_["tmp"], rows_["iv"]
    R_ctot, R_nmct = rows_["ctot"], rows_["nmct"]
    R_c2, R_nmc2 = rows_["c2"], rows_["nmc2"]
    CROW, NMCROW = rows_["CROW"], rows_["NMCROW"]

    bg = ctx.cfv("l1a_bg", w=NCH)
    bv = ctx.cfv("l1a_bv", w=NCH)
    bd = ctx.cfv("l1a_bd", w=NCH)
    wmix = ctx.cfv("l1a_wmix", w=NCH)
    wffn = ctx.cfv("l1a_wffn", w=NCH)

    def wload_cols(dst, wname, c0, ncols, kchunks, wait=None):
        p.wait("sync", ctx.wtok.get(wname))
        p.wait("sync", wait)
        return p.op("sync", "dma_start", inc="ld", incn=16,
                    out=dst[:, 0:kchunks * ncols].rearrange(
                        "p (c m) -> p c m", m=ncols),
                    in_=W[wname][:, c0:c0 + ncols].rearrange(
                        "(c p) m -> p c m", p=128))

    def halfload(dram, c0, width, dst):
        return p.op("sync", "dma_start", inc="ld", incn=16,
                    out=dst.rearrange("p (c w) -> p c w", w=width),
                    in_=dram[:, :, c0:c0 + width].rearrange("c p w -> p c w"))

    def halfstore(dram, c0, width, srcv):
        return p.op("sync", "dma_start", inc="st", incn=16,
                    out=dram[:, :, c0:c0 + width].rearrange("c p w -> p c w"),
                    in_=srcv.rearrange("p (c w) -> p c w", w=width))

    def stats_and_coeffs(srcf, width, rows, fused_eps2, extra_cast_dep=()):
        p.wait("vector", g["xb_mm"])
        p.wait_all("vector", [g["slot"][0], g["slot"][1]])
        p.wait_all("vector", extra_cast_dep)
        tcast = p.op("vector", "tensor_copy", XB[:, 0:NCH * width], srcf,
                     inc="ve")
        g["slot"][0] = tcast
        g["slot"][1] = tcast
        p.wait_all("scalar", [g["slot"][2], g["slot"][3]])
        p.wait("scalar", g["xsq_mm"])
        p.wait_all("scalar", extra_cast_dep)
        tsq = p.op("scalar", "activation", XSQ[:, 0:NCH * width], srcf,
                   AF.Square, inc="sc")
        g["slot"][2] = tsq
        g["slot"][3] = tsq
        isx, bsx = ctx.banks.acquire(p)
        isq, bsq = ctx.banks.acquire(p)
        t1 = colsum_mms(ctx, bsx, XB[:, 0:NCH * width], width, dep=(tcast,),
                        r0=rows.get("base", 0))
        t2 = colsum_mms(ctx, bsq, XSQ[:, 0:NCH * width], width, dep=(tsq,),
                        r0=rows.get("base", 0))
        g["xb_mm"], g["xsq_mm"] = t1, t2
        done, tfree = ln_coeffs(ctx, bsx, bsq, rows, width, (t1, t2),
                                want_outer=True, fused_eps2=fused_eps2)
        ctx.banks.release(isx, tfree)
        ctx.banks.release(isq, tfree)
        return done, tcast, tsq

    p.label = "l1a.entry"
    xn_toks_all = []
    for n in range(NG):
        for hh in range(HPN):
            c0 = n * NQ + hh * NH
            ixf, XF = ctx.take("sync")
            tx = halfload(x_in, c0, NH, XF)
            p.wait("vector", tx)
            p.wait("scalar", tx)
            rs = ctx.take_rows()
            rows = dict(rs)
            rows["c"] = CROW[:, c0:c0 + NH]
            rows["nmc"] = NMCROW[:, c0:c0 + NH]
            rows["cross"] = rs["base"] != 0
            done, tcast, tsq = stats_and_coeffs(XF, NH, rows, 1e-6)
            ib1, bc1 = ctx.banks.acquire(p)
            ib2, bc2 = ctx.banks.acquire(p)
            ib3, bc3 = ctx.banks.acquire(p)
            ib4, bc4 = ctx.banks.acquire(p)
            tb1 = bcast_row(ctx, bc1, CROW[:, c0:c0 + NH], NH, dep=done)
            tb2 = bcast_row(ctx, bc2, NMCROW[:, c0:c0 + NH], NH, dep=done)
            tb3 = bcast_row(ctx, bc3, rows["ctot"], NH, dep=done)
            tb4 = bcast_row(ctx, bc4, rows["nmct"], NH, dep=done)
            ctx.give_rows(rs, [tb3, tb4])
            iscl, SCl = ctx.take("vector")
            iaox, AOx = ctx.take("vector")
            p.wait_all("vector", [tb1, tb2, tb3, tb4])
            last_l = None
            last_xn = None
            for cc in range(NCH):
                xc = XF[:, cc * NH:(cc + 1) * NH]
                s1 = SCl[:, cc * NH:(cc + 1) * NH]
                ac = AOx[:, cc * NH:(cc + 1) * NH]
                p.op("vector", "tensor_mul", s1, xc, bc1[0:128, 0:NH])
                last_l = p.op("vector", "tensor_add", s1, s1,
                              bc2[0:128, 0:NH], inc="ve")
                p.op("vector", "tensor_mul", ac, xc, bc3[0:128, 0:NH])
                tadd4 = p.op("vector", "tensor_add", ac, ac,
                             bc4[0:128, 0:NH], inc="ve")
                p.wait("vector", tadd4)
                last_xn = p.op("vector", "tensor_scalar",
                               XN[:, cc * S + c0:cc * S + c0 + NH], ac,
                               wmix[:, cc:cc + 1], None, ALU.mult, inc="ve")
            p.wait("sync", last_l)
            tst_l = halfstore(ell, c0, NH, SCl)
            xn_toks_all.append(last_xn)
            ctx.give(ixf, [last_l, last_xn, tsq, tcast])
            ctx.give(iscl, [tst_l])
            ctx.give(iaox, [last_xn])
            for ii in (ib1, ib2, ib3, ib4):
                ctx.banks.release(ii, last_xn)

    p.label = "l1a.scan"
    qb = [ctx.banks.acquire(p) for _ in range(4)]
    p.wait_all("tensor", xn_toks_all)
    mix_w_tok = [None, None]
    prev_store = None
    prev_scan = None
    prev_sel = None
    prev_sq2 = None
    for m in range(NCH):
        slot = m % 2
        wload_cols(WM3[slot], "l1a_Wg", m * 128, 128, 8, wait=mix_w_tok[slot])
        wload_cols(WM3[slot][:, 1024:2048], "l1a_Wv", m * 128, 128, 8)
        tw = wload_cols(WM3[slot][:, 2048:3072], "l1a_Wd", m * 128, 128, 8)
        for qd in range(4):
            q0 = qd * NQ
            outs = []
            p.wait("tensor", tw)
            for wi in range(3):
                ib, bk = ctx.banks.acquire(p)
                tok = None
                for kc in range(NCH):
                    tok = p.op("tensor", "matmul", bk[0:128, 0:NQ],
                               WM3[slot][:, wi * 1024 + kc * 128:
                                         wi * 1024 + kc * 128 + 128],
                               XN[:, kc * S + q0:kc * S + q0 + NQ],
                               start=(kc == 0), stop=(kc == 7),
                               inc="mm" if kc == 7 else None)
                outs.append((ib, bk, tok))
            mix_w_tok[slot] = outs[2][2]
            ibg_, bkg, tokg = outs[0]
            ibv_, bkv, tokv = outs[1]
            ibd_, bkd, tokd = outs[2]
            p.wait("scalar", tokg)
            p.wait("scalar", prev_scan)
            tsg = p.op("scalar", "activation", XSA, bkg[0:128, 0:NQ],
                       AF.Sigmoid, bias=bg[:, m:m + 1], inc="sc")
            ctx.banks.release(ibg_, tsg)
            p.wait("scalar", tokv)
            ttv = p.op("scalar", "activation", TMP, bkv[0:128, 0:NQ],
                       AF.Tanh, bias=bv[:, m:m + 1], inc="sc")
            ctx.banks.release(ibv_, ttv)
            p.wait("vector", tsg)
            p.wait("vector", ttv)
            txs = p.op("vector", "tensor_mul", XSA, XSA, TMP, inc="ve")
            p.wait("scalar", tokd)
            p.wait("scalar", txs)
            tsd = p.op("scalar", "activation", TMP, bkd[0:128, 0:NQ],
                       AF.Sigmoid, bias=bd[:, m:m + 1], inc="sc")
            ctx.banks.release(ibd_, tsd)
            p.wait("vector", tsd)
            ta = p.op("vector", "tensor_scalar", TMP, TMP, 0.998, 0.001,
                      ALU.mult, ALU.add, inc="ve")
            p.wait("vector", ta)
            p.wait("vector", g.get("sco_r"))
            init = 0.0 if qd == 0 else SCST
            tsc = p.op("vector", "tensor_tensor_scan", SCO, TMP, XSA, init,
                       ALU.mult, ALU.add, inc="ve")
            p.wait("vector", tsc)
            tcar = p.op("vector", "tensor_copy", SCST, SCO[:, NQ - 1:NQ],
                        inc="ve")
            prev_scan = tcar
            p.wait("sync", prev_store)
            p.wait("sync", prev_sq2)
            tlm = p.op("sync", "dma_start", inc="ld", incn=16,
                       out=LM, in_=ell[m, :, q0:q0 + NQ])
            p.wait("gpsimd", tlm)
            p.wait("gpsimd", tsc)
            tl2 = p.op("gpsimd", "tensor_add", LM, LM, SCO, inc="gp")
            g["sco_r"] = tl2
            p.wait("sync", tl2)
            prev_store = p.op("sync", "dma_start", inc="st", incn=16,
                              out=ell[m, :, q0:q0 + NQ], in_=LM)
            p.wait("scalar", tl2)
            p.wait("scalar", prev_sel)
            tsq2 = p.op("scalar", "activation", SQT, LM, AF.Square, inc="sc")
            prev_sq2 = tsq2
            p.wait("tensor", tsq2)
            iq, bq = qb[qd]
            prev_sel = p.op("tensor", "matmul", bq[0:1, 0:NQ],
                            ctx.cbv("ones_col"), SQT,
                            start=(m == 0), stop=(m == 7), inc="mm")
    p.wait("vector", prev_sel)
    tq1 = None
    for qd in range(4):
        iq, bq = qb[qd]
        tq1 = p.op("vector", "tensor_scalar", RC2ROW[:, qd * NQ:(qd + 1) * NQ],
                   bq[0:1, 0:NQ], 1.0 / D, 1e-6, ALU.mult, ALU.add, inc="ve")
        ctx.banks.release(iq, tq1)
    p.wait("scalar", tq1)
    trc = p.op("scalar", "activation", RC2ROW, RC2ROW, AF.Ln, inc="sc")
    p.wait("scalar", trc)
    trc2 = p.op("scalar", "activation", RC2ROW, RC2ROW, AF.Exp, scale=-0.5,
                inc="sc")

    p.label = "l1a.yn"
    yn_toks = []
    xn_guard = ("mm", p.counts["mm"])
    for n in range(NG):
        for hh in range(HPN):
            c0 = n * NQ + hh * NH
            ixf, XF = ctx.take("sync")
            tx = halfload(ell, c0, NH, XF)
            ibc, bcq = ctx.banks.acquire(p)
            tbq = bcast_row(ctx, bcq, RC2ROW[:, c0:c0 + NH], NH, dep=(trc2,))
            isc, SCv = ctx.take("vector")
            p.wait_all("vector", [tx, tbq, xn_guard])
            last = None
            for cc in range(NCH):
                xc = XF[:, cc * NH:(cc + 1) * NH]
                s1 = SCv[:, cc * NH:(cc + 1) * NH]
                tmq = p.op("vector", "tensor_mul", s1, xc, bcq[0:128, 0:NH],
                           inc="ve")
                p.wait("vector", tmq)
                last = p.op("vector", "tensor_scalar",
                            XN[:, cc * S + c0:cc * S + c0 + NH], s1,
                            wffn[:, cc:cc + 1], None, ALU.mult, inc="ve")
            yn_toks.append(last)
            ctx.banks.release(ibc, last)
            ctx.give(ixf, [last, tx])
            ctx.give(isc, [last])

    p.label = "l1a.ffn"
    p.wait_all("tensor", yn_toks)
    ff_w_tok = [None, None]
    wo_tok = [None, None]
    fo_guard = [[] for _ in range(NCH)]
    hid_guard = None
    tst = None
    for n in range(NG):
        nq0 = n * NQ
        hid_toks = []
        p.wait("vector", hid_guard)
        for f in range(24):
            slot = f % 2
            twf = wload_cols(WFF[slot], "l1a_Wgate", f * 128, 128, 8,
                             wait=ff_w_tok[slot])
            twf2 = wload_cols(WFF[slot][:, 1024:2048], "l1a_Wup",
                              f * 128, 128, 8)
            ibg_, bkg = ctx.banks.acquire(p)
            ibu_, bku = ctx.banks.acquire(p)
            p.wait("tensor", twf2)
            tokg = toku = None
            for kc in range(NCH):
                tokg = p.op("tensor", "matmul", bkg[0:128, 0:NQ],
                            WFF[slot][:, kc * 128:kc * 128 + 128],
                            XN[:, kc * S + nq0:kc * S + nq0 + NQ],
                            start=(kc == 0), stop=(kc == 7),
                            inc="mm" if kc == 7 else None)
            for kc in range(NCH):
                toku = p.op("tensor", "matmul", bku[0:128, 0:NQ],
                            WFF[slot][:, 1024 + kc * 128:1024 + kc * 128 + 128],
                            XN[:, kc * S + nq0:kc * S + nq0 + NQ],
                            start=(kc == 0), stop=(kc == 7),
                            inc="mm" if kc == 7 else None)
            ff_w_tok[slot] = toku
            p.wait("scalar", tokg)
            p.wait("scalar", g.get("sgb_r"))
            tsg = p.op("scalar", "activation", SGB, bkg[0:128, 0:NQ],
                       AF.Silu, inc="sc")
            ctx.banks.release(ibg_, tsg)
            p.wait("scalar", toku)
            tup = p.op("scalar", "activation", UPB, bku[0:128, 0:NQ],
                       AF.Copy, inc="sc")
            ctx.banks.release(ibu_, tup)
            p.wait("vector", tsg)
            p.wait("vector", tup)
            th = p.op("vector", "tensor_mul", HID[:, f * NQ:(f + 1) * NQ],
                      SGB, UPB, inc="ve")
            g["sgb_r"] = th
            hid_toks.append(th)
        p.wait_all("tensor", hid_toks)
        fo_toks = []
        for m in range(NCH):
            slot = m % 2
            two = wload_cols(WOS[slot], "l1a_Wout", m * 128, 128, 24,
                             wait=wo_tok[slot])
            ib, bk = ctx.banks.acquire(p)
            p.wait("tensor", two)
            tok = None
            for kc in range(24):
                tok = p.op("tensor", "matmul", bk[0:128, 0:NQ],
                           WOS[slot][:, kc * 128:kc * 128 + 128],
                           HID[:, kc * NQ:(kc + 1) * NQ],
                           start=(kc == 0), stop=(kc == 23),
                           inc="mm" if kc == 23 else None)
            wo_tok[slot] = tok
            p.wait("scalar", tok)
            p.wait_all("scalar", fo_guard[m])
            tev = p.op("scalar", "activation", FO[:, m * NQ:(m + 1) * NQ],
                       bk[0:128, 0:NQ], AF.Copy, inc="sc")
            fo_toks.append(tev)
            ctx.banks.release(ib, tev)
        hid_guard = tok
        for hh in range(HPN):
            c0 = nq0 + hh * NH
            ixf, XF = ctx.take("sync")
            tx2 = halfload(x_in, c0, NH, XF)
            iao, AO = ctx.take("sync")
            tl2 = halfload(ell, c0, NH, AO)
            p.wait_all("gpsimd", fo_toks + [tl2])
            tgs = []
            for cc in range(NCH):
                fslice = FO[:, cc * NQ + hh * NH:cc * NQ + hh * NH + NH]
                tg_ = p.op("gpsimd", "tensor_add", fslice, fslice,
                           AO[:, cc * NH:(cc + 1) * NH], inc="gp")
                tgs.append(tg_)
            rs = ctx.take_rows()
            rows = dict(rs)
            rows["c"] = rows["c2"]
            rows["nmc"] = rows["nmc2"]
            fof = FO.rearrange("p (c x) -> p c x", x=NQ)[:, :, hh * NH:hh * NH + NH]
            done, tcast, tsq = stats_and_coeffs(fof, NH, rows, None,
                                                extra_cast_dep=tgs)
            ibc1, bc1 = ctx.banks.acquire(p)
            ibc2, bc2 = ctx.banks.acquire(p)
            tb1 = bcast_row(ctx, bc1, rows["c2"], NH, dep=done)
            tb2 = bcast_row(ctx, bc2, rows["nmc2"], NH, dep=done)
            ctx.give_rows(rs, [tb1, tb2])
            isc, SCv = ctx.take("vector")
            p.wait_all("vector", [tb1, tb2, tx2] + tgs)
            xn_toks = []
            exit_tt = None
            for cc in range(NCH):
                foc = FO[:, cc * NQ + hh * NH:cc * NQ + hh * NH + NH]
                s1 = SCv[:, cc * NH:(cc + 1) * NH]
                xc = XF[:, cc * NH:(cc + 1) * NH]
                p.op("vector", "tensor_mul", s1, foc, bc1[0:128, 0:NH])
                exit_tt = p.op("vector", "tensor_add", s1, s1,
                               bc2[0:128, 0:NH], inc="ve")
                p.wait("gpsimd", exit_tt)
                tg_ = p.op("gpsimd", "tensor_add", s1, s1, xc, inc="gp")
                xn_toks.append(tg_)
            ctx.banks.release(ibc1, exit_tt)
            ctx.banks.release(ibc2, exit_tt)
            p.wait_all("sync", xn_toks)
            tst = p.op("sync", "dma_start", inc="st", incn=16,
                       out=x_out[:, :, c0:c0 + NH].rearrange("c p w -> p c w"),
                       in_=SCv.rearrange("p (c w) -> p c w", w=NH))
            ctx.give(ixf, [xn_toks[-1], exit_tt, tcast, tsq])
            ctx.give(iao, tgs)
            ctx.give(isc, xn_toks + [tst])
            fo_guard = [[xn_toks[cc], tcast, tsq] for cc in range(NCH)]

    ba.pop(); fa.pop()
    return tst


def build_consts_l1a(c, inputs):
    c.add_f32("l1a_bg", col_chunks(inputs["L1_attn_bg"]))
    c.add_f32("l1a_bv", col_chunks(inputs["L1_attn_bv"]))
    c.add_f32("l1a_bd", col_chunks(inputs["L1_attn_bd"]))
    c.add_f32("l1a_wmix", col_chunks(inputs["L1_attn_w_rms_mix"]))
    c.add_f32("l1a_wffn", col_chunks(inputs["L1_attn_w_rms_ffn"]))


def host_weights_l1a(inputs):
    A = lambda a: np.ascontiguousarray(np.asarray(a, np.float32))
    return {
        "l1a_Wg": A(inputs["L1_attn_Wg"]),
        "l1a_Wv": A(inputs["L1_attn_Wv"]),
        "l1a_Wd": A(inputs["L1_attn_Wd"]),
        "l1a_Wgate": A(inputs["L1_attn_W_gate"]),
        "l1a_Wup": A(inputs["L1_attn_W_up"]),
        "l1a_Wout": A(inputs["L1_attn_W_out"]),
    }


L1A_WSHAPES = {"l1a_Wg": [1024, 1024], "l1a_Wv": [1024, 1024],
               "l1a_Wd": [1024, 1024], "l1a_Wgate": [1024, 3072],
               "l1a_Wup": [1024, 3072], "l1a_Wout": [3072, 1024]}



def emit_l1f(ctx, x_in, x_out, W):
    p = ctx.p
    ba, fa = ctx.ba, ctx.fa
    ba.push(); fa.push()
    cm, g, rows_ = ctx.cm, ctx.g, ctx.rows
    ell = ctx.ell

    HB = ba.alloc(NCH * NQ)
    WQ = [ba.alloc(NCH * 512) for _ in range(2)]
    VT2 = ba.alloc(16 * 520)
    qb2_off = ba.off
    QB2 = ba.alloc(4 * S)
    KB2 = ba.alloc(4 * S)
    HID2 = ba.t[:, qb2_off:qb2_off + 22 * NQ]
    XPT = ba.alloc(2 * NQ)
    AT2 = ba.alloc(4 * NQ)
    QRAW = ba.alloc(4 * NQ)
    KRAW = ba.alloc(4 * NQ)
    W1S = [ba.alloc(NCH * 128) for _ in range(4)]
    W2S = [ba.alloc(22 * 128) for _ in range(2)]
    WOS2 = [ba.alloc(4 * 128) for _ in range(2)]
    GEL = ba.alloc(NQ)
    ABF = ba.alloc(NQ)

    XW = fa.alloc(NCH * NQ)
    RNB = fa.alloc(NQ)
    HCAR = fa.alloc(8)

    XB, XSQ, SQT = cm["XB"], cm["XSQ"], cm["SQT"]
    PT = cm["PT"]
    TIN, DEN, DENI = cm["TIN"], cm["DEN"], cm["DENI"]
    BCOL, SMV = cm["BCOL"], cm["SMV"]
    TBa, TBb, TBr = cm["TBa"], cm["TBb"], cm["TBr"]
    R_mu, R_var, R_tmp, R_iv = rows_["mu"], rows_["var"], rows_["tmp"], rows_["iv"]
    R_ctot, R_nmct = rows_["ctot"], rows_["nmct"]
    ident = ctx.cfv("ident_f32", w=128)
    identb = ctx.cbv("ident_bf", w=128)
    permrh = ctx.cbv("perm_rothalf", w=128)
    maskc = ctx.cbv("l1f_maskc", w=128)
    maskw = ctx.cbv("l1f_maskw", w=128)

    def wload_cols(dst, wname, li, c0, ncols, kchunks, wait=None):
        p.wait("sync", ctx.wtok.get(wname))
        p.wait("sync", wait)
        return p.op("sync", "dma_start", inc="ld", incn=16,
                    out=dst[:, 0:kchunks * ncols].rearrange(
                        "p (c m) -> p c m", m=ncols),
                    in_=W[wname][li][:, c0:c0 + ncols].rearrange(
                        "(c p) m -> p c m", p=128))

    def halfload(dram, c0, width, dst):
        return p.op("sync", "dma_start", inc="ld", incn=16,
                    out=dst.rearrange("p (c w) -> p c w", w=width),
                    in_=dram[:, :, c0:c0 + width].rearrange("c p w -> p c w"))

    def stats_and_coeffs(srcf, width, rows, fused_eps2, extra_cast_dep=()):
        p.wait("vector", g["xb_mm"])
        p.wait_all("vector", [g["slot"][0], g["slot"][1]])
        p.wait_all("vector", extra_cast_dep)
        tcast = p.op("vector", "tensor_copy", XB[:, 0:NCH * width], srcf,
                     inc="ve")
        g["slot"][0] = tcast
        g["slot"][1] = tcast
        p.wait_all("scalar", [g["slot"][2], g["slot"][3]])
        p.wait("scalar", g["xsq_mm"])
        p.wait_all("scalar", extra_cast_dep)
        tsq = p.op("scalar", "activation", XSQ[:, 0:NCH * width], srcf,
                   AF.Square, inc="sc")
        g["slot"][2] = tsq
        g["slot"][3] = tsq
        isx, bsx = ctx.banks.acquire(p)
        isq, bsq = ctx.banks.acquire(p)
        t1 = colsum_mms(ctx, bsx, XB[:, 0:NCH * width], width, dep=(tcast,),
                        r0=rows.get("base", 0))
        t2 = colsum_mms(ctx, bsq, XSQ[:, 0:NCH * width], width, dep=(tsq,),
                        r0=rows.get("base", 0))
        g["xb_mm"], g["xsq_mm"] = t1, t2
        done, tfree = ln_coeffs(ctx, bsx, bsq, rows, width, (t1, t2),
                                want_outer=True, fused_eps2=fused_eps2)
        ctx.banks.release(isx, tfree)
        ctx.banks.release(isq, tfree)
        return done, tcast, tsq

    p.label = "l1f.entry"
    for n in range(NG):
        for hh in range(HPN):
            c0 = n * NQ + hh * NH
            ixf, XF = ctx.take("sync")
            tx = halfload(x_in, c0, NH, XF)
            p.wait("vector", tx)
            p.wait("scalar", tx)
            rs = ctx.take_rows()
            rows = dict(rs)
            rows["c"] = rows["ctot"]
            rows["nmc"] = rows["nmct"]
            done, tcast, tsq = stats_and_coeffs(XF, NH, rows, None)
            ib1, bc1 = ctx.banks.acquire(p)
            ib2, bc2 = ctx.banks.acquire(p)
            tb1 = bcast_row(ctx, bc1, rows["ctot"], NH, dep=done)
            tb2 = bcast_row(ctx, bc2, rows["nmct"], NH, dep=done)
            ctx.give_rows(rs, [tb1, tb2])
            isc, SCv = ctx.take("vector")
            p.wait_all("vector", [tb1, tb2])
            last_l = None
            for cc in range(NCH):
                xc = XF[:, cc * NH:(cc + 1) * NH]
                s1 = SCv[:, cc * NH:(cc + 1) * NH]
                p.op("vector", "tensor_mul", s1, xc, bc1[0:128, 0:NH])
                last_l = p.op("vector", "tensor_add", s1, s1,
                              bc2[0:128, 0:NH], inc="ve")
            p.wait("sync", last_l)
            tst_l = p.op("sync", "dma_start", inc="st", incn=16,
                         out=ell[:, :, c0:c0 + NH].rearrange("c p w -> p c w"),
                         in_=SCv.rearrange("p (c w) -> p c w", w=NH))
            ctx.give(ixf, [last_l, tsq, tcast])
            ctx.give(isc, [tst_l])
            ctx.banks.release(ib1, last_l)
            ctx.banks.release(ib2, last_l)

    tst = None
    for li in range(2):
        qs8 = ctx.cfv(f"l1f_qs{li}", w=1)
        ksc = ctx.cfv(f"l1f_ks{li}", w=1)
        lg = ctx.cfv(f"l1f_lg{li}", w=NCH)
        lb = ctx.cfv(f"l1f_lb{li}", w=NCH)
        b1c = ctx.cfv(f"l1f_b1_{li}", w=44)
        b2c = ctx.cfv(f"l1f_b2_{li}", w=NCH)

        p.wait("scalar", g["tin_s"]) if g.get("tin_s") else None
        p.wait("scalar", g["tin_r"])
        tq1 = p.op("scalar", "activation", TIN[0:128, 0:1], qs8, AF.Square,
                   inc="sc")
        it, btr = ctx.banks.acquire(p)
        p.wait("tensor", tq1)
        ttr = p.op("tensor", "transpose", btr[0:128, 0:128], TIN, ident,
                   inc="mm")
        p.wait("vector", ttr)
        tqm = p.op("vector", "reduce_max", SMV[0:1, 0:1], btr[0:1, 0:128],
                   axis=AX.X, inc="ve")
        ctx.banks.release(it, tqm)
        p.wait("scalar", tqm)
        tk1 = p.op("scalar", "activation", TIN[0:128, 0:1], ksc, AF.Square,
                   inc="sc")
        g["tin_s"] = tk1
        it, btr = ctx.banks.acquire(p)
        p.wait("tensor", tk1)
        ttr = p.op("tensor", "transpose", btr[0:128, 0:128], TIN, ident,
                   inc="mm")
        g["tin_r"] = ttr
        p.wait("vector", ttr)
        tkm = p.op("vector", "reduce_max", SMV[0:1, 1:2], btr[0:1, 0:128],
                   axis=AX.X, inc="ve")
        ctx.banks.release(it, tkm)
        p.wait("vector", tkm)
        tb2q = p.op("vector", "scalar_tensor_tensor", SMV[0:1, 2:3],
                    SMV[0:1, 0:1], 1.0, SMV[0:1, 1:2], ALU.mult, ALU.mult,
                    inc="ve")
        p.wait("scalar", tb2q)
        tlb = p.op("scalar", "activation", SMV[0:1, 3:4], SMV[0:1, 2:3],
                   AF.Ln, bias=ctx.cfv("epsc", w=5, rn=1)[:, 4:5], inc="sc")
        p.wait("scalar", tlb)
        tsb = p.op("scalar", "activation", SMV[0:1, 3:4], SMV[0:1, 3:4],
                   AF.Exp, scale=0.5, inc="sc")
        p.wait("vector", tsb)
        tnb = p.op("vector", "tensor_scalar_mul", SMV[0:1, 2:3],
                   SMV[0:1, 3:4], -1.0, inc="ve")
        ibb, bbb = ctx.banks.acquire(p)
        p.wait("tensor", tnb)
        tbm = p.op("tensor", "matmul", bbb[0:128, 0:1],
                   ctx.cfv("ones_row", w=128, rn=1), SMV[0:1, 2:3],
                   start=True, stop=True, inc="mm")
        p.wait("scalar", tbm)
        tbc = p.op("scalar", "activation", BCOL[0:128, 0:1], bbb[0:128, 0:1],
                   AF.Copy, inc="sc")
        ctx.banks.release(ibb, tbc)

        p.label = "l1f.qkv"
        p.wait("vector", g.get("vt2_r"))
        t_vones = p.op("vector", "memset", VT2[:, 64:16 * 520:65], 1.0,
                       inc="ve")
        hcar_z = p.op("vector", "memset", HCAR, 0.0, inc="ve")
        wq_tok = [None, None]
        qkv_last = None
        for n in range(NG):
            nq0 = n * NQ
            hb_toks = []
            for hh in range(HPN):
                c0 = nq0 + hh * NH
                ixf, XF = ctx.take("sync")
                tx = halfload(ell, c0, NH, XF)
                p.wait("vector", tx)
                p.wait("vector", g.get("hb_r"))
                p.wait("vector", hcar_z)
                tcp = None
                for cc in range(NCH):
                    dst0 = cc * NQ + hh * NH
                    if cc < 4:
                        tcp = p.op("vector", "tensor_copy",
                                   HB[:, dst0:dst0 + NH],
                                   XF[:, cc * NH:(cc + 1) * NH], inc="ve")
                    else:
                        hcv = HCAR[:, cc - 4:cc - 3]
                        tcp = p.op("vector", "tensor_copy",
                                   HB[:, dst0:dst0 + 1], hcv, inc="ve")
                        tcp = p.op("vector", "tensor_copy",
                                   HB[:, dst0 + 1:dst0 + NH],
                                   XF[:, cc * NH:(cc + 1) * NH - 1], inc="ve")
                        p.wait("vector", tcp)
                        tcp = p.op("vector", "tensor_copy", hcv,
                                   XF[:, (cc + 1) * NH - 1:(cc + 1) * NH],
                                   inc="ve")
                hb_toks.append(tcp)
                ctx.give(ixf, [tcp, tx])
            p.wait_all("tensor", hb_toks)
            inq, bnq = ctx.banks.acquire(p)
            ink, bnk = ctx.banks.acquire(p)
            sel_tok = {"q": None, "k": None}
            for which, dstraw, bnx in (("q", QRAW, bnq), ("k", KRAW, bnk)):
                base_f = 0 if which == "q" else 512
                for mq in range(4):
                    slot = mq % 2
                    tw = wload_cols(WQ[slot], "l1f_wqkv", li,
                                    base_f + mq * 128, 128, 8,
                                    wait=wq_tok[slot])
                    ib, bk = ctx.banks.acquire(p)
                    p.wait("tensor", tw)
                    tok = None
                    for kc in range(NCH):
                        tok = p.op("tensor", "matmul", bk[0:128, 0:NQ],
                                   WQ[slot][:, kc * 128:kc * 128 + 128],
                                   HB[:, kc * NQ:(kc + 1) * NQ],
                                   start=(kc == 0), stop=(kc == 7),
                                   inc="mm" if kc == 7 else None)
                    wq_tok[slot] = tok
                    p.wait("scalar", tok)
                    p.wait_all("scalar", g.get(f"raw_r_{which}_{mq}", []))
                    tev = p.op("scalar", "activation",
                               dstraw[:, mq * NQ:(mq + 1) * NQ],
                               bk[0:128, 0:NQ], AF.Copy, inc="sc")
                    ctx.banks.release(ib, tev)
                    p.wait("scalar", g.get("sqt_r"))
                    tsq_ = p.op("scalar", "activation", SQT,
                                dstraw[:, mq * NQ:(mq + 1) * NQ], AF.Square,
                                inc="sc")
                    p.wait("tensor", tsq_)
                    sel_tok[which] = p.op(
                        "tensor", "matmul", bnx[0:16, 0:NQ],
                        ctx.cbv(f"sel_qk_{mq}", w=16), SQT,
                        start=(mq == 0), stop=(mq == 3), inc="mm")
                    g["sqt_r"] = sel_tok[which]
            p.label = "l1f.v"
            tw = wload_cols(WQ[0], "l1f_wqkv", li, 1024, 512, 8,
                            wait=wq_tok[0])
            vtok = None
            for tt4 in range(4):
                tt = n * 4 + tt4
                ib, bk = ctx.banks.acquire(p)
                p.wait("tensor", tw)
                tok = None
                for kc in range(NCH):
                    tok = p.op("tensor", "matmul", bk[0:128, 0:NQ],
                               HB[:, kc * NQ + tt4 * 128:
                                  kc * NQ + tt4 * 128 + 128],
                               WQ[0][:, kc * 512:(kc + 1) * 512],
                               start=(kc == 0), stop=(kc == 7),
                               inc="mm" if kc == 7 else None)
                vtok = tok
                p.wait("vector", tok)
                p.wait("vector", t_vones)
                base = tt * 520
                dst = VT2[:, base:base + 520].rearrange(
                    "p (h dd) -> p h dd", dd=65)[:, :, 0:64]
                tev = p.op("vector", "tensor_copy", dst,
                           bk[0:128, 0:NQ].rearrange("p (h d) -> p h d", d=64),
                           inc="ve")
                ctx.banks.release(ib, tev)
                g["vt2_last"] = tev
            wq_tok[0] = vtok
            tfin = None
            for which, bnx, inx in (("q", bnq, inq), ("k", bnk, ink)):
                p.wait("scalar", sel_tok[which])
                p.wait("scalar", g.get("rnb_r"))
                tsr = p.op("scalar", "activation", RNB[0:8, 0:NQ],
                           bnx[0:8, 0:NQ], AF.Ln,
                           bias=ctx.cfv("epsc", w=5, rn=8)[:, 3:4], inc="sc")
                ctx.banks.release(inx, tsr)
                p.wait("scalar", tsr)
                p.wait("scalar", g.get("deni8_r"))
                trc = p.op("scalar", "activation", DENI[0:8, 0:NQ],
                           RNB[0:8, 0:NQ], AF.Exp, scale=-0.5, inc="sc")
                g["rnb_r"] = trc
                sraw = QRAW if which == "q" else KRAW
                sdst = QB2 if which == "q" else KB2
                scol = qs8 if which == "q" else ksc
                r0 = 0 if which == "q" else 256
                p.wait("sync", g.get("xpt_r"))
                txp = p.op("sync", "dma_start", inc="ld", incn=16,
                           out=XPT.rearrange("p (c w) -> p c w", w=NQ),
                           in_=W["l1f_xpos"][r0:r0 + 256, nq0:nq0 + NQ].rearrange(
                               "(c p) w -> p c w", p=128))
                ctab = XPT[:, 0:NQ]
                stab = XPT[:, NQ:2 * NQ]
                for mq in range(4):
                    irn, brn = ctx.banks.acquire(p)
                    p.wait("tensor", trc)
                    trb = p.op("tensor", "matmul", brn[0:128, 0:NQ],
                               ctx.cfv(f"sel8_{mq}", w=128, rn=8),
                               DENI[0:8, 0:NQ],
                               start=True, stop=True, inc="mm")
                    g["deni8_r"] = trb
                    p.wait("vector", trb)
                    p.wait("vector", g.get("tb_r"))
                    p.wait("vector", txp)
                    p.op("vector", "tensor_mul", TBa,
                         sraw[:, mq * NQ:(mq + 1) * NQ], brn[0:128, 0:NQ])
                    t2_ = p.op("vector", "tensor_scalar", TBr, TBa,
                               scol, None, ALU.mult, inc="ve")
                    ctx.banks.release(irn, t2_)
                    ir, br = ctx.banks.acquire(p)
                    p.wait("tensor", t2_)
                    trot = p.op("tensor", "matmul", br[0:128, 0:NQ], permrh,
                                TBr, start=True, stop=True, inc="mm")
                    p.op("vector", "tensor_mul", TBa, TBr, ctab)
                    p.wait("vector", trot)
                    tb_ = p.op("vector", "tensor_mul", TBb,
                               br[0:128, 0:NQ], stab, inc="ve")
                    ctx.banks.release(ir, tb_)
                    g["tb_r"] = tb_
                    p.wait("vector", tb_)
                    p.wait("vector", g.get("qb2_r"))
                    tfin = p.op("vector", "tensor_add",
                                sdst[:, mq * S + nq0:mq * S + nq0 + NQ],
                                TBa, TBb, inc="ve")
                    g[f"raw_r_{which}_{mq}"] = [tfin]
                g["xpt_r"] = tfin
            qkv_last = tfin
        g["hb_r"] = ("mm", p.counts["mm"])

        p.label = "l1f.attn"
        p.wait_all("tensor", [qkv_last, g["vt2_last"], tbc])
        at_guard = None
        for w in range(NG):
            nq0 = w * NQ
            at_toks = []
            for h in range(8):
                ch = h // 2
                base = (h % 2) * 64
                tp = (0, 0) if base == 0 else (64, 0)
                io_, obank = ctx.banks.acquire(p)
                m_order = [4, 5, 6, 7] + ([0, 1, 2, 3] if w > 0 else [])
                nm = len(m_order)
                for mi, m in enumerate(m_order):
                    jglob = 4 * (w - 1) + m
                    if m >= 4:
                        off = 128 * (m - 4)
                        wd = NQ - off
                        moff = off
                        msk = maskc
                    else:
                        off = 0
                        wd = 128 * (m + 1)
                        moff = wd - 128
                        msk = maskw
                    isc, bsc = ctx.banks.acquire(p)
                    p.op("tensor", "matmul", bsc[0:128, off:off + wd],
                         KB2[base:base + 64, ch * S + jglob * 128:
                             ch * S + jglob * 128 + 128],
                         QB2[base:base + 64, ch * S + nq0 + off:
                             ch * S + nq0 + off + wd],
                         start=True, stop=False, tile_position=tp)
                    tms = p.op("tensor", "matmul",
                               bsc[0:128, moff:moff + 128], identb, msk,
                               start=False, stop=True, inc="mm")
                    sl = g["ptrr"] = (g.get("ptrr") or 0) + 1
                    sl = sl % 4
                    p.wait("scalar", tms)
                    p.wait("scalar", g["slot"][sl])
                    p.wait("scalar", g["xb_mm"] if sl < 2 else g["xsq_mm"])
                    tex = p.op("scalar", "activation", PT[sl][:, off:off + wd],
                               bsc[0:128, off:off + wd], AF.Exp,
                               bias=BCOL[0:128, 0:1], inc="sc")
                    ctx.banks.release(isc, tex)
                    p.wait("tensor", tex)
                    tvv = p.op("tensor", "matmul", obank[0:65, off:off + wd],
                               VT2[:, jglob * 520 + h * 65:
                                   jglob * 520 + h * 65 + 65],
                               PT[sl][:, off:off + wd],
                               start=(mi == 0), stop=(mi == nm - 1), inc="mm")
                    g["slot"][sl] = tvv
                p.wait("scalar", tvv)
                p.wait_all("scalar", g["den_w"])
                td1 = p.op("scalar", "activation", DEN[0:1, 0:NQ],
                           obank[64:65, 0:NQ], AF.Copy, bias=1e-30, inc="sc")
                p.wait("scalar", g.get("deni_r"))
                p.wait("scalar", td1)
                tlx = p.op("scalar", "activation", DENI[0:1, 0:NQ],
                           DEN[0:1, 0:NQ], AF.Ln, inc="sc")
                p.wait("scalar", tlx)
                tri = p.op("scalar", "activation", DENI[0:1, 0:NQ],
                           DENI[0:1, 0:NQ], AF.Exp, scale=-1.0, inc="sc")
                ir2, brb = ctx.banks.acquire(p)
                p.wait("tensor", tri)
                trb = p.op("tensor", "matmul", brb[0:128, 0:NQ],
                           ctx.cfv("ones_row", w=128, rn=1), DENI[0:1, 0:NQ],
                           start=True, stop=True, inc="mm")
                g["deni_r"] = trb
                p.wait("vector", trb)
                trbe = p.op("vector", "tensor_copy", DEN, brb[0:128, 0:NQ],
                            inc="ve")
                ctx.banks.release(ir2, trbe)
                p.wait("vector", trbe)
                p.wait("vector", g.get("at2_r"))
                ta1 = p.op("vector", "tensor_mul",
                           AT2[base:base + 64, ch * NQ:(ch + 1) * NQ],
                           obank[0:64, 0:NQ], DEN[base:base + 64, 0:NQ],
                           inc="ve")
                ctx.banks.release(io_, [td1, ta1])
                g["den_w"] = [ta1, tri]
                at_toks.append(ta1)
            p.wait_all("sync", g.get("xw_w", []))
            txw = p.op("sync", "dma_start", inc="ld", incn=16,
                       out=XW.rearrange("p (c w) -> p c w", w=NQ),
                       in_=ell[:, :, nq0:nq0 + NQ].rearrange("c p w -> p c w"))
            p.wait_all("tensor", at_toks)
            wo_toks = []
            for m in range(NCH):
                slot = m % 2
                two = wload_cols(WOS2[slot], "l1f_wo", li, m * 128, 128, 4,
                                 wait=g.get(f"wos_{slot}"))
                ib, bk = ctx.banks.acquire(p)
                p.wait("tensor", two)
                tok = None
                for kc in range(4):
                    tok = p.op("tensor", "matmul", bk[0:128, 0:NQ],
                               WOS2[slot][:, kc * 128:kc * 128 + 128],
                               AT2[:, kc * NQ:(kc + 1) * NQ],
                               start=(kc == 0), stop=(kc == 3),
                               inc="mm" if kc == 3 else None)
                g[f"wos_{slot}"] = tok
                p.wait("vector", tok)
                p.wait("vector", txw)
                txx = p.op("vector", "tensor_add",
                           XW[:, m * NQ:(m + 1) * NQ],
                           XW[:, m * NQ:(m + 1) * NQ], bk[0:128, 0:NQ],
                           inc="ve")
                wo_toks.append(txx)
                ctx.banks.release(ib, txx)
            p.wait_all("sync", wo_toks)
            tso = p.op("sync", "dma_start", inc="st", incn=16,
                       out=ell[:, :, nq0:nq0 + NQ].rearrange("c p w -> p c w"),
                       in_=XW.rearrange("p (c w) -> p c w", w=NQ))
            g["xw_w"] = [tso] + wo_toks
            g["at2_r"] = ("mm", p.counts["mm"])
        g["qb2_r"] = ("mm", p.counts["mm"])
        g["vt2_r"] = ("mm", p.counts["mm"])
        g["hid2_r"] = g["qb2_r"]
        g["hb2_r"] = g["qb2_r"]

        p.label = "l1f.ffn"
        hcar_z2 = p.op("vector", "memset", HCAR[:, 4:8], 0.0, inc="ve")
        final = li == 1
        for n in range(NG):
            nq0 = n * NQ
            p.wait_all("sync", g.get("xw_w", []))
            txw = p.op("sync", "dma_start", inc="ld", incn=16,
                       out=XW.rearrange("p (c w) -> p c w", w=NQ),
                       in_=ell[:, :, nq0:nq0 + NQ].rearrange("c p w -> p c w"))
            yn_toks = []
            for hh in range(HPN):
                c0 = nq0 + hh * NH
                iao, AOs = ctx.take("vector")
                p.wait("vector", txw)
                p.wait("vector", hcar_z2)
                sh_toks = []
                tcp = None
                for cc in range(NCH):
                    xs_ = XW[:, cc * NQ + hh * NH:cc * NQ + hh * NH + NH]
                    dst = AOs[:, cc * NH:(cc + 1) * NH]
                    if cc < 4:
                        tcp = p.op("vector", "tensor_copy", dst, xs_, inc="ve")
                    else:
                        hcv = HCAR[:, cc:cc + 1]
                        tcp = p.op("vector", "tensor_copy", dst[:, 0:1], hcv,
                                   inc="ve")
                        tcp = p.op("vector", "tensor_copy", dst[:, 1:NH],
                                   xs_[:, 0:NH - 1], inc="ve")
                        p.wait("vector", tcp)
                        tcp = p.op("vector", "tensor_copy", hcv,
                                   xs_[:, NH - 1:NH], inc="ve")
                    sh_toks.append(tcp)
                rs = ctx.take_rows()
                rows = dict(rs)
                rows["c"] = rows["ctot"]
                rows["nmc"] = rows["nmct"]
                done, tcast, tsq = stats_and_coeffs(AOs, NH, rows, None,
                                                    extra_cast_dep=sh_toks)
                ib1, bc1 = ctx.banks.acquire(p)
                ib2, bc2 = ctx.banks.acquire(p)
                tb1 = bcast_row(ctx, bc1, rows["ctot"], NH, dep=done)
                tb2 = bcast_row(ctx, bc2, rows["nmct"], NH, dep=done)
                ctx.give_rows(rs, [tb1, tb2])
                isc, SCv = ctx.take("vector")
                p.wait_all("vector", [tb1, tb2])
                tyn = None
                for cc in range(NCH):
                    ac = AOs[:, cc * NH:(cc + 1) * NH]
                    s1 = SCv[:, cc * NH:(cc + 1) * NH]
                    p.op("vector", "tensor_mul", s1, ac, bc1[0:128, 0:NH])
                    ta2_ = p.op("vector", "tensor_add", s1, s1,
                                bc2[0:128, 0:NH], inc="ve")
                    p.wait("vector", ta2_)
                    tyn = p.op("vector", "tensor_scalar",
                               HB[:, cc * NQ + hh * NH:cc * NQ + hh * NH + NH],
                               s1, lg[:, cc:cc + 1], lb[:, cc:cc + 1],
                               ALU.mult, ALU.add, inc="ve")
                yn_toks.append(tyn)
                ctx.give(iao, [tyn, tcast, tsq])
                ctx.give(isc, [tyn])
                ctx.banks.release(ib1, tyn)
                ctx.banks.release(ib2, tyn)
            p.wait_all("tensor", yn_toks)
            hid_toks = []
            p.wait("vector", g.get("hid2_r"))
            for ma in range(22):
                slot = ma % 2
                slg = 2 + slot
                twa = wload_cols(W1S[slot], "l1f_w1p", li, ma * 128, 128, 8,
                                 wait=g.get(f"w1s_{slot}"))
                twg = wload_cols(W1S[slg], "l1f_w1p", li, (22 + ma) * 128,
                                 128, 8, wait=g.get(f"w1s_{slg}"))
                ib1_, bka = ctx.banks.acquire(p)
                p.wait("tensor", twa)
                toka = None
                for kc in range(NCH):
                    toka = p.op("tensor", "matmul", bka[0:128, 0:NQ],
                                W1S[slot][:, kc * 128:kc * 128 + 128],
                                HB[:, kc * NQ:(kc + 1) * NQ],
                                start=(kc == 0), stop=(kc == 7),
                                inc="mm" if kc == 7 else None)
                g[f"w1s_{slot}"] = toka
                ib2_, bkg = ctx.banks.acquire(p)
                p.wait("tensor", twg)
                tokg = None
                for kc in range(NCH):
                    tokg = p.op("tensor", "matmul", bkg[0:128, 0:NQ],
                                W1S[slg][:, kc * 128:kc * 128 + 128],
                                HB[:, kc * NQ:(kc + 1) * NQ],
                                start=(kc == 0), stop=(kc == 7),
                                inc="mm" if kc == 7 else None)
                g[f"w1s_{slg}"] = tokg
                p.wait("vector", toka)
                p.wait("vector", g.get("gel_r"))
                tab = p.op("vector", "tensor_scalar", ABF, bka[0:128, 0:NQ],
                           b1c[:, ma:ma + 1], None, ALU.add, inc="ve")
                ctx.banks.release(ib1_, tab)
                p.wait("scalar", tokg)
                p.wait("scalar", g.get("gel_r"))
                tge = p.op("scalar", "activation", GEL, bkg[0:128, 0:NQ],
                           AF.Gelu, bias=b1c[:, 22 + ma:23 + ma], inc="sc")
                ctx.banks.release(ib2_, tge)
                p.wait("vector", tab)
                p.wait("vector", tge)
                th = p.op("vector", "tensor_mul",
                          HID2[:, ma * NQ:(ma + 1) * NQ], ABF, GEL, inc="ve")
                g["gel_r"] = th
                hid_toks.append(th)
            p.wait_all("tensor", hid_toks)
            w2_toks = []
            for m in range(NCH):
                slot = m % 2
                tw2 = wload_cols(W2S[slot], "l1f_w2p", li, m * 128, 128, 22,
                                 wait=g.get(f"w2s_{slot}"))
                ib, bk = ctx.banks.acquire(p)
                p.wait("tensor", tw2)
                tok = None
                for kc in range(22):
                    tok = p.op("tensor", "matmul", bk[0:128, 0:NQ],
                               W2S[slot][:, kc * 128:kc * 128 + 128],
                               HID2[:, kc * NQ:(kc + 1) * NQ],
                               start=(kc == 0), stop=(kc == 21),
                               inc="mm" if kc == 21 else None)
                g[f"w2s_{slot}"] = tok
                p.wait("vector", tok)
                txx = p.op("vector", "scalar_tensor_tensor",
                           XW[:, m * NQ:(m + 1) * NQ], bk[0:128, 0:NQ],
                           b2c[:, m:m + 1], XW[:, m * NQ:(m + 1) * NQ],
                           ALU.add, ALU.add, inc="ve")
                w2_toks.append(txx)
                ctx.banks.release(ib, txx)
            g["hid2_r"] = ("mm", p.counts["mm"])
            g["hb2_r"] = ("mm", p.counts["mm"])
            if not final:
                p.wait_all("sync", w2_toks)
                tso = p.op("sync", "dma_start", inc="st", incn=16,
                           out=ell[:, :, nq0:nq0 + NQ].rearrange(
                               "c p w -> p c w"),
                           in_=XW.rearrange("p (c w) -> p c w", w=NQ))
                g["xw_w"] = [tso] + w2_toks
            else:
                for hh in range(HPN):
                    c0 = nq0 + hh * NH
                    ixf, XF = ctx.take("sync")
                    tx2 = halfload(x_in, c0, NH, XF)
                    rs = ctx.take_rows()
                    rows = dict(rs)
                    rows["c"] = rows["ctot"]
                    rows["nmc"] = rows["nmct"]
                    xwf = XW.rearrange("p (c x) -> p c x", x=NQ)[
                        :, :, hh * NH:hh * NH + NH]
                    done, tcast, tsq = stats_and_coeffs(
                        xwf, NH, rows, None, extra_cast_dep=w2_toks)
                    ib1, bc1 = ctx.banks.acquire(p)
                    ib2, bc2 = ctx.banks.acquire(p)
                    tb1 = bcast_row(ctx, bc1, rows["ctot"], NH, dep=done)
                    tb2 = bcast_row(ctx, bc2, rows["nmct"], NH, dep=done)
                    ctx.give_rows(rs, [tb1, tb2])
                    isc, SCv = ctx.take("vector")
                    p.wait_all("vector", [tb1, tb2, tx2] + w2_toks)
                    xn_toks = []
                    exit_tt = None
                    for cc in range(NCH):
                        xwc = XW[:, cc * NQ + hh * NH:cc * NQ + hh * NH + NH]
                        s1 = SCv[:, cc * NH:(cc + 1) * NH]
                        xc = XF[:, cc * NH:(cc + 1) * NH]
                        p.op("vector", "tensor_mul", s1, xwc,
                             bc1[0:128, 0:NH])
                        exit_tt = p.op("vector", "tensor_add", s1, s1,
                                       bc2[0:128, 0:NH], inc="ve")
                        p.wait("gpsimd", exit_tt)
                        tg_ = p.op("gpsimd", "tensor_add", s1, s1, xc,
                                   inc="gp")
                        xn_toks.append(tg_)
                    ctx.banks.release(ib1, exit_tt)
                    ctx.banks.release(ib2, exit_tt)
                    p.wait_all("sync", xn_toks)
                    tst = p.op("sync", "dma_start", inc="st", incn=16,
                               out=x_out[:, :, c0:c0 + NH].rearrange(
                                   "c p w -> p c w"),
                               in_=SCv.rearrange("p (c w) -> p c w", w=NH))
                    ctx.give(ixf, [xn_toks[-1], exit_tt, tcast, tsq])
                    ctx.give(isc, xn_toks + [tst])
                g["xw_w"] = w2_toks + [("ve", p.counts["ve"])]
        g["qb2_r"] = ("mm", p.counts["mm"])

    ba.pop(); fa.pop()
    return tst


def build_consts_l1f(c, inputs):
    P = np.zeros((128, 128), np.float32)
    for b in (0, 64):
        for dd in range(32):
            P[b + dd + 32, b + dd] = -1.0
            P[b + dd, b + dd + 32] = 1.0
    c.add_bf("perm_rothalf", P)
    mc = np.where(np.arange(128)[None, :] < np.arange(128)[:, None],
                  -1e9, 0.0).astype(np.float32)
    mw = np.where(np.arange(128)[None, :] > np.arange(128)[:, None],
                  -1e9, 0.0).astype(np.float32)
    c.add_bf("l1f_maskc", mc)
    c.add_bf("l1f_maskw", mw)
    for mq in range(4):
        s = np.zeros((128, 128), np.float32)
        s[2 * mq, 0:64] = 1.0
        s[2 * mq + 1, 64:128] = 1.0
        c.add_f32(f"sel8_{mq}", s)
    for li in range(2):
        qs = np.asarray(inputs["L1_ffn_q_scale"][li], np.float32)
        ks = np.asarray(inputs["L1_ffn_k_scale"][li], np.float32)
        c.add_f32(f"l1f_qs{li}", np.tile(qs, 2)[:, None] * 8.0)
        c.add_f32(f"l1f_ks{li}", np.tile(ks, 2)[:, None])
        c.add_f32(f"l1f_lg{li}", col_chunks(inputs["L1_ffn_ln_g"][li]))
        c.add_f32(f"l1f_lb{li}", col_chunks(inputs["L1_ffn_ln_b"][li]))
        b1 = np.asarray(inputs["L1_ffn_b1"][li], np.float32)
        b1p = np.zeros(44 * 128, np.float32)
        b1p[0:2730] = b1[0:2730]
        b1p[22 * 128:22 * 128 + 2730] = b1[2730:5460]
        c.add_f32(f"l1f_b1_{li}", col_chunks(b1p))
        c.add_f32(f"l1f_b2_{li}", col_chunks(inputs["L1_ffn_b2"][li]))


def xpos_tables_host():
    dim_head, window, n = 64, 512, S
    half = np.arange(0, dim_head, 2, dtype=np.float32)
    inv_freq = 1.0 / 10000.0 ** (half / dim_head)
    t = np.arange(n, dtype=np.float32)
    fr = inv_freq[:, None] * t[None, :]
    freqs = np.concatenate([fr, fr], axis=0)
    base = (half + 0.4 * dim_head) / (1.4 * dim_head)
    power = (t - n // 2) / float(window // 2)
    scale = base[:, None] ** power[None, :]
    scale = np.concatenate([scale, scale], axis=0)
    cos, sin = np.cos(freqs), np.sin(freqs)
    Cq = cos * scale
    Sq = sin * scale
    Ck = cos / scale
    Sk = sin / scale
    out = np.concatenate([np.tile(T, (2, 1)) for T in (Cq, Sq, Ck, Sk)],
                         axis=0)
    return np.ascontiguousarray(out)


def host_weights_l1f(inputs):
    A = lambda a: np.ascontiguousarray(np.asarray(a, np.float32))
    w1 = np.asarray(inputs["L1_ffn_w1"], np.float32)
    w1p = np.zeros((2, 1024, 44 * 128), np.float32)
    w1p[:, :, 0:2730] = w1[:, :, 0:2730]
    w1p[:, :, 22 * 128:22 * 128 + 2730] = w1[:, :, 2730:5460]
    w2 = np.asarray(inputs["L1_ffn_w2"], np.float32)
    w2p = np.zeros((2, 22 * 128, 1024), np.float32)
    w2p[:, 0:2730, :] = w2
    return {
        "l1f_wqkv": A(inputs["L1_ffn_wqkv"]),
        "l1f_wo": A(inputs["L1_ffn_wo"]),
        "l1f_w1p": np.ascontiguousarray(w1p),
        "l1f_w2p": np.ascontiguousarray(w2p),
        "l1f_xpos": xpos_tables_host(),
    }


L1F_WSHAPES = {"l1f_wqkv": [2, 1024, 1536], "l1f_wo": [2, 512, 1024],
               "l1f_w1p": [2, 1024, 5632], "l1f_w2p": [2, 2816, 1024],
               "l1f_xpos": [512, 2048]}



def host_weights_l0a(inputs):
    T = lambda a: np.ascontiguousarray(np.asarray(a, np.float32).T)
    return {
        "wdqT": T(inputs["L0_attn_w_down_q"]),
        "wdkvT": T(inputs["L0_attn_w_down_kv"]),
        "wupqT": T(inputs["L0_attn_w_up_q"]),
        "wupkT": T(inputs["L0_attn_w_up_k"]),
        "wupvT": T(inputs["L0_attn_w_up_v"]),
        "woutT": T(inputs["L0_attn_w_out"]),
    }


def declare_weights(nc, wdict):
    out = {}
    for k, v in wdict.items():
        out[k] = nc.dram_tensor(k, list(v.shape), F32, kind="ExternalInput").ap()
    return out


BLOCK_WSHAPES = {
    "l0a": {"wdqT": [1024, 512], "wdkvT": [1024, 512],
            "wupqT": [512, 1024], "wupkT": [512, 1024],
            "wupvT": [512, 1024], "woutT": [1024, 1024]},
    "l0f": L0F_WSHAPES,
}

BLOCK_EMIT = {}


def _register_blocks():
    BLOCK_EMIT["l0a"] = emit_l0a
    BLOCK_EMIT["l0f"] = emit_l0f
    BLOCK_EMIT["l1a"] = emit_l1a
    BLOCK_WSHAPES["l1a"] = L1A_WSHAPES
    BLOCK_EMIT["l1f"] = emit_l1f
    BLOCK_WSHAPES["l1f"] = L1F_WSHAPES


def build_chain_program(consts, blocks, nrows=NROWS, dbg_shapes=None):
    _register_blocks()
    nc = bass.Bass("TRN2", target_bir_lowering=False)
    ctx = Ctx(nc, consts)
    if dbg_shapes:
        ctx.dbgmap = {k: nc.dram_tensor("dbg_" + k, v[0], v[1],
                                        kind="ExternalOutput").ap()
                      for k, v in dbg_shapes.items()}
    shape = [nrows, NCH, 128, S]
    x_in = nc.dram_tensor("x_in", shape, F32, kind="ExternalInput").ap()
    x_out = nc.dram_tensor("x_out", shape, F32, kind="ExternalOutput").ap()
    ios = [x_in]
    for i in range(len(blocks) - 1):
        ios.append(nc.dram_tensor(f"x_mid{i}", shape, F32, kind="Internal").ap())
    ios.append(x_out)
    ctx.ell = nc.dram_tensor("ell_scr", [NCH, 128, S], F32, kind="Internal").ap()
    ctx.cqs = nc.dram_tensor("cqs_scr", [NG, 4, 128, NQ], BF16, kind="Internal").ap()
    W = {}
    ctx.wtok = {}
    worder = []
    for b in blocks:
        for k, v in BLOCK_WSHAPES[b].items():
            if k not in W:
                wf = nc.dram_tensor(k, v, F32, kind="ExternalInput").ap()
                wb = nc.dram_tensor(k + "__bf", v, BF16, kind="Internal").ap()
                W[k] = wb
                worder.append((k, wf, wb))
    ctx.load_consts()
    for k, wf, wb in worder:
        ctx.wtok[k] = ctx.p.op("gpsimd", "dma_start", inc="gd", incn=16,
                               out=wb, in_=wf)
    for r in range(nrows):
        for bi, b in enumerate(blocks):
            BLOCK_EMIT[b](ctx, ios[bi][r], ios[bi + 1][r], W)
            ctx.barrier()
    ctx.p.wait("sync", ("st", ctx.p.counts["st"]))
    ctx.p.build()
    return nc


def build_l0a_program(consts, nrows=NROWS):
    return build_chain_program(consts, ["l0a"], nrows)


def to_fm(x_rows):
    r = x_rows.shape[0]
    return np.ascontiguousarray(
        x_rows.transpose(0, 2, 1).reshape(r, NCH, 128, S))


def from_fm(x_fm):
    r = x_fm.shape[0]
    return np.ascontiguousarray(
        x_fm.reshape(r, D, S).transpose(0, 2, 1))


ALL_BLOCKS = ["l0a", "l0f", "l1a", "l1f"]


def kernel(**inputs):
    consts = Consts()
    build_consts_common(consts)
    build_consts_l0a(consts, inputs)
    build_consts_l0f(consts)
    build_consts_l1a(consts, inputs)
    build_consts_l1f(consts, inputs)
    cf, cb = consts.finish()
    wmap = {}
    wmap.update(host_weights_l0a(inputs))
    wmap.update(host_weights_l0f(inputs))
    wmap.update(host_weights_l1a(inputs))
    wmap.update(host_weights_l1f(inputs))

    x = np.asarray(inputs["L0_attn_x"], np.float32)
    x_fm = to_fm(x)

    nc = build_chain_program(consts, ALL_BLOCKS, nrows=NROWS)
    in_maps = []
    for c in range(NCORES):
        m = dict(wmap)
        m["c_f32"] = cf
        m["c_bf"] = cb
        m["x_in"] = np.ascontiguousarray(x_fm[c * NROWS:(c + 1) * NROWS])
        in_maps.append(m)
    res = run_bass_kernel_spmd(nc, in_maps, core_ids=list(range(NCORES)))
    out = np.concatenate([from_fm(r["x_out"]) for r in res.results], axis=0)
    return np.ascontiguousarray(out.astype(np.float32))
